# version 07
# baseline (speedup 1.0000x reference)
import math
import numpy as np
import ml_dtypes
import concourse.bass as bass
import concourse.mybir as mybir
from concourse.bass_utils import run_bass_kernel_spmd

F32 = mybir.dt.float32
BF16 = mybir.dt.bfloat16
AF = mybir.ActivationFunctionType
ALU = mybir.AluOpType
AX = mybir.AxisListType
BF = ml_dtypes.bfloat16

D = 1024
NCK = 8
HD = 16
TT = 512
EPS_LN = 1e-5
EPS_RMS = 1e-6
EPS_GN = 1e-5 * 64
EPS_LNF = 1e-5


def ap3(t, off, dims):
    return bass.AP(t, off, dims)


def build_program(NT=8, rg=None, taps=(), strict=False, nsteps=None):
    if rg is None:
        rg = [[0, 1], [2, 3], [4, 5], [6, 7]]
    TLOC = NT * TT
    nc = bass.Bass("TRN2", target_bir_lowering=False,
                   detect_race_conditions=bool(strict))

    def din(name, shape, dt=F32):
        return nc.dram_tensor(name, list(shape), dt, kind="ExternalInput")

    x_d = din("x", [D, TLOC])
    wkT = din("wkT", [D, D]); wvT = din("wvT", [D, D])
    wrT = din("wrT", [D, D]); woT = din("woT", [D, D])
    w1T = din("w1T", [D, 4096]); w1N = din("w1N", [4096, D])
    w2T = din("w2T", [4096, D]); w2N = din("w2N", [D, 4096])
    wr1T = din("wr1T", [D, D]); wk1T = din("wk1T", [D, D]); wv1T = din("wv1T", [D, D])
    wg1T = din("wg1T", [D, D]); wo1T = din("wo1T", [D, D])
    wfcT = din("wfcT", [D, 4096]); wpjT = din("wpjT", [4096, D])
    p_decay = din("p_decay", [128, 8]); p_ln0w = din("p_ln0w", [128, 8])
    p_b1 = din("p_b1", [128, 32]); p_b2 = din("p_b2", [128, 8])
    p_maa = din("p_maa", [128, 32])
    p_td = din("p_td", [16, 1]); p_fa = din("p_fa", [16, 1])
    p_ln1w = din("p_ln1w", [128, 8]); p_ln1b = din("p_ln1b", [128, 8])
    p_lnfw = din("p_lnfw", [128, 32]); p_lnfb = din("p_lnfb", [128, 32])
    c_ones = din("c_ones", [128, 128])
    c_onesb = din("c_onesb", [128, 128], BF16)
    c_ident = din("c_ident", [128, 128])
    c_ltri = din("c_ltri", [128, 128], BF16)
    c_e16 = din("c_e16", [128, 128])
    c_e16t = din("c_e16t", [16, 1024])
    c_s64 = din("c_s64", [16, 1024])
    c_psel = din("c_psel", [16, 8])
    c_hsel = din("c_hsel", [16, 128])
    c_edt = din("c_edt", [128, 1024])
    c_trit = din("c_trit", [128, 1024])
    c_eyet = din("c_eyet", [128, 1024])
    c_tmod = din("c_tmod", [128, 512], BF16)
    c_qrev = din("c_qrev", [128, 1])
    p_decrow = din("p_decrow", [1, 1024])
    c_qidx = din("c_qidx", [128, 1])
    c_ar256 = din("c_ar256", [128, 256], BF16)
    m_sr = din("m_sr", [128, 2])

    def scr(name, shape, dt):
        k = "ExternalOutput" if name in taps else "Internal"
        return nc.dram_tensor(name, list(shape), dt, kind=k)

    xn_s = scr("xn_s", [D, TLOC], BF16)
    terms_s = scr("terms_s", [TLOC, D], BF16)
    g_s = scr("g_s", [D, TLOC], BF16)
    x1_s = scr("x1_s", [D, TLOC], F32)
    xn1_s = scr("xn1_s", [D, TLOC], BF16)
    r1_s = scr("r1_s", [D, TLOC], BF16)
    kf_s = scr("kf_s", [D, TLOC], BF16)
    kt_s = scr("kt_s", [TLOC, D], BF16)
    vt_s = scr("vt_s", [TLOC, D], BF16)
    g1_s = scr("g1_s", [D, TLOC], BF16)
    g2_s = scr("g2_s", [D, TLOC], BF16)
    sw1_s = scr("sw1_s", [D, 4096], BF16)
    sw2_s = scr("sw2_s", [4096, D], BF16)
    wqf_s = scr("wqf_s", [D, 4096], BF16)
    wqp_s = scr("wqp_s", [4096, D], BF16)
    cc1i = nc.dram_tensor("cc1i", [1, 1024], F32); cc1o = nc.dram_tensor("cc1o", [1, 1024], F32)
    cc2i = nc.dram_tensor("cc2i", [128, 8], F32); cc2o = nc.dram_tensor("cc2o", [128, 8], F32)
    cc3i = nc.dram_tensor("cc3i", [64, 1024], F32); cc3o = nc.dram_tensor("cc3o", [64, 1024], F32)
    out_d = nc.dram_tensor("out", [D, TLOC], F32, kind="ExternalOutput")
    h_dbg = scr("h_dbg", [4096, TLOC], BF16) if "h_dbg" in taps else None
    a_dbg = scr("a_dbg", [128, 64], F32) if "a_dbg" in taps else None
    b_dbg = scr("b_dbg", [128, 64], F32) if "b_dbg" in taps else None
    y0_dbg = nc.dram_tensor("y0_dbg", [D, 512], BF16, kind="ExternalOutput") if "b_dbg" in taps else None
    z_dbg = scr("z_dbg", [4096, TLOC], F32) if "z_dbg" in taps else None
    y_dbg = scr("y_dbg", [D, TLOC], BF16) if "y_dbg" in taps else None

    A = nc.alloc_sbuf_tensor
    ones = A("ones", [128, 128], F32); onesb = A("onesb", [128, 128], BF16)
    ident = A("ident", [128, 128], F32); ltri = A("ltri", [128, 128], BF16)
    e16 = A("e16", [128, 128], F32); e16t = A("e16t", [16, 1024], F32)
    psel = A("psel", [16, 8], F32); hsel = A("hsel", [16, 128], F32)
    tmod = A("tmod", [128, 512], BF16); qrev = A("qrev", [128, 1], F32)
    msr = A("msr", [128, 2], F32)
    decay = A("decay", [128, 8], F32); ln0w = A("ln0w", [128, 8], F32)
    b1c = A("b1c", [128, 32], F32); b2c = A("b2c", [128, 8], F32)
    maa = A("maa", [128, 32], F32)
    tdc = A("tdc", [16, 1], F32); fac = A("fac", [16, 1], F32)
    ln1w = A("ln1w", [128, 8], F32); ln1b = A("ln1b", [128, 8], F32)
    lnfw = A("lnfw", [128, 32], F32); lnfb = A("lnfb", [128, 32], F32)
    lnfwB = A("lnfwB", [128, 32], BF16); lnfbB = A("lnfbB", [128, 32], BF16)
    dec = A("dec", [128, 8], F32)
    d512 = A("d512", [128, 8 * 256], BF16)
    lfm = A("lfm", [128, 8], F32)
    lfmH = A("lfmH", [128, 8], F32)
    lrow = A("lrow", [1, 1024], F32)
    lrowH = A("lrowH", [1, 1024], F32)
    invdt = A("invdt", [128, 1024], BF16)
    invcr = A("invcr", [1, 1024], F32)
    qidx = A("qidx", [128, 1], F32)
    ar256 = A("ar256", [128, 256], BF16)
    decrow = A("decrow", [1, 1024], F32)
    ccol = A("ccol", [128, 8], F32); ccol2 = A("ccol2", [128, 8], F32)
    ptmp = A("ptmp", [128, 8], F32)
    carry0 = A("carry0", [128, 8], F32)
    crun = A("crun", [128, 8], F32)
    ucol = A("ucol", [16, 1], F32); lnwcol = A("lnwcol", [16, 1], F32)
    lnwfm = A("lnwfm", [128, 8], F32)
    wkrep = A("wkrep", [128, 1024], BF16)
    e16b = A("e16b", [128, 128], BF16)
    lnwH = A("lnwH", [64, 16], F32)
    wsH = A("wsH", [64, 16], F32)
    wsrepH = A("wsrepH", [64, 1024], F32)
    state_H = A("state_H", [64, 1024], F32)
    stateb_H = A("stateb_H", [64, 1024], BF16)
    wmt = A("wmt", [128, 1024], BF16)
    rows16 = A("rows16", [16, 512], F32)
    st16 = A("st16", [16, 512], F32)
    st16b = A("st16b", [16, 512], F32)
    wA = A("wA", [128, 8 * 1024], BF16)
    wB = A("wB", [128, 8 * 1024], BF16)
    wStr = A("wStr", [128, 2 * 1024], BF16)
    stg = A("stg", [128, 4096], F32)
    wtmp = A("wtmp", [128, 2048], F32)
    xnA = A("xnA", [128, 8 * 512], BF16)
    xnB = A("xnB", [128, 8 * 512], BF16)
    trm = A("trm", [128, 1024], BF16)
    ysb = A("ysb", [128, 8 * 512], BF16)
    rsb = A("rsb", [128, 8 * 512], BF16)
    gsb = A("gsb", [128, 8 * 512], BF16)
    dlt = A("dlt", [128, 8 * 512], BF16)
    xmix = A("xmix", [128, 8 * 512], BF16)
    ktm = A("ktm", [128, 1024], BF16)
    vtm = A("vtm", [128, 1024], BF16)
    kwkb = A("kwkb", [128, 1024], BF16)
    attwt = A("attwt", [128, 1024], BF16)
    yL1 = A("yL1", [128, 2048], BF16)
    rowA = A("rowA", [1, 512], F32); rowB = A("rowB", [1, 512], F32)
    rowC = A("rowC", [1, 512], F32)
    absb = A("absb", [128, 512], F32)
    bbsb = A("bbsb", [128, 512], F32)
    absbB = A("absbB", [128, 512], BF16)
    sqb = A("sqb", [128, 512], F32)
    jit1 = A("jit1", [128, 512], F32)
    jit2 = A("jit2", [128, 512], BF16)
    tA = A("tA", [128, 512], F32)
    tBb = A("tBb", [128, 512], BF16)
    tCb = A("tCb", [128, 512], BF16)
    shb = A("shb", [128, 512], BF16)
    pcol = A("pcol", [128, 8], BF16)
    brow = A("brow", [128, 8], F32)
    cbuf = A("cbuf", [128, 512], F32)
    alpha1 = A("alpha1", [128, 32], F32)
    alpha2 = A("alpha2", [128, 8], F32)
    gcol = A("gcol", [128, 1], F32)
    gcol2 = A("gcol2", [128, 1], F32)
    cs1 = A("cs1", [128, 8], F32); cs2 = A("cs2", [128, 8], F32)
    rowg = A("rowg", [1, 512], F32)

    s64v = stg[0:16, 0:1024]
    edtv = stg[:, 1024:2048]
    tritv = stg[:, 2048:3072]
    eyetv = stg[:, 3072:4096]

    ps = [nc.alloc_psum_tensor(f"ps{i}", [128, 512], F32) for i in range(8)]

    ENG = ["T", "V", "S", "D", "G", "C"]
    cnt = {e: 0 for e in ENG}
    steps = []
    psum_r = {}
    psum_w = {}
    sem_handles = {}
    last_wait = {}

    cnt_hist = []
    def E(engn, fn, inc=0):
        if engn != "D":
            for o in ("T", "V", "S", "D", "G"):
                W(engn, o)
        if engn in ("V", "S") and cnt[engn] > 0:
            steps.append((engn, lambda eng, s=engn, v=cnt[engn]: eng.wait_ge(sem_handles[s], v)))
            cnt_hist.append(dict(cnt))
        elif strict and cnt[engn] > 0:
            steps.append((engn, lambda eng, s=engn, v=cnt[engn]: eng.wait_ge(sem_handles[s], v)))
            cnt_hist.append(dict(cnt))
        if inc:
            cnt[engn] += inc
        def run(eng, fn=fn, inc=inc, engn=engn):
            r = fn(eng)
            if inc:
                r.then_inc(sem_handles[engn], inc)
        steps.append((engn, run))
        cnt_hist.append(dict(cnt))

    def W(engn, src, val=None):
        if engn == src:
            return
        v = cnt[src] if val is None else val
        if v <= 0 or last_wait.get((engn, src), -1) >= v:
            return
        last_wait[(engn, src)] = v
        steps.append((engn, lambda eng, s=src, v=v: eng.wait_ge(sem_handles[s], v)))
        cnt_hist.append(dict(cnt))

    def mark_read(engn, banks):
        for b in banks:
            psum_r[b] = (engn, cnt[engn])

    def mark_write(banks):
        for b in banks:
            psum_w[b] = ("T", cnt["T"])

    def guard_write(banks):
        seen = {}
        for b in banks:
            if b in psum_r:
                s, v = psum_r[b]
                seen[s] = max(seen.get(s, 0), v)
        for s, v in seen.items():
            W("T", s, v)

    def guard_read(engn, banks):
        vv = 0
        for b in banks:
            if b in psum_w:
                vv = max(vv, psum_w[b][1])
        if vv:
            W(engn, "T", vv)

    def dma(qeng, dst, src):
        E(qeng, lambda e, d=dst, s=src: e.dma_start(out=d, in_=s), 16)

    def ld(dst, src):
        W("D", "T"); W("D", "V"); W("D", "S"); W("D", "G")
        dma("D", dst, src)

    def mmp(bank_ap, lh, rh, start, stop, banks, inc=True, tp=None):
        def fn(e, o=bank_ap, l=lh, r=rh, st=start, sp=stop, t=tp):
            if t is None:
                return e.matmul(o, l, r, start=st, stop=sp)
            return e.matmul(o, l, r, start=st, stop=sp, tile_position=t)
        E("T", fn, 1 if (inc or strict) else 0)
        mark_write(banks)

    def V(fn):
        E("V", fn, 1)

    def SW(engn):
        steps.append((engn, lambda eng, s=engn, v=cnt[engn]: eng.wait_ge(sem_handles[s], v)))
        cnt_hist.append(dict(cnt))

    def S(fn):
        E("S", fn, 1)

    def G(fn):
        E("G", fn, 1)

    def CC(fn):
        for o in ("T", "V", "S", "D"):
            W("G", o)
        steps.append(("G", lambda eng, v=cnt["G"]: eng.wait_ge(sem_handles["G"], v)))
        cnt_hist.append(dict(cnt))
        cnt["C"] += 1
        def run(eng, fn=fn):
            fn(eng).then_inc(sem_handles["C"], 1)
        steps.append(("G", run))
        cnt_hist.append(dict(cnt))
        steps.append(("G", lambda eng, v=cnt["C"]: eng.wait_ge(sem_handles["C"], v)))
        cnt_hist.append(dict(cnt))

    for dst, src in [(ones, c_ones), (onesb, c_onesb), (ident, c_ident), (ltri, c_ltri),
                     (e16, c_e16), (e16t, c_e16t), (psel, c_psel), (hsel, c_hsel),
                     (tmod, c_tmod), (qrev, c_qrev), (msr, m_sr),
                     (decrow, p_decrow), (qidx, c_qidx), (ar256, c_ar256),
                     (decay, p_decay), (ln0w, p_ln0w), (b1c, p_b1), (b2c, p_b2), (maa, p_maa),
                     (tdc, p_td), (fac, p_fa), (ln1w, p_ln1w), (ln1b, p_ln1b),
                     (lnfw, p_lnfw), (lnfb, p_lnfb),
                     (s64v, c_s64[:, :]), (edtv, c_edt[:, :]), (tritv, c_trit[:, :]),
                     (eyetv, c_eyet[:, :])]:
        dma("D", dst if isinstance(dst, bass.AP) else dst[:, :],
            src if isinstance(src, bass.AP) else src[:, :])
    W("S", "D"); W("V", "D"); W("T", "D"); W("G", "D")

    S(lambda e: e.activation(out=dec[:, :], in_=decay[:, :], func=AF.Sigmoid))
    S(lambda e: e.activation(out=lrow[:, :], in_=decrow[:, :], func=AF.Sigmoid))
    W("V", "S")
    V(lambda e: e.tensor_scalar(out=dec[:, :], in0=dec[:, :], scalar1=1e-7, scalar2=None, op0=ALU.max))
    V(lambda e: e.tensor_scalar(out=lrow[:, :], in0=lrow[:, :], scalar1=1e-7, scalar2=None, op0=ALU.max))
    W("S", "V")
    S(lambda e: e.activation(out=lfm[:, :], in_=dec[:, :], func=AF.Ln))
    S(lambda e: e.activation(out=lrow[:, :], in_=lrow[:, :], func=AF.Ln))
    W("V", "S")
    V(lambda e: e.tensor_scalar(out=lfmH[:, :], in0=lfm[:, :], scalar1=msr[:, 1:2],
                                scalar2=None, op0=ALU.mult))
    V(lambda e: e.tensor_scalar(out=lfmH[:, :], in0=lfmH[:, :], scalar1=float(TLOC),
                                scalar2=None, op0=ALU.mult))
    V(lambda e: e.tensor_scalar(out=lrowH[:, :], in0=lrow[:, :], scalar1=msr[0:1, 1:2],
                                scalar2=None, op0=ALU.mult))
    V(lambda e: e.tensor_scalar(out=lrowH[:, :], in0=lrowH[:, :], scalar1=-float(TLOC),
                                scalar2=None, op0=ALU.mult))
    W("S", "V")
    for c in range(8):
        S(lambda e, c=c: e.activation(out=d512[:, c * 256:(c + 1) * 256], in_=ar256[:, :],
                                      func=AF.Exp, scale=lfm[:, c:c + 1]))
    W("T", "S")
    guard_write([1, 2])
    mmp(ps[1][:, 0:512], ones[0:1, :], lrow[:, 0:512], True, True, [1], inc=False)
    mmp(ps[2][:, 0:512], ones[0:1, :], lrow[:, 512:1024], True, True, [2])
    guard_read("V", [1, 2])
    V(lambda e: e.tensor_copy(out=wtmp[:, 0:512], in_=ps[1][:, 0:512]))
    V(lambda e: e.tensor_copy(out=wtmp[:, 512:1024], in_=ps[2][:, 0:512]))
    mark_read("V", [1, 2])
    W("S", "V")
    S(lambda e: e.activation(out=wtmp[:, 0:1024], in_=wtmp[:, 0:1024], func=AF.Exp, scale=qidx[:, 0:1]))
    W("V", "S")
    V(lambda e: e.tensor_scalar(out=wtmp[:, 0:1024], in0=wtmp[:, 0:1024], scalar1=1e-11,
                                scalar2=None, op0=ALU.max))
    V(lambda e: e.reciprocal(out=wtmp[:, 0:1024], in_=wtmp[:, 0:1024]))
    SW("V")
    V(lambda e: e.tensor_scalar(out=invdt[:, :], in0=wtmp[:, 0:1024], scalar1=1.001e10,
                                scalar2=None, op0=ALU.min))

    lnwbc = wtmp[:, 0:1024]
    S(lambda e: e.activation(out=ucol[:, :], in_=tdc[:, :], func=AF.Exp))
    W("V", "S")
    V(lambda e: e.tensor_scalar(out=lnwcol[:, :], in0=ucol[:, :], scalar1=-1.0, scalar2=None, op0=ALU.mult))
    V(lambda e: e.tensor_scalar(out=rows16[:, 0:8], in0=psel[:, :], scalar1=lnwcol[:, 0:1],
                                scalar2=None, op0=ALU.mult))
    W("T", "V")
    guard_write([1])
    mmp(ps[1][:, 0:8], hsel[:, :], rows16[0:16, 0:8], True, True, [1])
    guard_read("V", [1])
    V(lambda e: e.tensor_copy(out=lnwfm[:, :], in_=ps[1][:, 0:8]))
    mark_read("V", [1])
    V(lambda e: e.tensor_scalar(out=rows16[:, 0:512], in0=s64v[:, 0:512], scalar1=lnwcol[:, 0:1],
                                scalar2=None, op0=ALU.mult))
    V(lambda e: e.tensor_scalar(out=st16[:, 0:512], in0=s64v[:, 512:1024], scalar1=lnwcol[:, 0:1],
                                scalar2=None, op0=ALU.mult))
    W("T", "V")
    guard_write([1, 2])
    mmp(ps[1][:, 0:512], ones[0:16, :], rows16[0:16, 0:512], True, True, [1], inc=False)
    mmp(ps[2][:, 0:512], ones[0:16, :], st16[0:16, 0:512], True, True, [2])
    guard_read("V", [1, 2])
    V(lambda e: e.tensor_copy(out=lnwbc[:, 0:512] if False else wtmp[:, 0:512], in_=ps[1][:, 0:512]))
    V(lambda e: e.tensor_copy(out=wtmp[:, 512:1024], in_=ps[2][:, 0:512]))
    mark_read("V", [1, 2])
    W("S", "V")
    S(lambda e: e.activation(out=wkrep[:, :], in_=wtmp[:, 0:1024], func=AF.Exp, scale=qrev[:, 0:1]))
    V(lambda e: e.tensor_scalar(out=rows16[:, 0:16], in0=ident[0:16, 0:16],
                                scalar1=lnwcol[:, 0:1], scalar2=None, op0=ALU.mult))
    W("T", "V")
    guard_write([3])
    mmp(ps[3][0:64, 0:16], ones[0:16, 0:64], rows16[0:16, 0:16], True, True, [3])
    guard_read("V", [3])
    V(lambda e: e.tensor_copy(out=lnwH[:, :], in_=ps[3][0:64, 0:16]))
    mark_read("V", [3])
    W("S", "V")
    S(lambda e: e.activation(out=wsH[:, :], in_=lnwH[:, :], func=AF.Exp, scale=64.0))
    W("V", "S")
    for h in range(16):
        V(lambda e, h=h: e.tensor_scalar(out=wsrepH[:, h * 64:(h + 1) * 64],
                                         in0=ones[0:64, 0:64],
                                         scalar1=wsH[:, h:h + 1], scalar2=None, op0=ALU.mult))
    V(lambda e: e.tensor_copy(out=e16b[:, :], in_=e16[:, :]))
    V(lambda e: e.tensor_tensor(out=cbuf[:, :], in0=edtv[:, 0:512], in1=wtmp[:, 0:512], op=ALU.mult))
    V(lambda e: e.tensor_tensor(out=tA[:, :], in0=edtv[:, 512:1024], in1=wtmp[:, 512:1024], op=ALU.mult))
    W("S", "V")
    S(lambda e: e.activation(out=wmt[:, 0:512], in_=cbuf[:, :], func=AF.Exp))
    S(lambda e: e.activation(out=wmt[:, 512:1024], in_=tA[:, :], func=AF.Exp))
    W("V", "S")
    V(lambda e: e.tensor_tensor(out=wmt[:, :], in0=wmt[:, :], in1=tritv, op=ALU.mult))
    V(lambda e: e.tensor_scalar(out=rows16[:, 0:512], in0=s64v[:, 0:512], scalar1=fac[:, 0:1],
                                scalar2=None, op0=ALU.mult))
    V(lambda e: e.tensor_scalar(out=st16[:, 0:512], in0=s64v[:, 512:1024], scalar1=fac[:, 0:1],
                                scalar2=None, op0=ALU.mult))
    W("T", "V")
    guard_write([1, 2])
    mmp(ps[1][:, 0:512], ones[0:16, :], rows16[0:16, 0:512], True, True, [1], inc=False)
    mmp(ps[2][:, 0:512], ones[0:16, :], st16[0:16, 0:512], True, True, [2])
    guard_read("V", [1, 2])
    V(lambda e: e.tensor_tensor(out=wtmp[:, 0:512], in0=ps[1][:, 0:512], in1=eyetv[:, 0:512], op=ALU.mult))
    V(lambda e: e.tensor_tensor(out=wtmp[:, 512:1024], in0=ps[2][:, 0:512],
                                in1=eyetv[:, 512:1024], op=ALU.mult))
    mark_read("V", [1, 2])
    V(lambda e: e.tensor_tensor(out=wmt[:, :], in0=wmt[:, :], in1=wtmp[:, 0:1024], op=ALU.add))
    V(lambda e: e.tensor_copy(out=lnfwB[:, :], in_=lnfw[:, :]))
    V(lambda e: e.tensor_copy(out=lnfbB[:, :], in_=lnfb[:, :]))

    def load_w(dst, src):
        for c in range(NCK):
            ld(stg[:, 0:1024], src[c * 128:(c + 1) * 128, :])
            W("V", "D")
            V(lambda e, c=c, dst=dst: e.tensor_copy(out=dst[:, c * 1024:(c + 1) * 1024],
                                                    in_=stg[:, 0:1024]))
        W("T", "V")

    def ln_stream(src_d, t0, outbuf, eps):
        for c in range(NCK):
            ld(jit1[:, :], src_d[c * 128:(c + 1) * 128, t0:t0 + TT])
            W("S", "D"); W("S", "T")
            S(lambda e: e.activation(out=sqb[:, :], in_=jit1[:, :], func=AF.Square))
            guard_write([0, 1])
            W("T", "S"); W("T", "D")
            mmp(ps[0][0:1, 0:512], ones[:, 0:1], jit1[:, :], c == 0, c == NCK - 1, [0], inc=False)
            mmp(ps[1][0:1, 0:512], ones[:, 0:1], sqb[:, :], c == 0, c == NCK - 1, [1])
        guard_read("V", [0, 1])
        V(lambda e: e.tensor_scalar(out=rowA[:, :], in0=ps[0][0:1, 0:512], scalar1=1.0 / D,
                                    scalar2=None, op0=ALU.mult))
        V(lambda e: e.tensor_scalar(out=rowB[:, :], in0=ps[1][0:1, 0:512], scalar1=1.0 / D,
                                    scalar2=None, op0=ALU.mult))
        mark_read("V", [0, 1])
        V(lambda e: e.tensor_tensor(out=rowC[:, :], in0=rowA[:, :], in1=rowA[:, :], op=ALU.mult))
        V(lambda e: e.tensor_tensor(out=rowB[:, :], in0=rowB[:, :], in1=rowC[:, :], op=ALU.subtract))
        V(lambda e, eps=eps: e.tensor_scalar(out=rowB[:, :], in0=rowB[:, :], scalar1=eps,
                                             scalar2=None, op0=ALU.add))
        W("S", "V")
        S(lambda e: e.activation(out=rowB[:, :], in_=rowB[:, :], func=AF.Sqrt))
        W("V", "S")
        V(lambda e: e.reciprocal(out=rowB[:, :], in_=rowB[:, :]))
        SW("V")
        V(lambda e: e.tensor_tensor(out=rowC[:, :], in0=rowA[:, :], in1=rowB[:, :], op=ALU.mult))
        V(lambda e: e.tensor_scalar(out=rowC[:, :], in0=rowC[:, :], scalar1=-1.0, scalar2=None, op0=ALU.mult))
        W("T", "V")
        guard_write([2, 3])
        mmp(ps[2][:, 0:512], ones[0:1, :], rowB[:, :], True, True, [2], inc=False)
        mmp(ps[3][:, 0:512], ones[0:1, :], rowC[:, :], True, True, [3])
        guard_read("V", [2, 3])
        V(lambda e: e.tensor_copy(out=absb[:, :], in_=ps[2][:, 0:512]))
        V(lambda e: e.tensor_copy(out=bbsb[:, :], in_=ps[3][:, 0:512]))
        mark_read("V", [2, 3])
        for c in range(NCK):
            ld(jit1[:, :], src_d[c * 128:(c + 1) * 128, t0:t0 + TT])
            W("V", "D")
            V(lambda e: e.tensor_tensor(out=tA[:, :], in0=jit1[:, :], in1=absb[:, :], op=ALU.mult))
            V(lambda e, c=c: e.tensor_tensor(out=outbuf[:, c * 512:(c + 1) * 512],
                                             in0=tA[:, :], in1=bbsb[:, :], op=ALU.add))

    def fm_proj(wres, rhsbuf, banks, evac):
        for c in range(NCK):
            bank = banks[c % 2]
            guard_write([bank])
            for k in range(NCK):
                mmp(ps[bank][:, 0:512], wres[:, k * 1024 + c * 128:k * 1024 + (c + 1) * 128],
                    rhsbuf[:, k * 512:(k + 1) * 512], k == 0, k == NCK - 1, [bank])
            evac(c, bank)

    def mk_delta(it):
        for c in range(NCK):
            V(lambda e, c=c: e.tensor_copy(out=shb[:, 1:512], in_=xnB[:, c * 512:c * 512 + 511]))
            V(lambda e, c=c: e.tensor_copy(out=shb[:, 0:1], in_=pcol[:, c:c + 1]))
            V(lambda e, c=c: e.tensor_tensor(out=dlt[:, c * 512:(c + 1) * 512], in0=shb[:, :],
                                             in1=xnB[:, c * 512:(c + 1) * 512], op=ALU.subtract))
            V(lambda e, c=c: e.tensor_copy(out=pcol[:, c:c + 1], in_=xnB[:, c * 512 + 511:c * 512 + 512]))

    def mk_mix(mi):
        for c in range(NCK):
            V(lambda e, c=c, mi=mi: e.tensor_scalar(out=tCb[:, :], in0=dlt[:, c * 512:(c + 1) * 512],
                                                    scalar1=maa[:, mi * 8 + c:mi * 8 + c + 1],
                                                    scalar2=None, op0=ALU.mult))
            V(lambda e, c=c: e.tensor_tensor(out=xmix[:, c * 512:(c + 1) * 512], in0=tCb[:, :],
                                             in1=xnB[:, c * 512:(c + 1) * 512], op=ALU.add))

    def tm_proj(tc, wres, rhsbuf, dstb, banks=(6, 7)):
        guard_write(list(banks))
        for half in range(2):
            for c in range(NCK):
                mmp(ps[banks[half]][:, 0:512], rhsbuf[:, c * 512 + tc * 128:c * 512 + (tc + 1) * 128],
                    wres[:, c * 1024 + half * 512:c * 1024 + (half + 1) * 512], c == 0, c == NCK - 1,
                    [banks[half]], inc=(c == NCK - 1))
        guard_read("V", list(banks))
        for half in range(2):
            V(lambda e, half=half, dstb=dstb: e.tensor_copy(
                out=dstb[:, half * 512:(half + 1) * 512], in_=ps[banks[half]][:, 0:512]))
        mark_read("V", list(banks))

    kchk = ktm[0:64, 0:1024]
    vchk = vtm[0:64, 0:1024]
    kwkH = kwkb[0:64, 0:1024]

    def scan_chunk():
        W("T", "V")
        guard_write([1, 5])
        for h in range(HD):
            bank = 1 if h < 8 else 5
            mmp(ps[bank][0:64, (h % 8) * 64:(h % 8) * 64 + 64],
                kwkH[:, h * 64:(h + 1) * 64],
                vchk[:, h * 64:(h + 1) * 64],
                True, True, [bank], inc=(h % 8 == 7))
        guard_read("V", [1, 5])
        V(lambda e: e.tensor_tensor(out=state_H[:, :], in0=state_H[:, :], in1=wsrepH[:, :], op=ALU.mult))
        V(lambda e: e.tensor_tensor(out=state_H[:, 0:512], in0=state_H[:, 0:512],
                                    in1=ps[1][0:64, 0:512], op=ALU.add))
        V(lambda e: e.tensor_tensor(out=state_H[:, 512:1024], in0=state_H[:, 512:1024],
                                    in1=ps[5][0:64, 0:512], op=ALU.add))
        mark_read("V", [1, 5])

    load_w(wA, wkT)
    load_w(wB, wvT)
    TOTB = 7
    V(lambda e: e.memset(rowg[:, :], 0.0))
    V(lambda e: e.memset(cbuf[0:1, 0:512], 0.0))
    for it in range(NT):
        t0 = it * TT
        ln_stream(x_d, t0, xnA, EPS_LN)
        W("G", "V")
        for c in range(NCK):
            dma("G", xn_s[c * 128:(c + 1) * 128, t0:t0 + TT], xnA[:, c * 512:(c + 1) * 512])
        W("T", "V")
        for tc in range(4):
            kb = [4, 5]; vb = [6, 2]
            guard_write(kb + vb)
            for half in range(2):
                for c in range(NCK):
                    mmp(ps[kb[half]][:, 0:512], xnA[:, c * 512 + tc * 128:c * 512 + (tc + 1) * 128],
                        wA[:, c * 1024 + half * 512:c * 1024 + (half + 1) * 512], c == 0, c == NCK - 1,
                        [kb[half]], inc=(c == NCK - 1))
            for half in range(2):
                for c in range(NCK):
                    mmp(ps[vb[half]][:, 0:512], xnA[:, c * 512 + tc * 128:c * 512 + (tc + 1) * 128],
                        wB[:, c * 1024 + half * 512:c * 1024 + (half + 1) * 512], c == 0, c == NCK - 1,
                        [vb[half]], inc=(c == NCK - 1))
            off = float(t0 + tc * 128)
            V(lambda e, off=off: e.tensor_scalar(out=invcr[:, :], in0=lrow[:, :], scalar1=-off,
                                                 scalar2=None, op0=ALU.mult))
            V(lambda e: e.tensor_tensor(out=invcr[:, :], in0=invcr[:, :], in1=lrowH[:, :], op=ALU.add))
            V(lambda e: e.tensor_scalar(out=invcr[:, :], in0=invcr[:, :], scalar1=23.03,
                                        scalar2=None, op0=ALU.min))
            W("S", "V")
            S(lambda e: e.activation(out=invcr[:, :], in_=invcr[:, :], func=AF.Exp))
            guard_write([3, 0])
            W("T", "S")
            mmp(ps[3][:, 0:512], ones[0:1, :], invcr[:, 0:512], True, True, [3], inc=False)
            mmp(ps[0][:, 0:512], ones[0:1, :], invcr[:, 512:1024], True, True, [0])
            guard_read("V", [3, 0])
            V(lambda e: e.tensor_tensor(out=wtmp[:, 0:512], in0=invdt[:, 0:512],
                                        in1=ps[3][:, 0:512], op=ALU.mult))
            V(lambda e: e.tensor_tensor(out=wtmp[:, 512:1024], in0=invdt[:, 512:1024],
                                        in1=ps[0][:, 0:512], op=ALU.mult))
            mark_read("V", [3, 0])
            V(lambda e: e.tensor_scalar(out=wtmp[:, 0:1024], in0=wtmp[:, 0:1024], scalar1=1e10,
                                        scalar2=None, op0=ALU.min))
            guard_read("V", kb + vb)
            for half in range(2):
                V(lambda e, half=half, kb=kb: e.tensor_tensor(
                    out=wtmp[:, 1024 + half * 512:1024 + (half + 1) * 512], in0=ps[kb[half]][:, 0:512],
                    in1=wtmp[:, half * 512:(half + 1) * 512], op=ALU.mult))
                V(lambda e, half=half, vb=vb: e.tensor_tensor(
                    out=trm[:, half * 512:(half + 1) * 512],
                    in0=wtmp[:, 1024 + half * 512:1024 + (half + 1) * 512],
                    in1=ps[vb[half]][:, 0:512], op=ALU.mult))
            mark_read("V", kb + vb)
            W("G", "V")
            dma("G", terms_s[t0 + tc * 128:t0 + (tc + 1) * 128, :], trm[:, :])
            W("T", "V")
            if tc == 0:
                guard_write([TOTB, 1])
            mmp(ps[TOTB][0:1, 0:512], onesb[:, 0:1], trm[:, 0:512],
                tc == 0, tc == 3, [TOTB], inc=False)
            mmp(ps[1][0:1, 0:512], onesb[:, 0:1], trm[:, 512:1024],
                tc == 0, tc == 3, [1], inc=True)
        guard_read("V", [TOTB, 1])
        V(lambda e: e.tensor_tensor(out=rowg[:, 0:512], in0=rowg[:, 0:512],
                                    in1=ps[TOTB][0:1, 0:512], op=ALU.add))
        V(lambda e: e.tensor_tensor(out=cbuf[0:1, 0:512], in0=cbuf[0:1, 0:512],
                                    in1=ps[1][0:1, 0:512], op=ALU.add))
        mark_read("V", [TOTB, 1])

    V(lambda e: e.tensor_scalar(out=rowg[:, 0:512], in0=rowg[:, 0:512], scalar1=msr[0:1, 0:1],
                                scalar2=None, op0=ALU.mult))
    V(lambda e: e.tensor_scalar(out=tA[0:1, 0:512], in0=cbuf[0:1, 0:512], scalar1=msr[0:1, 0:1],
                                scalar2=None, op0=ALU.mult))
    W("G", "V")
    dma("G", cc1i[0:1, 0:512], rowg[:, 0:512])
    dma("G", cc1i[0:1, 512:1024], tA[0:1, 0:512])
    CC(lambda e: e.collective_compute("AllReduce", ALU.add, replica_groups=rg,
                                      ins=[cc1i.ap().opt()], outs=[cc1o.ap().opt()]))
    E("G", lambda e: e.dma_start(out=ptmp[:, :], in_=ap3(cc1o, 0, [[1, 128], [128, 8]]),
                                 allow_slow_non_contiguous=True), 16)
    W("V", "G")
    V(lambda e: e.tensor_scalar(out=carry0[:, :], in0=ptmp[:, :], scalar1=msr[:, 1:2],
                                scalar2=None, op0=ALU.mult))

    for hc in range(32):
        ld(stg[:, 0:1024], w1N[hc * 128:(hc + 1) * 128, :])
        W("S", "D")
        S(lambda e: e.activation(out=stg[:, 0:1024], in_=stg[:, 0:1024], func=AF.Abs))
        W("V", "S")
        V(lambda e, hc=hc: e.reduce_sum(out=alpha1[:, hc:hc + 1], in_=stg[:, 0:1024], axis=AX.X))
    V(lambda e: e.tensor_scalar(out=alpha1[:, :], in0=alpha1[:, :], scalar1=1.0 / D,
                                scalar2=None, op0=ALU.mult))
    for c in range(NCK):
        ld(stg[:, 0:4096], w2N[c * 128:(c + 1) * 128, :])
        W("S", "D")
        S(lambda e: e.activation(out=stg[:, 0:4096], in_=stg[:, 0:4096], func=AF.Abs))
        W("V", "S")
        V(lambda e, c=c: e.reduce_sum(out=alpha2[:, c:c + 1], in_=stg[:, 0:4096], axis=AX.X))
    V(lambda e: e.tensor_scalar(out=alpha2[:, :], in0=alpha2[:, :], scalar1=1.0 / 4096,
                                scalar2=None, op0=ALU.mult))
    if a_dbg is not None:
        W("G", "V")
        dma("G", a_dbg[:, 0:32], alpha1[:, :])
        dma("G", a_dbg[:, 32:40], alpha2[:, :])
        dma("G", a_dbg[:, 40:72][:, 0:24] if False else a_dbg[:, 40:64], b1c[:, 0:24])
    for c in range(NCK):
        ld(stg[:, 0:4096], w1T[c * 128:(c + 1) * 128, :])
        W("S", "D")
        for q in range(4):
            S(lambda e, q=q: e.activation(out=ysb[:, q * 1024:(q + 1) * 1024],
                                          in_=stg[:, q * 1024:(q + 1) * 1024], func=AF.Sign))
        W("G", "S")
        dma("G", sw1_s[c * 128:(c + 1) * 128, :], ysb[:, 0:4096])
    for hc in range(32):
        ld(stg[:, 0:1024], w2T[hc * 128:(hc + 1) * 128, :])
        W("S", "D")
        S(lambda e: e.activation(out=ysb[:, 0:1024], in_=stg[:, 0:1024], func=AF.Sign))
        W("G", "S")
        dma("G", sw2_s[hc * 128:(hc + 1) * 128, :], ysb[:, 0:1024])
    for it in range(NT):
        t0 = it * TT
        for c in range(NCK):
            ld(xnA[:, c * 512:(c + 1) * 512], xn_s[c * 128:(c + 1) * 128, t0:t0 + TT])
        W("T", "D")
        for hc in range(32):
            half = hc % 2
            for c in range(NCK):
                ld(wStr[:, half * 1024 + c * 128:half * 1024 + (c + 1) * 128],
                   sw1_s[c * 128:(c + 1) * 128, hc * 128:(hc + 1) * 128])
            W("T", "D")
            bank = 4 + half
            guard_write([bank])
            for c in range(NCK):
                mmp(ps[bank][:, 0:512], wStr[:, half * 1024 + c * 128:half * 1024 + (c + 1) * 128],
                    xnA[:, c * 512:(c + 1) * 512], c == 0, c == NCK - 1, [bank])
            guard_read("S", [bank])
            hbuf = wA if hc < 16 else wB
            hoff = (hc % 16) * 512
            S(lambda e, hc=hc, bank=bank, hbuf=hbuf, hoff=hoff: e.activation(
                out=hbuf[:, hoff:hoff + 512], in_=ps[bank][:, 0:512], func=AF.Gelu,
                bias=b1c[:, hc:hc + 1], scale=alpha1[:, hc:hc + 1]))
            mark_read("S", [bank])
            if z_dbg is not None:
                guard_read("V", [bank])
                V(lambda e, bank=bank: e.tensor_copy(out=tA[:, :], in_=ps[bank][:, 0:512]))
                mark_read("V", [bank])
                W("G", "V")
                dma("G", z_dbg[hc * 128:(hc + 1) * 128, t0:t0 + TT], tA[:, :])
        if h_dbg is not None:
            W("G", "S")
            for hc in range(32):
                hbuf = wA if hc < 16 else wB
                hoff = (hc % 16) * 512
                dma("G", h_dbg[hc * 128:(hc + 1) * 128, t0:t0 + TT], hbuf[:, hoff:hoff + 512])
        W("T", "S")
        for c in range(NCK):
            bank = 6 + (c % 2)
            guard_write([bank])
            for hc in range(32):
                slot = hc % 8
                ld(wStr[:, slot * 128:(slot + 1) * 128],
                   sw2_s[hc * 128:(hc + 1) * 128, c * 128:(c + 1) * 128])
                W("T", "D")
                hbuf = wA if hc < 16 else wB
                hoff = (hc % 16) * 512
                mmp(ps[bank][:, 0:512], wStr[:, slot * 128:(slot + 1) * 128],
                    hbuf[:, hoff:hoff + 512], hc == 0, hc == 31, [bank])
            guard_read("S", [bank])
            S(lambda e, c=c, bank=bank: e.activation(
                out=gsb[:, c * 512:(c + 1) * 512], in_=ps[bank][:, 0:512], func=AF.Silu,
                bias=b2c[:, c:c + 1], scale=alpha2[:, c:c + 1]))
            mark_read("S", [bank])
        W("G", "S")
        for c in range(NCK):
            dma("G", g_s[c * 128:(c + 1) * 128, t0:t0 + TT], gsb[:, c * 512:(c + 1) * 512])

    load_w(wA, wrT)
    load_w(wB, woT)
    W("V", "G")
    V(lambda e: e.tensor_copy(out=crun[:, :], in_=carry0[:, :]))
    for it in range(NT):
        t0 = it * TT
        V(lambda e, t0=t0: e.tensor_scalar(out=ccol[:, :], in0=lfm[:, :], scalar1=float(t0),
                                           scalar2=None, op0=ALU.mult))
        V(lambda e: e.tensor_tensor(out=ccol[:, :], in0=ccol[:, :], in1=lfmH[:, :], op=ALU.add))
        V(lambda e, t0=t0: e.tensor_scalar(out=ccol2[:, :], in0=lfm[:, :], scalar1=float(t0 + 256),
                                           scalar2=None, op0=ALU.mult))
        V(lambda e: e.tensor_tensor(out=ccol2[:, :], in0=ccol2[:, :], in1=lfmH[:, :], op=ALU.add))
        W("S", "V")
        S(lambda e: e.activation(out=ccol[:, :], in_=ccol[:, :], func=AF.Exp))
        S(lambda e: e.activation(out=ccol2[:, :], in_=ccol2[:, :], func=AF.Exp))
        W("V", "S")
        for c in range(NCK):
            ld(xnA[:, c * 512:(c + 1) * 512], xn_s[c * 128:(c + 1) * 128, t0:t0 + TT])
        W("T", "D")
        def evac_r(c, bank):
            guard_read("S", [bank])
            S(lambda e, c=c, bank=bank: e.activation(out=rsb[:, c * 512:(c + 1) * 512],
                                                     in_=ps[bank][:, 0:512], func=AF.Sigmoid))
            mark_read("S", [bank])
        fm_proj(wA, xnA, [4, 5], evac_r)
        W("V", "S")
        for tc in range(4):
            ld(trm[:, :], terms_s[t0 + tc * 128:t0 + (tc + 1) * 128, :])
            W("T", "D")
            guard_write([6, 7])
            for c in range(NCK):
                bank = 6 + c // 4
                mmp(ps[bank][:, (c % 4) * 128:(c % 4 + 1) * 128], trm[:, c * 128:(c + 1) * 128],
                    ltri[:, :], True, True, [bank], inc=(c == NCK - 1))
            guard_read("V", [6, 7])
            scol = ccol if tc < 2 else ccol2
            dso = (tc % 2) * 128
            for c in range(NCK):
                bank = 6 + c // 4
                off = (c % 4) * 128
                V(lambda e, bank=bank, off=off, c=c: e.tensor_scalar(
                    out=tA[:, 0:128], in0=ps[bank][:, off:off + 128],
                    scalar1=crun[:, c:c + 1], scalar2=None, op0=ALU.add))
                V(lambda e, c=c, scol=scol: e.tensor_scalar(
                    out=tA[:, 128:256], in0=tA[:, 0:128],
                    scalar1=scol[:, c:c + 1], scalar2=None, op0=ALU.mult))
                V(lambda e, c=c, dso=dso: e.tensor_tensor(
                    out=tA[:, 256:384], in0=tA[:, 128:256],
                    in1=d512[:, c * 256 + dso:c * 256 + dso + 128], op=ALU.mult))
                V(lambda e, c=c, tc=tc: e.tensor_tensor(
                    out=ysb[:, c * 512 + tc * 128:c * 512 + (tc + 1) * 128],
                    in0=tA[:, 256:384], in1=rsb[:, c * 512 + tc * 128:c * 512 + (tc + 1) * 128],
                    op=ALU.mult))
                V(lambda e, c=c, bank=bank, off=off: e.tensor_tensor(
                    out=crun[:, c:c + 1], in0=crun[:, c:c + 1],
                    in1=ps[bank][:, off + 127:off + 128], op=ALU.add))
            mark_read("V", [6, 7])
        if b_dbg is not None and it == 0:
            V(lambda e: e.tensor_copy(out=cbuf[:, 0:8], in_=ccol[:, :]))
            V(lambda e: e.tensor_copy(out=cbuf[:, 8:16], in_=ccol2[:, :]))
            V(lambda e: e.tensor_copy(out=cbuf[:, 16:24], in_=crun[:, :]))
            V(lambda e: e.tensor_copy(out=cbuf[:, 24:32], in_=carry0[:, :]))
            V(lambda e: e.tensor_copy(out=cbuf[:, 32:40], in_=lfm[:, :]))
            V(lambda e: e.tensor_copy(out=cbuf[:, 40:48], in_=lfmH[:, :]))
            V(lambda e: e.tensor_copy(out=cbuf[:, 48:56], in_=d512[:, 0:8]))
            W("G", "V")
            dma("G", b_dbg[:, :], cbuf[:, 0:64])
            for c in range(NCK):
                dma("G", y0_dbg[c * 128:(c + 1) * 128, :], ysb[:, c * 512:(c + 1) * 512])
        for c in range(NCK):
            W("S", "V"); W("S", "T")
            S(lambda e, c=c: e.activation(out=sqb[:, :], in_=ysb[:, c * 512:(c + 1) * 512], func=AF.Square))
            guard_write([0])
            W("T", "S")
            mmp(ps[0][0:1, 0:512], ones[:, 0:1], sqb[:, :], c == 0, c == NCK - 1, [0],
                inc=(c == NCK - 1))
        guard_read("V", [0])
        V(lambda e: e.tensor_scalar(out=rowB[:, :], in0=ps[0][0:1, 0:512], scalar1=1.0 / D,
                                    scalar2=EPS_RMS, op0=ALU.mult, op1=ALU.add))
        mark_read("V", [0])
        W("S", "V")
        S(lambda e: e.activation(out=rowB[:, :], in_=rowB[:, :], func=AF.Sqrt))
        W("V", "S")
        V(lambda e: e.reciprocal(out=rowB[:, :], in_=rowB[:, :]))
        SW("V")
        W("T", "V")
        guard_write([2])
        mmp(ps[2][:, 0:512], ones[0:1, :], rowB[:, :], True, True, [2])
        guard_read("V", [2])
        V(lambda e: e.tensor_copy(out=absbB[:, :], in_=ps[2][:, 0:512]))
        mark_read("V", [2])
        for c in range(NCK):
            V(lambda e, c=c: e.tensor_scalar(out=tBb[:, :], in0=ysb[:, c * 512:(c + 1) * 512],
                                             scalar1=ln0w[:, c:c + 1], scalar2=None, op0=ALU.mult))
            V(lambda e, c=c: e.tensor_tensor(out=ysb[:, c * 512:(c + 1) * 512],
                                             in0=tBb[:, :], in1=absbB[:, :], op=ALU.mult))
        W("T", "V")
        for c in range(NCK):
            bank = 4 + (c % 2)
            guard_write([bank])
            for k in range(NCK):
                mmp(ps[bank][:, 0:512], wB[:, k * 1024 + c * 128:k * 1024 + (c + 1) * 128],
                    ysb[:, k * 512:(k + 1) * 512], k == 0, k == NCK - 1, [bank])
            ld(jit1[:, :], x_d[c * 128:(c + 1) * 128, t0:t0 + TT])
            ld(jit2[:, :], g_s[c * 128:(c + 1) * 128, t0:t0 + TT])
            guard_read("V", [bank])
            W("V", "D")
            V(lambda e, bank=bank: e.tensor_tensor(out=tA[:, :], in0=ps[bank][:, 0:512],
                                                   in1=jit2[:, :], op=ALU.mult))
            mark_read("V", [bank])
            V(lambda e: e.tensor_tensor(out=sqb[:, :], in0=tA[:, :], in1=jit1[:, :], op=ALU.add))
            W("G", "V")
            dma("G", x1_s[c * 128:(c + 1) * 128, t0:t0 + TT], sqb[:, :])
        ln_stream(x1_s, t0, xnB, EPS_LN)
        W("G", "V")
        for c in range(NCK):
            dma("G", xn1_s[c * 128:(c + 1) * 128, t0:t0 + TT], xnB[:, c * 512:(c + 1) * 512])
        if it == NT - 1:
            for c in range(NCK):
                V(lambda e, c=c: e.tensor_copy(out=brow[:, c:c + 1],
                                               in_=xnB[:, c * 512 + 511:c * 512 + 512]))

    V(lambda e: e.tensor_scalar(out=cbuf[:, 0:8], in0=brow[:, :], scalar1=msr[:, 0:1],
                                scalar2=None, op0=ALU.mult))
    W("G", "V")
    dma("G", cc2i[:, :], cbuf[:, 0:8])
    CC(lambda e: e.collective_compute("AllReduce", ALU.add, replica_groups=rg,
                                      ins=[cc2i.ap().opt()], outs=[cc2o.ap().opt()]))
    dma("G", cbuf[:, 8:16], cc2o[:, :])
    W("V", "G")
    V(lambda e: e.tensor_scalar(out=cbuf[:, 16:24], in0=cbuf[:, 8:16], scalar1=msr[:, 1:2],
                                scalar2=None, op0=ALU.mult))
    V(lambda e: e.tensor_copy(out=pcol[:, :], in_=cbuf[:, 16:24]))

    load_w(wA, wr1T)
    load_w(wB, wk1T)
    for it in range(NT):
        t0 = it * TT
        for c in range(NCK):
            ld(xnB[:, c * 512:(c + 1) * 512], xn1_s[c * 128:(c + 1) * 128, t0:t0 + TT])
        W("V", "D")
        mk_delta(it)
        mk_mix(2)
        W("T", "V")
        def evac_cp(dstb):
            def f(c, bank):
                guard_read("V", [bank])
                V(lambda e, c=c, bank=bank, dstb=dstb: e.tensor_copy(
                    out=dstb[:, c * 512:(c + 1) * 512], in_=ps[bank][:, 0:512]))
                mark_read("V", [bank])
            return f
        fm_proj(wA, xmix, [4, 5], evac_cp(ysb))
        W("G", "V")
        for c in range(NCK):
            dma("G", r1_s[c * 128:(c + 1) * 128, t0:t0 + TT], ysb[:, c * 512:(c + 1) * 512])
        mk_mix(0)
        W("T", "V")
        fm_proj(wB, xmix, [4, 5], evac_cp(gsb))
        W("G", "V")
        for c in range(NCK):
            dma("G", kf_s[c * 128:(c + 1) * 128, t0:t0 + TT], gsb[:, c * 512:(c + 1) * 512])
        for tc in range(4):
            tm_proj(tc, wB, xmix, ktm)
            W("G", "V")
            dma("G", kt_s[t0 + tc * 128:t0 + (tc + 1) * 128, :], ktm[:, :])

    load_w(wA, wv1T)
    load_w(wB, wg1T)
    V(lambda e: e.memset(state_H[:, :], 0.0))
    V(lambda e: e.tensor_copy(out=pcol[:, :], in_=cbuf[:, 16:24]))
    for it in range(NT):
        t0 = it * TT
        for c in range(NCK):
            ld(xnB[:, c * 512:(c + 1) * 512], xn1_s[c * 128:(c + 1) * 128, t0:t0 + TT])
        W("V", "D")
        mk_delta(it)
        mk_mix(3)
        W("T", "V")
        def evac_gelu(c, bank):
            guard_read("S", [bank])
            S(lambda e, c=c, bank=bank: e.activation(out=gsb[:, c * 512:(c + 1) * 512],
                                                     in_=ps[bank][:, 0:512], func=AF.Gelu))
            mark_read("S", [bank])
        fm_proj(wB, xmix, [4, 5], evac_gelu)
        W("G", "S")
        for c in range(NCK):
            dma("G", g1_s[c * 128:(c + 1) * 128, t0:t0 + TT], gsb[:, c * 512:(c + 1) * 512])
        mk_mix(1)
        W("T", "V")
        for tc in range(4):
            tm_proj(tc, wA, xmix, vtm)
            W("G", "V")
            dma("G", vt_s[t0 + tc * 128:t0 + (tc + 1) * 128, :], vtm[:, :])
            for par in range(2):
                tj = t0 + tc * 128 + par * 64
                ld(kchk, kt_s[tj:tj + 64, :])
                ld(vchk, vt_s[tj:tj + 64, :])
                W("V", "D")
                V(lambda e: e.tensor_tensor(out=kwkH, in0=kchk, in1=wkrep[0:64, :], op=ALU.mult))
                scan_chunk()

    V(lambda e: e.tensor_scalar(out=stg[0:64, 0:1024], in0=state_H[:, :], scalar1=msr[0:64, 0:1],
                                scalar2=None, op0=ALU.mult))
    W("G", "V")
    dma("G", cc3i[0:64, 0:1024], stg[0:64, 0:1024])
    CC(lambda e: e.collective_compute("AllReduce", ALU.add, replica_groups=rg,
                                      ins=[cc3i.ap().opt()], outs=[cc3o.ap().opt()]))
    dma("G", stg[0:64, 1024:2048], cc3o[0:64, 0:1024])
    W("V", "G")
    V(lambda e: e.tensor_scalar(out=state_H[:, :], in0=stg[0:64, 1024:2048], scalar1=msr[0:64, 1:2],
                                scalar2=None, op0=ALU.mult))

    def gamma_pass(wsrc, nchunks, width, gdst):
        V(lambda e: e.memset(rowg[:, :], 0.0))
        for c in range(nchunks):
            ld(stg[:, 0:width], wsrc[c * 128:(c + 1) * 128, :])
            W("S", "D"); W("S", "T"); W("S", "V")
            S(lambda e, width=width: e.activation(out=stg[:, 0:width], in_=stg[:, 0:width], func=AF.Abs))
            W("T", "S")
            for eb in range(width // 512):
                bank = eb % 2
                guard_write([bank])
                mmp(ps[bank][0:1, 0:512], ones[:, 0:1], stg[:, eb * 512:(eb + 1) * 512],
                    True, True, [bank], inc=True)
                guard_read("V", [bank])
                V(lambda e, bank=bank: e.tensor_tensor(
                    out=rowg[:, 0:512], in0=rowg[:, 0:512],
                    in1=ps[bank][0:1, 0:512], op=ALU.add))
                mark_read("V", [bank])
        V(lambda e: e.reduce_sum(out=rowA[:, 0:1], in_=rowg[:, 0:512], axis=AX.X))
        V(lambda e: e.tensor_scalar(out=rowA[:, 0:1], in0=rowA[:, 0:1], scalar1=1.0 / (D * 4096),
                                    scalar2=1e-6, op0=ALU.mult, op1=ALU.add))
        V(lambda e: e.tensor_scalar(out=rowA[:, 1:2], in0=rowA[:, 0:1], scalar1=0.5,
                                    scalar2=None, op0=ALU.mult))
        W("T", "V")
        guard_write([2])
        mmp(ps[2][:, 0:1], ones[0:1, :], rowA[:, 1:2], True, True, [2])
        guard_read("V", [2])
        V(lambda e, gdst=gdst: e.tensor_copy(out=gdst[:, :], in_=ps[2][:, 0:1]))
        mark_read("V", [2])

    def quant_pass(wsrc, nchunks, width, gc, dst_s):
        for c in range(nchunks):
            ld(stg[:, 0:width], wsrc[c * 128:(c + 1) * 128, :])
            W("S", "D"); W("S", "V"); W("S", "G")
            for q in range(max(1, width // 1024)):
                S(lambda e, q=q: e.activation(out=ysb[:, q * 1024:(q + 1) * 1024],
                                              in_=stg[:, q * 1024:(q + 1) * 1024], func=AF.Sign))
            S(lambda e, width=width: e.activation(out=stg[:, 0:width], in_=stg[:, 0:width], func=AF.Abs))
            W("V", "S")
            V(lambda e, width=width, gc=gc: e.tensor_scalar(out=stg[:, 0:width], in0=stg[:, 0:width],
                                                            scalar1=gc[:, 0:1], scalar2=None,
                                                            op0=ALU.is_ge))
            V(lambda e, width=width: e.tensor_copy(out=gsb[:, 0:width], in_=stg[:, 0:width]))
            V(lambda e, width=width: e.tensor_tensor(out=ysb[:, 0:width], in0=ysb[:, 0:width],
                                                     in1=gsb[:, 0:width], op=ALU.mult))
            W("G", "V")
            dma("G", dst_s[c * 128:(c + 1) * 128, :], ysb[:, 0:width])

    gamma_pass(wfcT, NCK, 4096, gcol)
    gamma_pass(wpjT, 32, 1024, gcol2)
    SW("V")
    quant_pass(wfcT, NCK, 4096, gcol, wqf_s)
    quant_pass(wpjT, 32, 1024, gcol2, wqp_s)
    for (target, lncol) in [(cs1, lnfwB), (cs2, lnfbB)]:
        for c in range(NCK):
            bank = c % 2
            guard_write([bank])
            for hc in range(32):
                slot = hc % 8
                ld(wStr[:, slot * 128:(slot + 1) * 128],
                   wqp_s[hc * 128:(hc + 1) * 128, c * 128:(c + 1) * 128])
                W("T", "D")
                mmp(ps[bank][:, 0:1], wStr[:, slot * 128:(slot + 1) * 128],
                    lncol[:, hc:hc + 1], hc == 0, hc == 31, [bank], inc=True)
            guard_read("V", [bank])
            V(lambda e, c=c, bank=bank, target=target: e.tensor_copy(out=target[:, c:c + 1],
                                                                     in_=ps[bank][:, 0:1]))
            mark_read("V", [bank])
    for it in range(NT):
        t0 = it * TT
        for c in range(NCK):
            ld(xnB[:, c * 512:(c + 1) * 512], xn1_s[c * 128:(c + 1) * 128, t0:t0 + TT])
        W("T", "D")
        for hc in range(32):
            half = hc % 2
            for c in range(NCK):
                ld(wStr[:, half * 1024 + c * 128:half * 1024 + (c + 1) * 128],
                   wqf_s[c * 128:(c + 1) * 128, hc * 128:(hc + 1) * 128])
            W("T", "D")
            bank = 4 + half
            guard_write([bank])
            for c in range(NCK):
                mmp(ps[bank][:, 0:512], wStr[:, half * 1024 + c * 128:half * 1024 + (c + 1) * 128],
                    xnB[:, c * 512:(c + 1) * 512], c == 0, c == NCK - 1, [bank])
            guard_read("S", [bank])
            hbuf = wA if hc < 16 else wB
            hoff = (hc % 16) * 512
            S(lambda e, bank=bank, hbuf=hbuf, hoff=hoff: e.activation(
                out=hbuf[:, hoff:hoff + 512], in_=ps[bank][:, 0:512], func=AF.Gelu))
            mark_read("S", [bank])
        for hc in range(32):
            hbuf = wA if hc < 16 else wB
            hoff = (hc % 16) * 512
            W("S", "T")
            S(lambda e, hbuf=hbuf, hoff=hoff: e.activation(out=sqb[:, :], in_=hbuf[:, hoff:hoff + 512],
                                                           func=AF.Square))
            guard_write([0, 1])
            W("T", "S")
            mmp(ps[0][0:1, 0:512], onesb[:, 0:1], hbuf[:, hoff:hoff + 512], hc == 0, hc == 31,
                [0], inc=False)
            mmp(ps[1][0:1, 0:512], ones[:, 0:1], sqb[:, :], hc == 0, hc == 31, [1])
        guard_read("V", [0, 1])
        V(lambda e: e.tensor_scalar(out=rowA[:, :], in0=ps[0][0:1, 0:512], scalar1=1.0 / 4096,
                                    scalar2=None, op0=ALU.mult))
        V(lambda e: e.tensor_scalar(out=rowB[:, :], in0=ps[1][0:1, 0:512], scalar1=1.0 / 4096,
                                    scalar2=None, op0=ALU.mult))
        mark_read("V", [0, 1])
        V(lambda e: e.tensor_tensor(out=rowC[:, :], in0=rowA[:, :], in1=rowA[:, :], op=ALU.mult))
        V(lambda e: e.tensor_tensor(out=rowB[:, :], in0=rowB[:, :], in1=rowC[:, :], op=ALU.subtract))
        V(lambda e: e.tensor_scalar(out=rowB[:, :], in0=rowB[:, :], scalar1=EPS_LNF,
                                    scalar2=None, op0=ALU.add))
        W("S", "V")
        S(lambda e: e.activation(out=rowB[:, :], in_=rowB[:, :], func=AF.Sqrt))
        W("V", "S")
        V(lambda e: e.reciprocal(out=rowB[:, :], in_=rowB[:, :]))
        SW("V")
        V(lambda e: e.tensor_tensor(out=rowC[:, :], in0=rowA[:, :], in1=rowB[:, :], op=ALU.mult))
        V(lambda e: e.tensor_scalar(out=rowC[:, :], in0=rowC[:, :], scalar1=-1.0,
                                    scalar2=None, op0=ALU.mult))
        W("T", "V")
        guard_write([2, 3])
        mmp(ps[2][:, 0:512], ones[0:1, :], rowB[:, :], True, True, [2], inc=False)
        mmp(ps[3][:, 0:512], ones[0:1, :], rowC[:, :], True, True, [3])
        guard_read("V", [2, 3])
        V(lambda e: e.tensor_copy(out=absb[:, :], in_=ps[2][:, 0:512]))
        V(lambda e: e.tensor_copy(out=bbsb[:, :], in_=ps[3][:, 0:512]))
        mark_read("V", [2, 3])
        for hc in range(32):
            hbuf = wA if hc < 16 else wB
            hoff = (hc % 16) * 512
            W("V", "T")
            V(lambda e, hc=hc, hbuf=hbuf, hoff=hoff: e.tensor_scalar(
                out=hbuf[:, hoff:hoff + 512], in0=hbuf[:, hoff:hoff + 512],
                scalar1=lnfw[:, hc:hc + 1], scalar2=None, op0=ALU.mult))
        W("T", "V")
        for c in range(NCK):
            bank = 6 + (c % 2)
            guard_write([bank])
            for hc in range(32):
                slot = hc % 8
                ld(wStr[:, slot * 128:(slot + 1) * 128],
                   wqp_s[hc * 128:(hc + 1) * 128, c * 128:(c + 1) * 128])
                W("T", "D")
                hbuf = wA if hc < 16 else wB
                hoff = (hc % 16) * 512
                mmp(ps[bank][:, 0:512], wStr[:, slot * 128:(slot + 1) * 128],
                    hbuf[:, hoff:hoff + 512], hc == 0, hc == 31, [bank])
            guard_read("V", [bank])
            V(lambda e, bank=bank: e.tensor_tensor(out=tA[:, :], in0=ps[bank][:, 0:512],
                                                   in1=absb[:, :], op=ALU.mult))
            mark_read("V", [bank])
            V(lambda e, c=c: e.tensor_scalar(out=sqb[:, :], in0=bbsb[:, :], scalar1=cs1[:, c:c + 1],
                                             scalar2=cs2[:, c:c + 1], op0=ALU.mult, op1=ALU.add))
            V(lambda e: e.tensor_tensor(out=tA[:, :], in0=tA[:, :], in1=sqb[:, :], op=ALU.add))
            W("S", "V")
            S(lambda e, c=c: e.activation(out=gsb[:, c * 512:(c + 1) * 512], in_=tA[:, :], func=AF.Silu))
        W("G", "S")
        for c in range(NCK):
            dma("G", g2_s[c * 128:(c + 1) * 128, t0:t0 + TT], gsb[:, c * 512:(c + 1) * 512])

    load_w(wA, wo1T)
    kH = dlt[0:64, 0:4096]
    rH = xmix[0:64, 0:4096]
    rwbH = xnB[0:64, 0:4096]
    frepR = gsb[0:64, 0:4096]
    for h in range(HD):
        W("S", "V"); W("S", "G")
        S(lambda e, h=h: e.activation(out=frepR[:, h * 256:(h + 1) * 256], in_=tmod[0:64, 0:256],
                                      func=AF.Exp, scale=lnwH[:, h:h + 1]))
    NHT = 2 * NT
    for ht in range(NHT):
        t0 = ht * 256
        E("D", lambda e, t0=t0: e.dma_start(
            out=kH, in_=ap3(kf_s, t0, [[TLOC, 64], [64 * TLOC, 16], [1, 256]])), 16)
        E("D", lambda e, t0=t0: e.dma_start(
            out=rH, in_=ap3(r1_s, t0, [[TLOC, 64], [64 * TLOC, 16], [1, 256]])), 16)
        W("V", "D"); W("V", "S")
        V(lambda e: e.tensor_tensor(out=rwbH, in0=rH, in1=frepR, op=ALU.mult))
        for j in range(4):
            tj = t0 + j * 64
            ld(kchk, kt_s[tj:tj + 64, :])
            ld(vchk, vt_s[tj:tj + 64, :])
            W("V", "D")
            V(lambda e: e.tensor_tensor(out=kwkH, in0=kchk, in1=wkrep[0:64, :], op=ALU.mult))
            W("T", "V")
            guard_write([2, 3])
            for h in range(HD):
                bank = 2 + h // 8
                mmp(ps[bank][0:64, (h % 8) * 64:(h % 8) * 64 + 64],
                    kH[:, h * 256 + j * 64:h * 256 + j * 64 + 64],
                    rH[:, h * 256 + j * 64:h * 256 + j * 64 + 64],
                    True, True, [bank], inc=(h % 8 == 7))
            guard_read("V", [2, 3])
            V(lambda e: e.tensor_tensor(out=attwt[0:64, 0:512], in0=ps[2][0:64, 0:512],
                                        in1=wmt[0:64, 0:512], op=ALU.mult))
            V(lambda e: e.tensor_tensor(out=attwt[0:64, 512:1024], in0=ps[3][0:64, 0:512],
                                        in1=wmt[0:64, 512:1024], op=ALU.mult))
            mark_read("V", [2, 3])
            V(lambda e: e.tensor_copy(out=stateb_H[:, :], in_=state_H[:, :]))
            W("T", "V")
            guard_write([4])
            for h in range(HD):
                hp = 64 * (h % 2)
                mmp(ps[4][hp:hp + 64, (h // 2) * 64:(h // 2) * 64 + 64],
                    vchk[:, h * 64:(h + 1) * 64],
                    attwt[0:64, h * 64:(h + 1) * 64],
                    True, False, [4], inc=False, tp=(0, hp))
                mmp(ps[4][hp:hp + 64, (h // 2) * 64:(h // 2) * 64 + 64],
                    stateb_H[:, h * 64:(h + 1) * 64],
                    rwbH[:, h * 256 + j * 64:h * 256 + j * 64 + 64],
                    False, True, [4], inc=(h == HD - 1), tp=(0, hp))
            guard_read("V", [4])
            V(lambda e, j=j: e.tensor_copy(
                out=ap3(yL1, j * 64, [[2048, 128], [256, 8], [1, 64]]),
                in_=ap3(ps[4], 0, [[512, 128], [64, 8], [1, 64]])))
            mark_read("V", [4])
            scan_chunk()
        for p in range(NCK):
            W("S", "T"); W("S", "V")
            S(lambda e, p=p: e.activation(out=sqb[:, 0:256], in_=yL1[:, p * 256:(p + 1) * 256],
                                          func=AF.Square))
            guard_write([2, 3])
            W("T", "S"); W("T", "V")
            mmp(ps[2][0:16, 0:256], e16b[:, p * 16:(p + 1) * 16], yL1[:, p * 256:(p + 1) * 256],
                p == 0, p == NCK - 1, [2], inc=False)
            mmp(ps[3][0:16, 0:256], e16[:, p * 16:(p + 1) * 16], sqb[:, 0:256], p == 0, p == NCK - 1, [3])
        guard_read("V", [2, 3])
        V(lambda e: e.tensor_scalar(out=st16[:, 0:256], in0=ps[2][0:16, 0:256], scalar1=1.0 / 64,
                                    scalar2=None, op0=ALU.mult))
        V(lambda e: e.tensor_scalar(out=st16b[:, 0:256], in0=ps[3][0:16, 0:256], scalar1=1.0 / 64,
                                    scalar2=None, op0=ALU.mult))
        mark_read("V", [2, 3])
        V(lambda e: e.tensor_tensor(out=rows16[:, 0:256], in0=st16[:, 0:256], in1=st16[:, 0:256],
                                    op=ALU.mult))
        V(lambda e: e.tensor_tensor(out=st16b[:, 0:256], in0=st16b[:, 0:256], in1=rows16[:, 0:256],
                                    op=ALU.subtract))
        V(lambda e: e.tensor_scalar(out=st16b[:, 0:256], in0=st16b[:, 0:256], scalar1=EPS_GN,
                                    scalar2=None, op0=ALU.add))
        W("S", "V")
        S(lambda e: e.activation(out=st16b[:, 0:256], in_=st16b[:, 0:256], func=AF.Sqrt))
        W("V", "S")
        V(lambda e: e.reciprocal(out=st16b[:, 0:256], in_=st16b[:, 0:256]))
        SW("V")
        V(lambda e: e.tensor_tensor(out=st16[:, 0:256], in0=st16[:, 0:256], in1=st16b[:, 0:256],
                                    op=ALU.mult))
        V(lambda e: e.tensor_scalar(out=st16[:, 0:256], in0=st16[:, 0:256], scalar1=-1.0,
                                    scalar2=None, op0=ALU.mult))
        W("T", "V")
        for p in range(NCK):
            guard_write([2, 3])
            mmp(ps[2][:, 0:256], e16t[:, p * 128:(p + 1) * 128], st16b[0:16, 0:256], True, True,
                [2], inc=False)
            mmp(ps[3][:, 0:256], e16t[:, p * 128:(p + 1) * 128], st16[0:16, 0:256], True, True, [3])
            ld(jit2[:, 0:256], g1_s[p * 128:(p + 1) * 128, t0:t0 + 256])
            guard_read("V", [2, 3])
            W("V", "D")
            V(lambda e, p=p: e.tensor_tensor(out=tA[:, 0:256], in0=yL1[:, p * 256:(p + 1) * 256],
                                             in1=ps[2][:, 0:256], op=ALU.mult))
            V(lambda e: e.tensor_tensor(out=tA[:, 0:256], in0=tA[:, 0:256], in1=ps[3][:, 0:256],
                                        op=ALU.add))
            mark_read("V", [2, 3])
            V(lambda e, p=p: e.tensor_scalar(out=tBb[:, 0:256], in0=tA[:, 0:256],
                                             scalar1=ln1w[:, p:p + 1],
                                             scalar2=ln1b[:, p:p + 1], op0=ALU.mult, op1=ALU.add))
            V(lambda e, p=p: e.tensor_tensor(out=ysb[:, p * 256:(p + 1) * 256], in0=tBb[:, 0:256],
                                             in1=jit2[:, 0:256], op=ALU.mult))
        W("T", "V")
        for c in range(NCK):
            bank = 6 + (c % 2)
            guard_write([bank])
            for k in range(NCK):
                mmp(ps[bank][:, 0:256], wA[:, k * 1024 + c * 128:k * 1024 + (c + 1) * 128],
                    ysb[:, k * 256:(k + 1) * 256], k == 0, k == NCK - 1, [bank])
            ld(jit1[:, 0:256], x1_s[c * 128:(c + 1) * 128, t0:t0 + 256])
            ld(jit2[:, 0:256], g2_s[c * 128:(c + 1) * 128, t0:t0 + 256])
            guard_read("V", [bank])
            W("V", "D")
            V(lambda e, bank=bank: e.tensor_tensor(out=tA[:, 0:256], in0=ps[bank][:, 0:256],
                                                   in1=jit2[:, 0:256], op=ALU.mult))
            mark_read("V", [bank])
            V(lambda e: e.tensor_tensor(out=sqb[:, 0:256], in0=tA[:, 0:256], in1=jit1[:, 0:256],
                                        op=ALU.add))
            W("G", "V")
            dma("G", out_d[c * 128:(c + 1) * 128, t0:t0 + 256], sqb[:, 0:256])

    steps.append(("G", lambda eng, v=cnt["G"]: eng.wait_ge(sem_handles["G"], v)))

    if nsteps is not None:
        nsteps = min(nsteps, len(cnt_hist))
        final = cnt_hist[nsteps - 1]
        steps = steps[:nsteps]
        for e in ("T", "V", "S", "D", "G"):
            for s in ("T", "V", "S", "D", "G", "C"):
                vv = final.get(s, 0)
                if vv > 0:
                    steps.append((e, lambda eng, s2=s, v=vv: eng.wait_ge(sem_handles[s2], v)))
    with nc.semaphore("semT") as sT, nc.semaphore("semV") as sV, \
         nc.semaphore("semS") as sS, nc.semaphore("semD") as sD, nc.semaphore("semG") as sG, \
         nc.semaphore("semC") as sC, nc.Block() as block:
        sem_handles.update({"T": sT, "V": sV, "S": sS, "D": sD, "G": sG, "C": sC})
        @block.tensor
        def _(eng):
            for en, fn in steps:
                if en == "T":
                    fn(eng)
        @block.vector
        def _(eng):
            for en, fn in steps:
                if en == "V":
                    fn(eng)
        @block.scalar
        def _(eng):
            for en, fn in steps:
                if en == "S":
                    fn(eng)
        @block.sync
        def _(eng):
            for en, fn in steps:
                if en == "D":
                    fn(eng)
        @block.gpsimd
        def _(eng):
            for en, fn in steps:
                if en == "G":
                    fn(eng)
    return nc


def _consts():
    i128 = np.arange(128)
    c = {}
    c["c_ones"] = np.ones((128, 128), np.float32)
    c["c_onesb"] = np.ones((128, 128), BF)
    c["c_ident"] = np.eye(128, dtype=np.float32)
    c["c_ltri"] = (np.arange(128)[:, None] <= np.arange(128)[None, :]).astype(BF)
    e16 = np.zeros((128, 128), np.float32)
    for p in range(8):
        for r in range(128):
            e16[r, p * 16 + (2 * p + r // 64)] = 1.0
    c["c_e16"] = e16
    e16t = np.zeros((16, 1024), np.float32)
    for p in range(8):
        for r in range(128):
            e16t[2 * p + r // 64, p * 128 + r] = 1.0
    c["c_e16t"] = e16t
    s64 = np.zeros((16, 1024), np.float32)
    for k in range(16):
        s64[k, k * 64:(k + 1) * 64] = 1.0
    c["c_s64"] = s64
    psl = np.zeros((16, 8), np.float32)
    for k in range(16):
        psl[k, k // 2] = 1.0
    c["c_psel"] = psl
    hs = np.zeros((16, 128), np.float32)
    for k in range(16):
        hs[k, (k % 2) * 64:(k % 2) * 64 + 64] = 1.0
    c["c_hsel"] = hs
    q2g = i128 % 64
    q1g = np.arange(1024) % 64
    c["c_edt"] = np.maximum(q1g[None, :] - q2g[:, None] - 1, 0).astype(np.float32)
    c["c_trit"] = (q1g[None, :] > q2g[:, None]).astype(np.float32)
    c["c_eyet"] = (q1g[None, :] == q2g[:, None]).astype(np.float32)
    c["c_tmod"] = np.broadcast_to((np.arange(512) % 64).astype(BF), (128, 512)).copy()
    c["c_qrev"] = (63 - (i128 % 64)).astype(np.float32).reshape(128, 1)
    c["c_qidx"] = i128.astype(np.float32).reshape(128, 1)
    c["c_ar256"] = np.broadcast_to(np.arange(256).astype(BF), (128, 256)).copy()
    return c


def _vec_fm(v, k):
    return np.ascontiguousarray(v.reshape(k, 128).T.astype(np.float32))


def make_shared(inputs):
    f = np.float32
    T = lambda a: np.ascontiguousarray(np.asarray(a, f).T)
    C = lambda a: np.ascontiguousarray(np.asarray(a, f))
    shared = {
        "wkT": T(inputs["L0_attn_Wk"]), "wvT": T(inputs["L0_attn_Wv"]),
        "wrT": T(inputs["L0_attn_Wr"]), "woT": T(inputs["L0_attn_Wo"]),
        "w1T": T(inputs["L0_ffn_w1"]), "w1N": C(inputs["L0_ffn_w1"]),
        "w2T": T(inputs["L0_ffn_w2"]), "w2N": C(inputs["L0_ffn_w2"]),
        "wr1T": T(inputs["L1_attn_Wr"]), "wk1T": T(inputs["L1_attn_Wk"]),
        "wv1T": T(inputs["L1_attn_Wv"]), "wg1T": T(inputs["L1_attn_Wg"]),
        "wo1T": T(inputs["L1_attn_Wo"]),
        "wfcT": T(inputs["L1_ffn_W_fc"]), "wpjT": T(inputs["L1_ffn_W_proj"]),
        "p_decay": _vec_fm(np.asarray(inputs["L0_attn_decay"], f), 8),
        "p_decrow": np.ascontiguousarray(np.asarray(inputs["L0_attn_decay"], f).reshape(1, 1024)),
        "p_ln0w": _vec_fm(np.asarray(inputs["L0_attn_ln_w"], f), 8),
        "p_b1": _vec_fm(np.asarray(inputs["L0_ffn_b1"], f), 32),
        "p_b2": _vec_fm(np.asarray(inputs["L0_ffn_b2"], f), 8),
        "p_maa": np.concatenate([_vec_fm(np.asarray(inputs[n], f).reshape(-1), 8)
                                 for n in ["L1_attn_time_maa_k", "L1_attn_time_maa_v",
                                           "L1_attn_time_maa_r", "L1_attn_time_maa_g"]], axis=1),
        "p_td": np.asarray(inputs["L1_attn_time_decay"], f).reshape(16, 1),
        "p_fa": np.asarray(inputs["L1_attn_time_faaaa"], f).reshape(16, 1),
        "p_ln1w": _vec_fm(np.asarray(inputs["L1_attn_ln_w"], f), 8),
        "p_ln1b": _vec_fm(np.asarray(inputs["L1_attn_ln_b"], f), 8),
        "p_lnfw": _vec_fm(np.asarray(inputs["L1_ffn_ln_w"], f), 32),
        "p_lnfb": _vec_fm(np.asarray(inputs["L1_ffn_ln_b"], f), 32),
    }
    shared.update(_consts())
    return shared


_CACHE = {}


def kernel(**inputs):
    NT = 8
    TLOC = NT * TT
    key = ("prog", NT)
    if key not in _CACHE:
        _CACHE[key] = build_program(NT=NT)
    nc = _CACHE[key]
    x = np.asarray(inputs["L0_attn_x"], np.float32)
    shared = make_shared(inputs)
    in_maps = []
    for core in range(8):
        b, h = core // 2, core % 2
        m = np.zeros((128, 2), np.float32)
        m[:, 0] = 1.0 - h
        m[:, 1] = h
        im = dict(shared)
        im["x"] = np.ascontiguousarray(x[b, h * TLOC:(h + 1) * TLOC, :].T)
        im["m_sr"] = m
        in_maps.append(im)
    res = run_bass_kernel_spmd(nc, in_maps, core_ids=list(range(8)))
    out = np.empty((4, 2 * TLOC, D), np.float32)
    for core in range(8):
        b, h = core // 2, core % 2
        out[b, h * TLOC:(h + 1) * TLOC, :] = np.asarray(res.results[core]["out"]).T
    return out


# version 11
# speedup vs baseline: 1.6668x; 1.6668x over previous
import math
import numpy as np
import ml_dtypes
import concourse.bass as bass
import concourse.mybir as mybir
from concourse.bass_utils import run_bass_kernel_spmd

F32 = mybir.dt.float32
BF16 = mybir.dt.bfloat16
AF = mybir.ActivationFunctionType
ALU = mybir.AluOpType
AX = mybir.AxisListType
BF = ml_dtypes.bfloat16

D = 1024
NCK = 8
HD = 16
TT = 512
EPS_LN = 1e-5
EPS_RMS = 1e-6
EPS_GN = 1e-5 * 64
EPS_LNF = 1e-5


def ap3(t, off, dims):
    return bass.AP(t, off, dims)


def build_program(NT=8, rg=None, taps=(), strict=False, nsteps=None):
    if rg is None:
        rg = [[0, 1], [2, 3], [4, 5], [6, 7]]
    TLOC = NT * TT
    nc = bass.Bass("TRN2", target_bir_lowering=False,
                   detect_race_conditions=bool(strict))

    def din(name, shape, dt=F32):
        return nc.dram_tensor(name, list(shape), dt, kind="ExternalInput")

    x_d = din("x", [D, TLOC])
    wkT = din("wkT", [D, D]); wvT = din("wvT", [D, D])
    wrT = din("wrT", [D, D]); woT = din("woT", [D, D])
    w1T = din("w1T", [D, 4096]); w1N = din("w1N", [4096, D])
    w2T = din("w2T", [4096, D]); w2N = din("w2N", [D, 4096])
    wr1T = din("wr1T", [D, D]); wk1T = din("wk1T", [D, D]); wv1T = din("wv1T", [D, D])
    wg1T = din("wg1T", [D, D]); wo1T = din("wo1T", [D, D])
    wfcT = din("wfcT", [D, 4096]); wpjT = din("wpjT", [4096, D])
    p_decay = din("p_decay", [128, 8]); p_ln0w = din("p_ln0w", [128, 8])
    p_b1 = din("p_b1", [128, 32]); p_b2 = din("p_b2", [128, 8])
    p_maa = din("p_maa", [128, 32])
    p_td = din("p_td", [16, 1]); p_fa = din("p_fa", [16, 1])
    p_ln1w = din("p_ln1w", [128, 8]); p_ln1b = din("p_ln1b", [128, 8])
    p_lnfw = din("p_lnfw", [128, 32]); p_lnfb = din("p_lnfb", [128, 32])
    c_ones = din("c_ones", [128, 128])
    c_onesb = din("c_onesb", [128, 128], BF16)
    c_ident = din("c_ident", [128, 128])
    c_ltri = din("c_ltri", [128, 128], BF16)
    c_e16 = din("c_e16", [128, 128])
    c_e16t = din("c_e16t", [16, 1024])
    c_s64 = din("c_s64", [16, 1024])
    c_psel = din("c_psel", [16, 8])
    c_hsel = din("c_hsel", [16, 128])
    c_edt = din("c_edt", [128, 1024])
    c_trit = din("c_trit", [128, 1024])
    c_eyet = din("c_eyet", [128, 1024])
    c_tmod = din("c_tmod", [128, 512], BF16)
    c_qrev = din("c_qrev", [128, 1])
    p_decrow = din("p_decrow", [1, 1024])
    c_qidx = din("c_qidx", [128, 1])
    c_ar256 = din("c_ar256", [128, 256], BF16)
    m_sr = din("m_sr", [128, 2])

    def scr(name, shape, dt):
        k = "ExternalOutput" if name in taps else "Internal"
        return nc.dram_tensor(name, list(shape), dt, kind=k)

    xn_s = scr("xn_s", [D, TLOC], BF16)
    terms_s = scr("terms_s", [TLOC, D], BF16)
    g_s = scr("g_s", [D, TLOC], BF16)
    x1_s = scr("x1_s", [D, TLOC], F32)
    xn1_s = scr("xn1_s", [D, TLOC], BF16)
    r1_s = scr("r1_s", [D, TLOC], BF16)
    kf_s = scr("kf_s", [D, TLOC], BF16)
    kt_s = scr("kt_s", [TLOC, D], BF16)
    vt_s = scr("vt_s", [TLOC, D], BF16)
    g1_s = scr("g1_s", [D, TLOC], BF16)
    g2_s = scr("g2_s", [D, TLOC], BF16)
    sw1_s = scr("sw1_s", [D, 4096], BF16)
    sw2_s = scr("sw2_s", [4096, D], BF16)
    wqf_s = scr("wqf_s", [D, 4096], BF16)
    wqp_s = scr("wqp_s", [4096, D], BF16)
    cc1i = nc.dram_tensor("cc1i", [1, 1024], F32); cc1o = nc.dram_tensor("cc1o", [1, 1024], F32)
    cc2i = nc.dram_tensor("cc2i", [128, 8], F32); cc2o = nc.dram_tensor("cc2o", [128, 8], F32)
    cc3i = nc.dram_tensor("cc3i", [64, 1024], F32); cc3o = nc.dram_tensor("cc3o", [64, 1024], F32)
    out_d = nc.dram_tensor("out", [D, TLOC], F32, kind="ExternalOutput")
    h_dbg = scr("h_dbg", [4096, TLOC], BF16) if "h_dbg" in taps else None
    a_dbg = scr("a_dbg", [128, 64], F32) if "a_dbg" in taps else None
    b_dbg = scr("b_dbg", [128, 64], F32) if "b_dbg" in taps else None
    y0_dbg = nc.dram_tensor("y0_dbg", [D, 512], BF16, kind="ExternalOutput") if "b_dbg" in taps else None
    z_dbg = scr("z_dbg", [4096, TLOC], F32) if "z_dbg" in taps else None
    o_dbg = scr("o_dbg", [D, TLOC], F32) if "o_dbg" in taps else None
    o2_dbg = scr("o2_dbg", [D, TLOC], F32) if "o_dbg" in taps else None
    y_dbg = scr("y_dbg", [D, TLOC], BF16) if "y_dbg" in taps else None

    A = nc.alloc_sbuf_tensor
    ones = A("ones", [128, 128], F32); onesb = A("onesb", [128, 128], BF16)
    ident = A("ident", [128, 128], F32); ltri = A("ltri", [128, 128], BF16)
    e16 = A("e16", [128, 128], F32); e16t = A("e16t", [16, 1024], F32)
    psel = A("psel", [16, 8], F32); hsel = A("hsel", [16, 128], F32)
    tmod = A("tmod", [128, 512], BF16); qrev = A("qrev", [128, 1], F32)
    msr = A("msr", [128, 2], F32)
    decay = A("decay", [128, 8], F32); ln0w = A("ln0w", [128, 8], F32)
    b1c = A("b1c", [128, 32], F32); b2c = A("b2c", [128, 8], F32)
    maa = A("maa", [128, 32], F32)
    tdc = A("tdc", [16, 1], F32); fac = A("fac", [16, 1], F32)
    ln1w = A("ln1w", [128, 8], F32); ln1b = A("ln1b", [128, 8], F32)
    lnfw = A("lnfw", [128, 32], F32); lnfb = A("lnfb", [128, 32], F32)
    lnfwB = A("lnfwB", [128, 32], BF16); lnfbB = A("lnfbB", [128, 32], BF16)
    dec = A("dec", [128, 8], F32)
    d512 = A("d512", [128, 8 * 256], BF16)
    lfm = A("lfm", [128, 8], F32)
    lfmH = A("lfmH", [128, 8], F32)
    lrow = A("lrow", [1, 1024], F32)
    lrowH = A("lrowH", [1, 1024], F32)
    invdt = A("invdt", [128, 1024], BF16)
    invcr = A("invcr", [1, 1024], F32)
    qidx = A("qidx", [128, 1], F32)
    ar256 = A("ar256", [128, 256], BF16)
    decrow = A("decrow", [1, 1024], F32)
    ccol = A("ccol", [128, 8], F32); ccol2 = A("ccol2", [128, 8], F32)
    ptmp = A("ptmp", [128, 8], F32)
    carry0 = A("carry0", [128, 8], F32)
    crun = A("crun", [128, 8], F32)
    ucol = A("ucol", [16, 1], F32); lnwcol = A("lnwcol", [16, 1], F32)
    lnwfm = A("lnwfm", [128, 8], F32)
    wkrep = A("wkrep", [128, 1024], BF16)
    e16b = A("e16b", [128, 128], BF16)
    lnwH = A("lnwH", [64, 16], F32)
    wsH = A("wsH", [64, 16], F32)
    wsrepH = A("wsrepH", [64, 1024], F32)
    state_H = A("state_H", [64, 1024], F32)
    stateb_H = A("stateb_H", [64, 1024], BF16)
    wmt = A("wmt", [128, 1024], BF16)
    rows16 = A("rows16", [16, 512], F32)
    st16 = A("st16", [16, 512], F32)
    st16b = A("st16b", [16, 512], F32)
    wA = A("wA", [128, 8 * 1024], BF16)
    wB = A("wB", [128, 8 * 1024], BF16)
    wStr = A("wStr", [128, 2 * 1024], BF16)
    stg = A("stg", [128, 4096], F32)
    wtmp = A("wtmp", [128, 2048], F32)
    xnA = A("xnA", [128, 8 * 512], BF16)
    xnB = A("xnB", [128, 8 * 512], BF16)
    trm = A("trm", [128, 1024], BF16)
    ysb = A("ysb", [128, 8 * 512], BF16)
    rsb = A("rsb", [128, 8 * 512], BF16)
    gsb = A("gsb", [128, 8 * 512], BF16)
    dlt = A("dlt", [128, 8 * 512], BF16)
    xmix = A("xmix", [128, 8 * 512], BF16)
    ktm = A("ktm", [128, 1024], BF16)
    vtm = A("vtm", [128, 1024], BF16)
    kwkb = A("kwkb", [128, 1024], BF16)
    attwt = A("attwt", [128, 1024], BF16)
    yL1 = A("yL1", [128, 2048], BF16)
    rowA = A("rowA", [1, 512], F32); rowB = A("rowB", [1, 512], F32)
    rowC = A("rowC", [1, 512], F32)
    absb = A("absb", [128, 512], F32)
    bbsb = A("bbsb", [128, 512], F32)
    absbB = A("absbB", [128, 512], BF16)
    sqb = A("sqb", [128, 512], F32)
    jit1 = A("jit1", [128, 512], F32)
    jit2 = A("jit2", [128, 512], BF16)
    tA = A("tA", [128, 512], F32)
    tBb = A("tBb", [128, 512], BF16)
    tCb = A("tCb", [128, 512], BF16)
    shb = A("shb", [128, 512], BF16)
    pcol = A("pcol", [128, 8], BF16)
    brow = A("brow", [128, 8], F32)
    cbuf = A("cbuf", [128, 512], F32)
    alpha1 = A("alpha1", [128, 32], F32)
    alpha2 = A("alpha2", [128, 8], F32)
    gcol = A("gcol", [128, 1], F32)
    gcol2 = A("gcol2", [128, 1], F32)
    cs1 = A("cs1", [128, 8], F32); cs2 = A("cs2", [128, 8], F32)
    rowg = A("rowg", [1, 512], F32)

    s64v = stg[0:16, 0:1024]
    edtv = stg[:, 1024:2048]
    tritv = stg[:, 2048:3072]
    eyetv = stg[:, 3072:4096]

    ps = [nc.alloc_psum_tensor(f"ps{i}", [128, 512], F32) for i in range(8)]

    ENG = ["T", "V", "S", "D", "G", "C"]
    STREAMS = ["H0", "H1", "X", "T2"] + [f"S{i}" for i in range(8)]
    cnt = {e: 0 for e in ENG + STREAMS}
    steps = []
    psum_r = {}
    psum_w = {}
    sem_handles = {}
    last_wait = {}

    cnt_hist = []
    def E(engn, fn, inc=0):
        if engn != "D":
            for o in ("T", "V", "S", "D", "G"):
                W(engn, o)
        if engn in ("V", "S") and cnt[engn] > 0:
            steps.append((engn, lambda eng, s=engn, v=cnt[engn]: eng.wait_ge(sem_handles[s], v)))
            cnt_hist.append(dict(cnt))
        elif strict and cnt[engn] > 0:
            steps.append((engn, lambda eng, s=engn, v=cnt[engn]: eng.wait_ge(sem_handles[s], v)))
            cnt_hist.append(dict(cnt))
        if inc:
            cnt[engn] += inc
        def run(eng, fn=fn, inc=inc, engn=engn):
            r = fn(eng)
            if inc:
                r.then_inc(sem_handles[engn], inc)
        steps.append((engn, run))
        cnt_hist.append(dict(cnt))

    def W(engn, src, val=None):
        if engn == src:
            return
        v = cnt[src] if val is None else val
        if v <= 0 or last_wait.get((engn, src), -1) >= v:
            return
        last_wait[(engn, src)] = v
        steps.append((engn, lambda eng, s=src, v=v: eng.wait_ge(sem_handles[s], v)))
        cnt_hist.append(dict(cnt))

    def mark_read(engn, banks):
        for b in banks:
            psum_r[b] = (engn, cnt[engn])

    def mark_write(banks):
        for b in banks:
            psum_w[b] = ("T", cnt["T"])

    def guard_write(banks):
        seen = {}
        for b in banks:
            if b in psum_r:
                s, v = psum_r[b]
                seen[s] = max(seen.get(s, 0), v)
        for s, v in seen.items():
            W("T", s, v)

    def guard_read(engn, banks):
        vv = 0
        for b in banks:
            if b in psum_w:
                vv = max(vv, psum_w[b][1])
        if vv:
            W(engn, "T", vv)

    def dma(qeng, dst, src):
        E(qeng, lambda e, d=dst, s=src: e.dma_start(out=d, in_=s), 16)

    def ld(dst, src):
        W("D", "T"); W("D", "V"); W("D", "S"); W("D", "G")
        dma("D", dst, src)

    slotrd = {}
    gprod = {}

    def gsnap(name):
        gprod[name] = cnt["G"]

    def dma_s(dst, src, semname):
        cnt[semname] += 16
        def run(eng, d=dst, s=src, sn=semname):
            eng.dma_start(out=d, in_=s).then_inc(sem_handles[sn], 16)
        steps.append(("D", run))
        cnt_hist.append(dict(cnt))

    def ldw(dst, src, key, sem, prod=None):
        if prod is not None:
            W("D", "G", gprod[prod])
        for e, v in slotrd.get(key, {}).items():
            W("D", e, v)
        dma_s(dst, src, sem)

    def mark_slot(key, eng="T"):
        slotrd.setdefault(key, {})[eng] = cnt[eng]

    def mmp(bank_ap, lh, rh, start, stop, banks, inc=True, tp=None):
        def fn(e, o=bank_ap, l=lh, r=rh, st=start, sp=stop, t=tp):
            if t is None:
                return e.matmul(o, l, r, start=st, stop=sp)
            return e.matmul(o, l, r, start=st, stop=sp, tile_position=t)
        E("T", fn, 1 if (inc or strict) else 0)
        mark_write(banks)

    def V(fn):
        E("V", fn, 1)

    def SW(engn):
        steps.append((engn, lambda eng, s=engn, v=cnt[engn]: eng.wait_ge(sem_handles[s], v)))
        cnt_hist.append(dict(cnt))

    def S(fn):
        E("S", fn, 1)

    def G(fn):
        E("G", fn, 1)

    def CC(fn):
        for o in ("T", "V", "S", "D"):
            W("G", o)
        steps.append(("G", lambda eng, v=cnt["G"]: eng.wait_ge(sem_handles["G"], v)))
        cnt_hist.append(dict(cnt))
        cnt["C"] += 1
        def run(eng, fn=fn):
            fn(eng).then_inc(sem_handles["C"], 1)
        steps.append(("G", run))
        cnt_hist.append(dict(cnt))
        steps.append(("G", lambda eng, v=cnt["C"]: eng.wait_ge(sem_handles["C"], v)))
        cnt_hist.append(dict(cnt))

    for dst, src in [(ones, c_ones), (onesb, c_onesb), (ident, c_ident), (ltri, c_ltri),
                     (e16, c_e16), (e16t, c_e16t), (psel, c_psel), (hsel, c_hsel),
                     (tmod, c_tmod), (qrev, c_qrev), (msr, m_sr),
                     (decrow, p_decrow), (qidx, c_qidx), (ar256, c_ar256),
                     (decay, p_decay), (ln0w, p_ln0w), (b1c, p_b1), (b2c, p_b2), (maa, p_maa),
                     (tdc, p_td), (fac, p_fa), (ln1w, p_ln1w), (ln1b, p_ln1b),
                     (lnfw, p_lnfw), (lnfb, p_lnfb),
                     (s64v, c_s64[:, :]), (edtv, c_edt[:, :]), (tritv, c_trit[:, :]),
                     (eyetv, c_eyet[:, :])]:
        dma("D", dst if isinstance(dst, bass.AP) else dst[:, :],
            src if isinstance(src, bass.AP) else src[:, :])
    W("S", "D"); W("V", "D"); W("T", "D"); W("G", "D")

    S(lambda e: e.activation(out=dec[:, :], in_=decay[:, :], func=AF.Sigmoid))
    S(lambda e: e.activation(out=lrow[:, :], in_=decrow[:, :], func=AF.Sigmoid))
    W("V", "S")
    V(lambda e: e.tensor_scalar(out=dec[:, :], in0=dec[:, :], scalar1=1e-7, scalar2=None, op0=ALU.max))
    V(lambda e: e.tensor_scalar(out=lrow[:, :], in0=lrow[:, :], scalar1=1e-7, scalar2=None, op0=ALU.max))
    W("S", "V")
    S(lambda e: e.activation(out=lfm[:, :], in_=dec[:, :], func=AF.Ln))
    S(lambda e: e.activation(out=lrow[:, :], in_=lrow[:, :], func=AF.Ln))
    W("V", "S")
    V(lambda e: e.tensor_scalar(out=lfmH[:, :], in0=lfm[:, :], scalar1=msr[:, 1:2],
                                scalar2=None, op0=ALU.mult))
    V(lambda e: e.tensor_scalar(out=lfmH[:, :], in0=lfmH[:, :], scalar1=float(TLOC),
                                scalar2=None, op0=ALU.mult))
    V(lambda e: e.tensor_scalar(out=lrowH[:, :], in0=lrow[:, :], scalar1=msr[0:1, 1:2],
                                scalar2=None, op0=ALU.mult))
    V(lambda e: e.tensor_scalar(out=lrowH[:, :], in0=lrowH[:, :], scalar1=-float(TLOC),
                                scalar2=None, op0=ALU.mult))
    W("S", "V")
    for c in range(8):
        S(lambda e, c=c: e.activation(out=d512[:, c * 256:(c + 1) * 256], in_=ar256[:, :],
                                      func=AF.Exp, scale=lfm[:, c:c + 1]))
    W("T", "S")
    guard_write([1, 2])
    mmp(ps[1][:, 0:512], ones[0:1, :], lrow[:, 0:512], True, True, [1], inc=False)
    mmp(ps[2][:, 0:512], ones[0:1, :], lrow[:, 512:1024], True, True, [2])
    guard_read("V", [1, 2])
    V(lambda e: e.tensor_copy(out=wtmp[:, 0:512], in_=ps[1][:, 0:512]))
    V(lambda e: e.tensor_copy(out=wtmp[:, 512:1024], in_=ps[2][:, 0:512]))
    mark_read("V", [1, 2])
    W("S", "V")
    S(lambda e: e.activation(out=wtmp[:, 0:1024], in_=wtmp[:, 0:1024], func=AF.Exp, scale=qidx[:, 0:1]))
    W("V", "S")
    V(lambda e: e.tensor_scalar(out=wtmp[:, 0:1024], in0=wtmp[:, 0:1024], scalar1=1e-11,
                                scalar2=None, op0=ALU.max))
    V(lambda e: e.reciprocal(out=wtmp[:, 0:1024], in_=wtmp[:, 0:1024]))
    SW("V")
    V(lambda e: e.tensor_scalar(out=invdt[:, :], in0=wtmp[:, 0:1024], scalar1=1.001e10,
                                scalar2=None, op0=ALU.min))

    lnwbc = wtmp[:, 0:1024]
    S(lambda e: e.activation(out=ucol[:, :], in_=tdc[:, :], func=AF.Exp))
    W("V", "S")
    V(lambda e: e.tensor_scalar(out=lnwcol[:, :], in0=ucol[:, :], scalar1=-1.0, scalar2=None, op0=ALU.mult))
    V(lambda e: e.tensor_scalar(out=rows16[:, 0:8], in0=psel[:, :], scalar1=lnwcol[:, 0:1],
                                scalar2=None, op0=ALU.mult))
    W("T", "V")
    guard_write([1])
    mmp(ps[1][:, 0:8], hsel[:, :], rows16[0:16, 0:8], True, True, [1])
    guard_read("V", [1])
    V(lambda e: e.tensor_copy(out=lnwfm[:, :], in_=ps[1][:, 0:8]))
    mark_read("V", [1])
    V(lambda e: e.tensor_scalar(out=rows16[:, 0:512], in0=s64v[:, 0:512], scalar1=lnwcol[:, 0:1],
                                scalar2=None, op0=ALU.mult))
    V(lambda e: e.tensor_scalar(out=st16[:, 0:512], in0=s64v[:, 512:1024], scalar1=lnwcol[:, 0:1],
                                scalar2=None, op0=ALU.mult))
    W("T", "V")
    guard_write([1, 2])
    mmp(ps[1][:, 0:512], ones[0:16, :], rows16[0:16, 0:512], True, True, [1], inc=False)
    mmp(ps[2][:, 0:512], ones[0:16, :], st16[0:16, 0:512], True, True, [2])
    guard_read("V", [1, 2])
    V(lambda e: e.tensor_copy(out=lnwbc[:, 0:512] if False else wtmp[:, 0:512], in_=ps[1][:, 0:512]))
    V(lambda e: e.tensor_copy(out=wtmp[:, 512:1024], in_=ps[2][:, 0:512]))
    mark_read("V", [1, 2])
    W("S", "V")
    S(lambda e: e.activation(out=wkrep[:, :], in_=wtmp[:, 0:1024], func=AF.Exp, scale=qrev[:, 0:1]))
    V(lambda e: e.tensor_scalar(out=rows16[:, 0:16], in0=ident[0:16, 0:16],
                                scalar1=lnwcol[:, 0:1], scalar2=None, op0=ALU.mult))
    W("T", "V")
    guard_write([3])
    mmp(ps[3][0:64, 0:16], ones[0:16, 0:64], rows16[0:16, 0:16], True, True, [3])
    guard_read("V", [3])
    V(lambda e: e.tensor_copy(out=lnwH[:, :], in_=ps[3][0:64, 0:16]))
    mark_read("V", [3])
    W("S", "V")
    S(lambda e: e.activation(out=wsH[:, :], in_=lnwH[:, :], func=AF.Exp, scale=64.0))
    W("V", "S")
    for h in range(16):
        V(lambda e, h=h: e.tensor_scalar(out=wsrepH[:, h * 64:(h + 1) * 64],
                                         in0=ones[0:64, 0:64],
                                         scalar1=wsH[:, h:h + 1], scalar2=None, op0=ALU.mult))
    V(lambda e: e.tensor_copy(out=e16b[:, :], in_=e16[:, :]))
    V(lambda e: e.tensor_tensor(out=cbuf[:, :], in0=edtv[:, 0:512], in1=wtmp[:, 0:512], op=ALU.mult))
    V(lambda e: e.tensor_tensor(out=tA[:, :], in0=edtv[:, 512:1024], in1=wtmp[:, 512:1024], op=ALU.mult))
    W("S", "V")
    S(lambda e: e.activation(out=wmt[:, 0:512], in_=cbuf[:, :], func=AF.Exp))
    S(lambda e: e.activation(out=wmt[:, 512:1024], in_=tA[:, :], func=AF.Exp))
    W("V", "S")
    V(lambda e: e.tensor_tensor(out=wmt[:, :], in0=wmt[:, :], in1=tritv, op=ALU.mult))
    V(lambda e: e.tensor_scalar(out=rows16[:, 0:512], in0=s64v[:, 0:512], scalar1=fac[:, 0:1],
                                scalar2=None, op0=ALU.mult))
    V(lambda e: e.tensor_scalar(out=st16[:, 0:512], in0=s64v[:, 512:1024], scalar1=fac[:, 0:1],
                                scalar2=None, op0=ALU.mult))
    W("T", "V")
    guard_write([1, 2])
    mmp(ps[1][:, 0:512], ones[0:16, :], rows16[0:16, 0:512], True, True, [1], inc=False)
    mmp(ps[2][:, 0:512], ones[0:16, :], st16[0:16, 0:512], True, True, [2])
    guard_read("V", [1, 2])
    V(lambda e: e.tensor_tensor(out=wtmp[:, 0:512], in0=ps[1][:, 0:512], in1=eyetv[:, 0:512], op=ALU.mult))
    V(lambda e: e.tensor_tensor(out=wtmp[:, 512:1024], in0=ps[2][:, 0:512],
                                in1=eyetv[:, 512:1024], op=ALU.mult))
    mark_read("V", [1, 2])
    V(lambda e: e.tensor_tensor(out=wmt[:, :], in0=wmt[:, :], in1=wtmp[:, 0:1024], op=ALU.add))
    V(lambda e: e.tensor_copy(out=lnfwB[:, :], in_=lnfw[:, :]))
    V(lambda e: e.tensor_copy(out=lnfbB[:, :], in_=lnfb[:, :]))

    def load_w(dst, src):
        for c in range(NCK):
            ld(stg[:, 0:1024], src[c * 128:(c + 1) * 128, :])
            W("V", "D")
            V(lambda e, c=c, dst=dst: e.tensor_copy(out=dst[:, c * 1024:(c + 1) * 1024],
                                                    in_=stg[:, 0:1024]))
        W("T", "V")

    def ln_stream(src_d, t0, outbuf, eps):
        for c in range(NCK):
            ld(jit1[:, :], src_d[c * 128:(c + 1) * 128, t0:t0 + TT])
            W("S", "D"); W("S", "T")
            S(lambda e: e.activation(out=sqb[:, :], in_=jit1[:, :], func=AF.Square))
            guard_write([0, 1])
            W("T", "S"); W("T", "D")
            mmp(ps[0][0:1, 0:512], ones[:, 0:1], jit1[:, :], c == 0, c == NCK - 1, [0], inc=False)
            mmp(ps[1][0:1, 0:512], ones[:, 0:1], sqb[:, :], c == 0, c == NCK - 1, [1])
        guard_read("V", [0, 1])
        V(lambda e: e.tensor_scalar(out=rowA[:, :], in0=ps[0][0:1, 0:512], scalar1=1.0 / D,
                                    scalar2=None, op0=ALU.mult))
        V(lambda e: e.tensor_scalar(out=rowB[:, :], in0=ps[1][0:1, 0:512], scalar1=1.0 / D,
                                    scalar2=None, op0=ALU.mult))
        mark_read("V", [0, 1])
        V(lambda e: e.tensor_tensor(out=rowC[:, :], in0=rowA[:, :], in1=rowA[:, :], op=ALU.mult))
        V(lambda e: e.tensor_tensor(out=rowB[:, :], in0=rowB[:, :], in1=rowC[:, :], op=ALU.subtract))
        V(lambda e, eps=eps: e.tensor_scalar(out=rowB[:, :], in0=rowB[:, :], scalar1=eps,
                                             scalar2=None, op0=ALU.add))
        W("S", "V")
        S(lambda e: e.activation(out=rowB[:, :], in_=rowB[:, :], func=AF.Sqrt))
        W("V", "S")
        V(lambda e: e.reciprocal(out=rowB[:, :], in_=rowB[:, :]))
        SW("V")
        V(lambda e: e.tensor_tensor(out=rowC[:, :], in0=rowA[:, :], in1=rowB[:, :], op=ALU.mult))
        V(lambda e: e.tensor_scalar(out=rowC[:, :], in0=rowC[:, :], scalar1=-1.0, scalar2=None, op0=ALU.mult))
        W("T", "V")
        guard_write([2, 3])
        mmp(ps[2][:, 0:512], ones[0:1, :], rowB[:, :], True, True, [2], inc=False)
        mmp(ps[3][:, 0:512], ones[0:1, :], rowC[:, :], True, True, [3])
        guard_read("V", [2, 3])
        V(lambda e: e.tensor_copy(out=absb[:, :], in_=ps[2][:, 0:512]))
        V(lambda e: e.tensor_copy(out=bbsb[:, :], in_=ps[3][:, 0:512]))
        mark_read("V", [2, 3])
        for c in range(NCK):
            ld(jit1[:, :], src_d[c * 128:(c + 1) * 128, t0:t0 + TT])
            W("V", "D")
            V(lambda e: e.tensor_tensor(out=tA[:, :], in0=jit1[:, :], in1=absb[:, :], op=ALU.mult))
            V(lambda e, c=c: e.tensor_tensor(out=outbuf[:, c * 512:(c + 1) * 512],
                                             in0=tA[:, :], in1=bbsb[:, :], op=ALU.add))

    def fm_proj(wres, rhsbuf, banks, evac):
        for c in range(NCK):
            bank = banks[c % 2]
            guard_write([bank])
            for k in range(NCK):
                mmp(ps[bank][:, 0:512], wres[:, k * 1024 + c * 128:k * 1024 + (c + 1) * 128],
                    rhsbuf[:, k * 512:(k + 1) * 512], k == 0, k == NCK - 1, [bank])
            evac(c, bank)

    def mk_delta(it):
        for c in range(NCK):
            V(lambda e, c=c: e.tensor_copy(out=shb[:, 1:512], in_=xnB[:, c * 512:c * 512 + 511]))
            V(lambda e, c=c: e.tensor_copy(out=shb[:, 0:1], in_=pcol[:, c:c + 1]))
            V(lambda e, c=c: e.tensor_tensor(out=dlt[:, c * 512:(c + 1) * 512], in0=shb[:, :],
                                             in1=xnB[:, c * 512:(c + 1) * 512], op=ALU.subtract))
            V(lambda e, c=c: e.tensor_copy(out=pcol[:, c:c + 1], in_=xnB[:, c * 512 + 511:c * 512 + 512]))

    def mk_mix(mi):
        for c in range(NCK):
            V(lambda e, c=c, mi=mi: e.tensor_scalar(out=tCb[:, :], in0=dlt[:, c * 512:(c + 1) * 512],
                                                    scalar1=maa[:, mi * 8 + c:mi * 8 + c + 1],
                                                    scalar2=None, op0=ALU.mult))
            V(lambda e, c=c: e.tensor_tensor(out=xmix[:, c * 512:(c + 1) * 512], in0=tCb[:, :],
                                             in1=xnB[:, c * 512:(c + 1) * 512], op=ALU.add))

    def tm_proj(tc, wres, rhsbuf, dstb, banks=(6, 7)):
        guard_write(list(banks))
        for half in range(2):
            for c in range(NCK):
                mmp(ps[banks[half]][:, 0:512], rhsbuf[:, c * 512 + tc * 128:c * 512 + (tc + 1) * 128],
                    wres[:, c * 1024 + half * 512:c * 1024 + (half + 1) * 512], c == 0, c == NCK - 1,
                    [banks[half]], inc=(c == NCK - 1))
        guard_read("V", list(banks))
        for half in range(2):
            V(lambda e, half=half, dstb=dstb: e.tensor_copy(
                out=dstb[:, half * 512:(half + 1) * 512], in_=ps[banks[half]][:, 0:512]))
        mark_read("V", list(banks))

    kchk = ktm[0:64, 0:1024]
    vchk = vtm[0:64, 0:1024]
    kwkH = kwkb[0:64, 0:1024]

    def scan_chunk():
        W("T", "V")
        guard_write([1, 5])
        for h in range(HD):
            bank = 1 if h < 8 else 5
            mmp(ps[bank][0:64, (h % 8) * 64:(h % 8) * 64 + 64],
                kwkH[:, h * 64:(h + 1) * 64],
                vchk[:, h * 64:(h + 1) * 64],
                True, True, [bank], inc=(h % 8 == 7))
        guard_read("V", [1, 5])
        V(lambda e: e.tensor_tensor(out=state_H[:, :], in0=state_H[:, :], in1=wsrepH[:, :], op=ALU.mult))
        V(lambda e: e.tensor_tensor(out=state_H[:, 0:512], in0=state_H[:, 0:512],
                                    in1=ps[1][0:64, 0:512], op=ALU.add))
        V(lambda e: e.tensor_tensor(out=state_H[:, 512:1024], in0=state_H[:, 512:1024],
                                    in1=ps[5][0:64, 0:512], op=ALU.add))
        mark_read("V", [1, 5])

    load_w(wA, wkT)
    load_w(wB, wvT)
    TOTB = 7
    V(lambda e: e.memset(rowg[:, :], 0.0))
    V(lambda e: e.memset(cbuf[0:1, 0:512], 0.0))
    for it in range(NT):
        t0 = it * TT
        ln_stream(x_d, t0, xnA, EPS_LN)
        W("G", "V")
        for c in range(NCK):
            dma("G", xn_s[c * 128:(c + 1) * 128, t0:t0 + TT], xnA[:, c * 512:(c + 1) * 512])
        W("T", "V")
        for tc in range(4):
            kb = [4, 5]; vb = [6, 2]
            guard_write(kb + vb)
            for half in range(2):
                for c in range(NCK):
                    mmp(ps[kb[half]][:, 0:512], xnA[:, c * 512 + tc * 128:c * 512 + (tc + 1) * 128],
                        wA[:, c * 1024 + half * 512:c * 1024 + (half + 1) * 512], c == 0, c == NCK - 1,
                        [kb[half]], inc=(c == NCK - 1))
            for half in range(2):
                for c in range(NCK):
                    mmp(ps[vb[half]][:, 0:512], xnA[:, c * 512 + tc * 128:c * 512 + (tc + 1) * 128],
                        wB[:, c * 1024 + half * 512:c * 1024 + (half + 1) * 512], c == 0, c == NCK - 1,
                        [vb[half]], inc=(c == NCK - 1))
            off = float(t0 + tc * 128)
            V(lambda e, off=off: e.tensor_scalar(out=invcr[:, :], in0=lrow[:, :], scalar1=-off,
                                                 scalar2=None, op0=ALU.mult))
            V(lambda e: e.tensor_tensor(out=invcr[:, :], in0=invcr[:, :], in1=lrowH[:, :], op=ALU.add))
            V(lambda e: e.tensor_scalar(out=invcr[:, :], in0=invcr[:, :], scalar1=23.03,
                                        scalar2=None, op0=ALU.min))
            W("S", "V")
            S(lambda e: e.activation(out=invcr[:, :], in_=invcr[:, :], func=AF.Exp))
            guard_write([3, 0])
            W("T", "S")
            mmp(ps[3][:, 0:512], ones[0:1, :], invcr[:, 0:512], True, True, [3], inc=False)
            mmp(ps[0][:, 0:512], ones[0:1, :], invcr[:, 512:1024], True, True, [0])
            guard_read("V", [3, 0])
            V(lambda e: e.tensor_tensor(out=wtmp[:, 0:512], in0=invdt[:, 0:512],
                                        in1=ps[3][:, 0:512], op=ALU.mult))
            V(lambda e: e.tensor_tensor(out=wtmp[:, 512:1024], in0=invdt[:, 512:1024],
                                        in1=ps[0][:, 0:512], op=ALU.mult))
            mark_read("V", [3, 0])
            V(lambda e: e.tensor_scalar(out=wtmp[:, 0:1024], in0=wtmp[:, 0:1024], scalar1=1e10,
                                        scalar2=None, op0=ALU.min))
            guard_read("V", kb + vb)
            for half in range(2):
                V(lambda e, half=half, kb=kb: e.tensor_tensor(
                    out=wtmp[:, 1024 + half * 512:1024 + (half + 1) * 512], in0=ps[kb[half]][:, 0:512],
                    in1=wtmp[:, half * 512:(half + 1) * 512], op=ALU.mult))
                V(lambda e, half=half, vb=vb: e.tensor_tensor(
                    out=trm[:, half * 512:(half + 1) * 512],
                    in0=wtmp[:, 1024 + half * 512:1024 + (half + 1) * 512],
                    in1=ps[vb[half]][:, 0:512], op=ALU.mult))
            mark_read("V", kb + vb)
            W("G", "V")
            dma("G", terms_s[t0 + tc * 128:t0 + (tc + 1) * 128, :], trm[:, :])
            W("T", "V")
            if tc == 0:
                guard_write([TOTB, 1])
            mmp(ps[TOTB][0:1, 0:512], onesb[:, 0:1], trm[:, 0:512],
                tc == 0, tc == 3, [TOTB], inc=False)
            mmp(ps[1][0:1, 0:512], onesb[:, 0:1], trm[:, 512:1024],
                tc == 0, tc == 3, [1], inc=True)
        guard_read("V", [TOTB, 1])
        V(lambda e: e.tensor_tensor(out=rowg[:, 0:512], in0=rowg[:, 0:512],
                                    in1=ps[TOTB][0:1, 0:512], op=ALU.add))
        V(lambda e: e.tensor_tensor(out=cbuf[0:1, 0:512], in0=cbuf[0:1, 0:512],
                                    in1=ps[1][0:1, 0:512], op=ALU.add))
        mark_read("V", [TOTB, 1])

    gsnap("ph1")
    V(lambda e: e.tensor_scalar(out=rowg[:, 0:512], in0=rowg[:, 0:512], scalar1=msr[0:1, 0:1],
                                scalar2=None, op0=ALU.mult))
    V(lambda e: e.tensor_scalar(out=tA[0:1, 0:512], in0=cbuf[0:1, 0:512], scalar1=msr[0:1, 0:1],
                                scalar2=None, op0=ALU.mult))
    W("G", "V")
    dma("G", cc1i[0:1, 0:512], rowg[:, 0:512])
    dma("G", cc1i[0:1, 512:1024], tA[0:1, 0:512])
    CC(lambda e: e.collective_compute("AllReduce", ALU.add, replica_groups=rg,
                                      ins=[cc1i.ap().opt()], outs=[cc1o.ap().opt()]))
    E("G", lambda e: e.dma_start(out=ptmp[:, :], in_=ap3(cc1o, 0, [[1, 128], [128, 8]]),
                                 allow_slow_non_contiguous=True), 16)
    W("V", "G")
    V(lambda e: e.tensor_scalar(out=carry0[:, :], in0=ptmp[:, :], scalar1=msr[:, 1:2],
                                scalar2=None, op0=ALU.mult))

    for hc in range(32):
        ld(stg[:, 0:1024], w1N[hc * 128:(hc + 1) * 128, :])
        W("S", "D")
        S(lambda e: e.activation(out=stg[:, 0:1024], in_=stg[:, 0:1024], func=AF.Abs))
        W("V", "S")
        V(lambda e, hc=hc: e.reduce_sum(out=alpha1[:, hc:hc + 1], in_=stg[:, 0:1024], axis=AX.X))
    V(lambda e: e.tensor_scalar(out=alpha1[:, :], in0=alpha1[:, :], scalar1=1.0 / D,
                                scalar2=None, op0=ALU.mult))
    for c in range(NCK):
        ld(stg[:, 0:4096], w2N[c * 128:(c + 1) * 128, :])
        W("S", "D")
        S(lambda e: e.activation(out=stg[:, 0:4096], in_=stg[:, 0:4096], func=AF.Abs))
        W("V", "S")
        V(lambda e, c=c: e.reduce_sum(out=alpha2[:, c:c + 1], in_=stg[:, 0:4096], axis=AX.X))
    V(lambda e: e.tensor_scalar(out=alpha2[:, :], in0=alpha2[:, :], scalar1=1.0 / 4096,
                                scalar2=None, op0=ALU.mult))
    if a_dbg is not None:
        W("G", "V")
        dma("G", a_dbg[:, 0:32], alpha1[:, :])
        dma("G", a_dbg[:, 32:40], alpha2[:, :])
        dma("G", a_dbg[:, 40:72][:, 0:24] if False else a_dbg[:, 40:64], b1c[:, 0:24])
    for c in range(NCK):
        ld(stg[:, 0:4096], w1T[c * 128:(c + 1) * 128, :])
        W("S", "D")
        for q in range(4):
            S(lambda e, q=q: e.activation(out=ysb[:, q * 1024:(q + 1) * 1024],
                                          in_=stg[:, q * 1024:(q + 1) * 1024], func=AF.Sign))
        W("G", "S")
        dma("G", sw1_s[c * 128:(c + 1) * 128, :], ysb[:, 0:4096])
    for hc in range(32):
        ld(stg[:, 0:1024], w2T[hc * 128:(hc + 1) * 128, :])
        W("S", "D")
        S(lambda e: e.activation(out=ysb[:, 0:1024], in_=stg[:, 0:1024], func=AF.Sign))
        W("G", "S")
        dma("G", sw2_s[hc * 128:(hc + 1) * 128, :], ysb[:, 0:1024])
    gsnap("sw")
    for it in range(NT):
        t0 = it * TT
        for c in range(NCK):
            ldw(xnA[:, c * 512:(c + 1) * 512], xn_s[c * 128:(c + 1) * 128, t0:t0 + TT],
                ("xna", c), "X", prod="ph1")
        W("T", "X")
        for hc in range(32):
            half = hc % 2
            for c in range(NCK):
                ldw(wStr[:, half * 1024 + c * 128:half * 1024 + (c + 1) * 128],
                    sw1_s[c * 128:(c + 1) * 128, hc * 128:(hc + 1) * 128],
                    ("wstr", half * 8 + c), "H%d" % half, prod="sw")
            W("T", "H%d" % half)
            bank = 4 + half
            guard_write([bank])
            for c in range(NCK):
                mmp(ps[bank][:, 0:512], wStr[:, half * 1024 + c * 128:half * 1024 + (c + 1) * 128],
                    xnA[:, c * 512:(c + 1) * 512], c == 0, c == NCK - 1, [bank])
                mark_slot(("wstr", half * 8 + c))
            if hc == 31:
                for c in range(NCK):
                    mark_slot(("xna", c))
            guard_read("S", [bank])
            hbuf = wA if hc < 16 else wB
            hoff = (hc % 16) * 512
            S(lambda e, hc=hc, bank=bank, hbuf=hbuf, hoff=hoff: e.activation(
                out=hbuf[:, hoff:hoff + 512], in_=ps[bank][:, 0:512], func=AF.Gelu,
                bias=b1c[:, hc:hc + 1], scale=alpha1[:, hc:hc + 1]))
            mark_read("S", [bank])
            if z_dbg is not None:
                guard_read("V", [bank])
                V(lambda e, bank=bank: e.tensor_copy(out=tA[:, :], in_=ps[bank][:, 0:512]))
                mark_read("V", [bank])
                W("G", "V")
                dma("G", z_dbg[hc * 128:(hc + 1) * 128, t0:t0 + TT], tA[:, :])
        if h_dbg is not None:
            W("G", "S")
            for hc in range(32):
                hbuf = wA if hc < 16 else wB
                hoff = (hc % 16) * 512
                dma("G", h_dbg[hc * 128:(hc + 1) * 128, t0:t0 + TT], hbuf[:, hoff:hoff + 512])
        W("T", "S")
        for c in range(NCK):
            bank = 6 + (c % 2)
            guard_write([bank])
            for hc in range(32):
                slot = hc % 8
                ldw(wStr[:, slot * 128:(slot + 1) * 128],
                    sw2_s[hc * 128:(hc + 1) * 128, c * 128:(c + 1) * 128],
                    ("wstr", slot), "S%d" % slot, prod="sw")
                W("T", "S%d" % slot)
                hbuf = wA if hc < 16 else wB
                hoff = (hc % 16) * 512
                mmp(ps[bank][:, 0:512], wStr[:, slot * 128:(slot + 1) * 128],
                    hbuf[:, hoff:hoff + 512], hc == 0, hc == 31, [bank])
                mark_slot(("wstr", slot))
            if o_dbg is not None:
                guard_read("V", [bank])
                V(lambda e, bank=bank: e.tensor_copy(out=tA[:, :], in_=ps[bank][:, 0:512]))
                W("G", "V")
                dma("G", o_dbg[c * 128:(c + 1) * 128, t0:t0 + TT], tA[:, :])
            guard_read("S", [bank])
            S(lambda e, c=c, bank=bank: e.activation(
                out=gsb[:, c * 512:(c + 1) * 512], in_=ps[bank][:, 0:512], func=AF.Silu,
                bias=b2c[:, c:c + 1], scale=alpha2[:, c:c + 1]))
            mark_read("S", [bank])
            if o_dbg is not None:
                W("V", "S"); W("V", "G")
                V(lambda e, bank=bank: e.tensor_copy(out=tA[:, :], in_=ps[bank][:, 0:512]))
                mark_read("V", [bank])
                W("G", "V")
                dma("G", o2_dbg[c * 128:(c + 1) * 128, t0:t0 + TT], tA[:, :])
        W("G", "S")
        for c in range(NCK):
            dma("G", g_s[c * 128:(c + 1) * 128, t0:t0 + TT], gsb[:, c * 512:(c + 1) * 512])

    load_w(wA, wrT)
    load_w(wB, woT)
    W("V", "G")
    V(lambda e: e.tensor_copy(out=crun[:, :], in_=carry0[:, :]))
    for it in range(NT):
        t0 = it * TT
        V(lambda e, t0=t0: e.tensor_scalar(out=ccol[:, :], in0=lfm[:, :], scalar1=float(t0),
                                           scalar2=None, op0=ALU.mult))
        V(lambda e: e.tensor_tensor(out=ccol[:, :], in0=ccol[:, :], in1=lfmH[:, :], op=ALU.add))
        V(lambda e, t0=t0: e.tensor_scalar(out=ccol2[:, :], in0=lfm[:, :], scalar1=float(t0 + 256),
                                           scalar2=None, op0=ALU.mult))
        V(lambda e: e.tensor_tensor(out=ccol2[:, :], in0=ccol2[:, :], in1=lfmH[:, :], op=ALU.add))
        W("S", "V")
        S(lambda e: e.activation(out=ccol[:, :], in_=ccol[:, :], func=AF.Exp))
        S(lambda e: e.activation(out=ccol2[:, :], in_=ccol2[:, :], func=AF.Exp))
        W("V", "S")
        for c in range(NCK):
            ld(xnA[:, c * 512:(c + 1) * 512], xn_s[c * 128:(c + 1) * 128, t0:t0 + TT])
        W("T", "D")
        def evac_r(c, bank):
            guard_read("S", [bank])
            S(lambda e, c=c, bank=bank: e.activation(out=rsb[:, c * 512:(c + 1) * 512],
                                                     in_=ps[bank][:, 0:512], func=AF.Sigmoid))
            mark_read("S", [bank])
        fm_proj(wA, xnA, [4, 5], evac_r)
        W("V", "S")
        for tc in range(4):
            ldw(trm[:, :], terms_s[t0 + tc * 128:t0 + (tc + 1) * 128, :], ("trm",),
                "T2", prod="ph1")
            W("T", "T2")
            guard_write([6, 7])
            for c in range(NCK):
                bank = 6 + c // 4
                mmp(ps[bank][:, (c % 4) * 128:(c % 4 + 1) * 128], trm[:, c * 128:(c + 1) * 128],
                    ltri[:, :], True, True, [bank], inc=(c == NCK - 1))
            mark_slot(("trm",))
            guard_read("V", [6, 7])
            scol = ccol if tc < 2 else ccol2
            dso = (tc % 2) * 128
            for c in range(NCK):
                bank = 6 + c // 4
                off = (c % 4) * 128
                V(lambda e, bank=bank, off=off, c=c: e.tensor_scalar(
                    out=tA[:, 0:128], in0=ps[bank][:, off:off + 128],
                    scalar1=crun[:, c:c + 1], scalar2=None, op0=ALU.add))
                V(lambda e, c=c, scol=scol: e.tensor_scalar(
                    out=tA[:, 128:256], in0=tA[:, 0:128],
                    scalar1=scol[:, c:c + 1], scalar2=None, op0=ALU.mult))
                V(lambda e, c=c, dso=dso: e.tensor_tensor(
                    out=tA[:, 256:384], in0=tA[:, 128:256],
                    in1=d512[:, c * 256 + dso:c * 256 + dso + 128], op=ALU.mult))
                V(lambda e, c=c, tc=tc: e.tensor_tensor(
                    out=ysb[:, c * 512 + tc * 128:c * 512 + (tc + 1) * 128],
                    in0=tA[:, 256:384], in1=rsb[:, c * 512 + tc * 128:c * 512 + (tc + 1) * 128],
                    op=ALU.mult))
                V(lambda e, c=c, bank=bank, off=off: e.tensor_tensor(
                    out=crun[:, c:c + 1], in0=crun[:, c:c + 1],
                    in1=ps[bank][:, off + 127:off + 128], op=ALU.add))
            mark_read("V", [6, 7])
        if b_dbg is not None and it == 0:
            V(lambda e: e.tensor_copy(out=cbuf[:, 0:8], in_=ccol[:, :]))
            V(lambda e: e.tensor_copy(out=cbuf[:, 8:16], in_=ccol2[:, :]))
            V(lambda e: e.tensor_copy(out=cbuf[:, 16:24], in_=crun[:, :]))
            V(lambda e: e.tensor_copy(out=cbuf[:, 24:32], in_=carry0[:, :]))
            V(lambda e: e.tensor_copy(out=cbuf[:, 32:40], in_=lfm[:, :]))
            V(lambda e: e.tensor_copy(out=cbuf[:, 40:48], in_=lfmH[:, :]))
            V(lambda e: e.tensor_copy(out=cbuf[:, 48:56], in_=d512[:, 0:8]))
            W("G", "V")
            dma("G", b_dbg[:, :], cbuf[:, 0:64])
            for c in range(NCK):
                dma("G", y0_dbg[c * 128:(c + 1) * 128, :], ysb[:, c * 512:(c + 1) * 512])
        for c in range(NCK):
            W("S", "V"); W("S", "T")
            S(lambda e, c=c: e.activation(out=sqb[:, :], in_=ysb[:, c * 512:(c + 1) * 512], func=AF.Square))
            guard_write([0])
            W("T", "S")
            mmp(ps[0][0:1, 0:512], ones[:, 0:1], sqb[:, :], c == 0, c == NCK - 1, [0],
                inc=(c == NCK - 1))
        guard_read("V", [0])
        V(lambda e: e.tensor_scalar(out=rowB[:, :], in0=ps[0][0:1, 0:512], scalar1=1.0 / D,
                                    scalar2=EPS_RMS, op0=ALU.mult, op1=ALU.add))
        mark_read("V", [0])
        W("S", "V")
        S(lambda e: e.activation(out=rowB[:, :], in_=rowB[:, :], func=AF.Sqrt))
        W("V", "S")
        V(lambda e: e.reciprocal(out=rowB[:, :], in_=rowB[:, :]))
        SW("V")
        W("T", "V")
        guard_write([2])
        mmp(ps[2][:, 0:512], ones[0:1, :], rowB[:, :], True, True, [2])
        guard_read("V", [2])
        V(lambda e: e.tensor_copy(out=absbB[:, :], in_=ps[2][:, 0:512]))
        mark_read("V", [2])
        for c in range(NCK):
            V(lambda e, c=c: e.tensor_scalar(out=tBb[:, :], in0=ysb[:, c * 512:(c + 1) * 512],
                                             scalar1=ln0w[:, c:c + 1], scalar2=None, op0=ALU.mult))
            V(lambda e, c=c: e.tensor_tensor(out=ysb[:, c * 512:(c + 1) * 512],
                                             in0=tBb[:, :], in1=absbB[:, :], op=ALU.mult))
        W("T", "V")
        for c in range(NCK):
            bank = 4 + (c % 2)
            guard_write([bank])
            for k in range(NCK):
                mmp(ps[bank][:, 0:512], wB[:, k * 1024 + c * 128:k * 1024 + (c + 1) * 128],
                    ysb[:, k * 512:(k + 1) * 512], k == 0, k == NCK - 1, [bank])
            ld(jit1[:, :], x_d[c * 128:(c + 1) * 128, t0:t0 + TT])
            ld(jit2[:, :], g_s[c * 128:(c + 1) * 128, t0:t0 + TT])
            guard_read("V", [bank])
            W("V", "D")
            V(lambda e, bank=bank: e.tensor_tensor(out=tA[:, :], in0=ps[bank][:, 0:512],
                                                   in1=jit2[:, :], op=ALU.mult))
            mark_read("V", [bank])
            V(lambda e: e.tensor_tensor(out=sqb[:, :], in0=tA[:, :], in1=jit1[:, :], op=ALU.add))
            W("G", "V")
            dma("G", x1_s[c * 128:(c + 1) * 128, t0:t0 + TT], sqb[:, :])
        ln_stream(x1_s, t0, xnB, EPS_LN)
        W("G", "V")
        for c in range(NCK):
            dma("G", xn1_s[c * 128:(c + 1) * 128, t0:t0 + TT], xnB[:, c * 512:(c + 1) * 512])
        if it == NT - 1:
            for c in range(NCK):
                V(lambda e, c=c: e.tensor_copy(out=brow[:, c:c + 1],
                                               in_=xnB[:, c * 512 + 511:c * 512 + 512]))

    gsnap("ph3")
    V(lambda e: e.tensor_scalar(out=cbuf[:, 0:8], in0=brow[:, :], scalar1=msr[:, 0:1],
                                scalar2=None, op0=ALU.mult))
    W("G", "V")
    dma("G", cc2i[:, :], cbuf[:, 0:8])
    CC(lambda e: e.collective_compute("AllReduce", ALU.add, replica_groups=rg,
                                      ins=[cc2i.ap().opt()], outs=[cc2o.ap().opt()]))
    dma("G", cbuf[:, 8:16], cc2o[:, :])
    W("V", "G")
    V(lambda e: e.tensor_scalar(out=cbuf[:, 16:24], in0=cbuf[:, 8:16], scalar1=msr[:, 1:2],
                                scalar2=None, op0=ALU.mult))
    V(lambda e: e.tensor_copy(out=pcol[:, :], in_=cbuf[:, 16:24]))

    load_w(wA, wr1T)
    load_w(wB, wk1T)
    for it in range(NT):
        t0 = it * TT
        for c in range(NCK):
            ld(xnB[:, c * 512:(c + 1) * 512], xn1_s[c * 128:(c + 1) * 128, t0:t0 + TT])
        W("V", "D")
        mk_delta(it)
        mk_mix(2)
        W("T", "V")
        def evac_cp(dstb):
            def f(c, bank):
                guard_read("V", [bank])
                V(lambda e, c=c, bank=bank, dstb=dstb: e.tensor_copy(
                    out=dstb[:, c * 512:(c + 1) * 512], in_=ps[bank][:, 0:512]))
                mark_read("V", [bank])
            return f
        fm_proj(wA, xmix, [4, 5], evac_cp(ysb))
        W("G", "V")
        for c in range(NCK):
            dma("G", r1_s[c * 128:(c + 1) * 128, t0:t0 + TT], ysb[:, c * 512:(c + 1) * 512])
        mk_mix(0)
        W("T", "V")
        fm_proj(wB, xmix, [4, 5], evac_cp(gsb))
        W("G", "V")
        for c in range(NCK):
            dma("G", kf_s[c * 128:(c + 1) * 128, t0:t0 + TT], gsb[:, c * 512:(c + 1) * 512])
        for tc in range(4):
            tm_proj(tc, wB, xmix, ktm)
            W("G", "V")
            dma("G", kt_s[t0 + tc * 128:t0 + (tc + 1) * 128, :], ktm[:, :])

    load_w(wA, wv1T)
    load_w(wB, wg1T)
    V(lambda e: e.memset(state_H[:, :], 0.0))
    V(lambda e: e.tensor_copy(out=pcol[:, :], in_=cbuf[:, 16:24]))
    for it in range(NT):
        t0 = it * TT
        for c in range(NCK):
            ld(xnB[:, c * 512:(c + 1) * 512], xn1_s[c * 128:(c + 1) * 128, t0:t0 + TT])
        W("V", "D")
        mk_delta(it)
        mk_mix(3)
        W("T", "V")
        def evac_gelu(c, bank):
            guard_read("S", [bank])
            S(lambda e, c=c, bank=bank: e.activation(out=gsb[:, c * 512:(c + 1) * 512],
                                                     in_=ps[bank][:, 0:512], func=AF.Gelu))
            mark_read("S", [bank])
        fm_proj(wB, xmix, [4, 5], evac_gelu)
        W("G", "S")
        for c in range(NCK):
            dma("G", g1_s[c * 128:(c + 1) * 128, t0:t0 + TT], gsb[:, c * 512:(c + 1) * 512])
        mk_mix(1)
        W("T", "V")
        for tc in range(4):
            tm_proj(tc, wA, xmix, vtm)
            W("G", "V")
            dma("G", vt_s[t0 + tc * 128:t0 + (tc + 1) * 128, :], vtm[:, :])
            for par in range(2):
                tj = t0 + tc * 128 + par * 64
                ld(kchk, kt_s[tj:tj + 64, :])
                ld(vchk, vt_s[tj:tj + 64, :])
                W("V", "D")
                V(lambda e: e.tensor_tensor(out=kwkH, in0=kchk, in1=wkrep[0:64, :], op=ALU.mult))
                scan_chunk()

    V(lambda e: e.tensor_scalar(out=stg[0:64, 0:1024], in0=state_H[:, :], scalar1=msr[0:64, 0:1],
                                scalar2=None, op0=ALU.mult))
    W("G", "V")
    dma("G", cc3i[0:64, 0:1024], stg[0:64, 0:1024])
    CC(lambda e: e.collective_compute("AllReduce", ALU.add, replica_groups=rg,
                                      ins=[cc3i.ap().opt()], outs=[cc3o.ap().opt()]))
    dma("G", stg[0:64, 1024:2048], cc3o[0:64, 0:1024])
    W("V", "G")
    V(lambda e: e.tensor_scalar(out=state_H[:, :], in0=stg[0:64, 1024:2048], scalar1=msr[0:64, 1:2],
                                scalar2=None, op0=ALU.mult))

    def gamma_pass(wsrc, nchunks, width, gdst):
        V(lambda e: e.memset(rowg[:, :], 0.0))
        for c in range(nchunks):
            ld(stg[:, 0:width], wsrc[c * 128:(c + 1) * 128, :])
            W("S", "D"); W("S", "T"); W("S", "V")
            S(lambda e, width=width: e.activation(out=stg[:, 0:width], in_=stg[:, 0:width], func=AF.Abs))
            W("T", "S")
            for eb in range(width // 512):
                bank = eb % 2
                guard_write([bank])
                mmp(ps[bank][0:1, 0:512], ones[:, 0:1], stg[:, eb * 512:(eb + 1) * 512],
                    True, True, [bank], inc=True)
                guard_read("V", [bank])
                V(lambda e, bank=bank: e.tensor_tensor(
                    out=rowg[:, 0:512], in0=rowg[:, 0:512],
                    in1=ps[bank][0:1, 0:512], op=ALU.add))
                mark_read("V", [bank])
        V(lambda e: e.reduce_sum(out=rowA[:, 0:1], in_=rowg[:, 0:512], axis=AX.X))
        V(lambda e: e.tensor_scalar(out=rowA[:, 0:1], in0=rowA[:, 0:1], scalar1=1.0 / (D * 4096),
                                    scalar2=1e-6, op0=ALU.mult, op1=ALU.add))
        V(lambda e: e.tensor_scalar(out=rowA[:, 1:2], in0=rowA[:, 0:1], scalar1=0.5,
                                    scalar2=None, op0=ALU.mult))
        W("T", "V")
        guard_write([2])
        mmp(ps[2][:, 0:1], ones[0:1, :], rowA[:, 1:2], True, True, [2])
        guard_read("V", [2])
        V(lambda e, gdst=gdst: e.tensor_copy(out=gdst[:, :], in_=ps[2][:, 0:1]))
        mark_read("V", [2])

    def quant_pass(wsrc, nchunks, width, gc, dst_s):
        for c in range(nchunks):
            ld(stg[:, 0:width], wsrc[c * 128:(c + 1) * 128, :])
            W("S", "D"); W("S", "V"); W("S", "G")
            for q in range(max(1, width // 1024)):
                S(lambda e, q=q: e.activation(out=ysb[:, q * 1024:(q + 1) * 1024],
                                              in_=stg[:, q * 1024:(q + 1) * 1024], func=AF.Sign))
            S(lambda e, width=width: e.activation(out=stg[:, 0:width], in_=stg[:, 0:width], func=AF.Abs))
            W("V", "S")
            V(lambda e, width=width, gc=gc: e.tensor_scalar(out=stg[:, 0:width], in0=stg[:, 0:width],
                                                            scalar1=gc[:, 0:1], scalar2=None,
                                                            op0=ALU.is_ge))
            V(lambda e, width=width: e.tensor_copy(out=gsb[:, 0:width], in_=stg[:, 0:width]))
            V(lambda e, width=width: e.tensor_tensor(out=ysb[:, 0:width], in0=ysb[:, 0:width],
                                                     in1=gsb[:, 0:width], op=ALU.mult))
            W("G", "V")
            dma("G", dst_s[c * 128:(c + 1) * 128, :], ysb[:, 0:width])

    gamma_pass(wfcT, NCK, 4096, gcol)
    gamma_pass(wpjT, 32, 1024, gcol2)
    SW("V")
    quant_pass(wfcT, NCK, 4096, gcol, wqf_s)
    quant_pass(wpjT, 32, 1024, gcol2, wqp_s)
    gsnap("quant")
    for (target, lncol) in [(cs1, lnfwB), (cs2, lnfbB)]:
        for c in range(NCK):
            bank = c % 2
            guard_write([bank])
            for hc in range(32):
                slot = hc % 8
                ldw(wStr[:, slot * 128:(slot + 1) * 128],
                    wqp_s[hc * 128:(hc + 1) * 128, c * 128:(c + 1) * 128],
                    ("wstr", slot), "S%d" % slot, prod="quant")
                W("T", "S%d" % slot)
                mmp(ps[bank][:, 0:1], wStr[:, slot * 128:(slot + 1) * 128],
                    lncol[:, hc:hc + 1], hc == 0, hc == 31, [bank], inc=True)
                mark_slot(("wstr", slot))
            guard_read("V", [bank])
            V(lambda e, c=c, bank=bank, target=target: e.tensor_copy(out=target[:, c:c + 1],
                                                                     in_=ps[bank][:, 0:1]))
            mark_read("V", [bank])
    for it in range(NT):
        t0 = it * TT
        for c in range(NCK):
            ldw(xnB[:, c * 512:(c + 1) * 512], xn1_s[c * 128:(c + 1) * 128, t0:t0 + TT],
                ("xnb", c), "X", prod="ph3")
        W("T", "X")
        for hc in range(32):
            half = hc % 2
            for c in range(NCK):
                ldw(wStr[:, half * 1024 + c * 128:half * 1024 + (c + 1) * 128],
                    wqf_s[c * 128:(c + 1) * 128, hc * 128:(hc + 1) * 128],
                    ("wstr", half * 8 + c), "H%d" % half, prod="quant")
            W("T", "H%d" % half)
            bank = 4 + half
            guard_write([bank])
            for c in range(NCK):
                mmp(ps[bank][:, 0:512], wStr[:, half * 1024 + c * 128:half * 1024 + (c + 1) * 128],
                    xnB[:, c * 512:(c + 1) * 512], c == 0, c == NCK - 1, [bank])
                mark_slot(("wstr", half * 8 + c))
            if hc == 31:
                for c in range(NCK):
                    mark_slot(("xnb", c))
            guard_read("S", [bank])
            hbuf = wA if hc < 16 else wB
            hoff = (hc % 16) * 512
            S(lambda e, bank=bank, hbuf=hbuf, hoff=hoff: e.activation(
                out=hbuf[:, hoff:hoff + 512], in_=ps[bank][:, 0:512], func=AF.Gelu))
            mark_read("S", [bank])
        for hc in range(32):
            hbuf = wA if hc < 16 else wB
            hoff = (hc % 16) * 512
            W("S", "T")
            S(lambda e, hbuf=hbuf, hoff=hoff: e.activation(out=sqb[:, :], in_=hbuf[:, hoff:hoff + 512],
                                                           func=AF.Square))
            guard_write([0, 1])
            W("T", "S")
            mmp(ps[0][0:1, 0:512], onesb[:, 0:1], hbuf[:, hoff:hoff + 512], hc == 0, hc == 31,
                [0], inc=False)
            mmp(ps[1][0:1, 0:512], ones[:, 0:1], sqb[:, :], hc == 0, hc == 31, [1])
        guard_read("V", [0, 1])
        V(lambda e: e.tensor_scalar(out=rowA[:, :], in0=ps[0][0:1, 0:512], scalar1=1.0 / 4096,
                                    scalar2=None, op0=ALU.mult))
        V(lambda e: e.tensor_scalar(out=rowB[:, :], in0=ps[1][0:1, 0:512], scalar1=1.0 / 4096,
                                    scalar2=None, op0=ALU.mult))
        mark_read("V", [0, 1])
        V(lambda e: e.tensor_tensor(out=rowC[:, :], in0=rowA[:, :], in1=rowA[:, :], op=ALU.mult))
        V(lambda e: e.tensor_tensor(out=rowB[:, :], in0=rowB[:, :], in1=rowC[:, :], op=ALU.subtract))
        V(lambda e: e.tensor_scalar(out=rowB[:, :], in0=rowB[:, :], scalar1=EPS_LNF,
                                    scalar2=None, op0=ALU.add))
        W("S", "V")
        S(lambda e: e.activation(out=rowB[:, :], in_=rowB[:, :], func=AF.Sqrt))
        W("V", "S")
        V(lambda e: e.reciprocal(out=rowB[:, :], in_=rowB[:, :]))
        SW("V")
        V(lambda e: e.tensor_tensor(out=rowC[:, :], in0=rowA[:, :], in1=rowB[:, :], op=ALU.mult))
        V(lambda e: e.tensor_scalar(out=rowC[:, :], in0=rowC[:, :], scalar1=-1.0,
                                    scalar2=None, op0=ALU.mult))
        W("T", "V")
        guard_write([2, 3])
        mmp(ps[2][:, 0:512], ones[0:1, :], rowB[:, :], True, True, [2], inc=False)
        mmp(ps[3][:, 0:512], ones[0:1, :], rowC[:, :], True, True, [3])
        guard_read("V", [2, 3])
        V(lambda e: e.tensor_copy(out=absb[:, :], in_=ps[2][:, 0:512]))
        V(lambda e: e.tensor_copy(out=bbsb[:, :], in_=ps[3][:, 0:512]))
        mark_read("V", [2, 3])
        for hc in range(32):
            hbuf = wA if hc < 16 else wB
            hoff = (hc % 16) * 512
            W("V", "T")
            V(lambda e, hc=hc, hbuf=hbuf, hoff=hoff: e.tensor_scalar(
                out=hbuf[:, hoff:hoff + 512], in0=hbuf[:, hoff:hoff + 512],
                scalar1=lnfw[:, hc:hc + 1], scalar2=None, op0=ALU.mult))
        W("T", "V")
        for c in range(NCK):
            bank = 6 + (c % 2)
            guard_write([bank])
            for hc in range(32):
                slot = hc % 8
                ldw(wStr[:, slot * 128:(slot + 1) * 128],
                    wqp_s[hc * 128:(hc + 1) * 128, c * 128:(c + 1) * 128],
                    ("wstr", slot), "S%d" % slot, prod="quant")
                W("T", "S%d" % slot)
                hbuf = wA if hc < 16 else wB
                hoff = (hc % 16) * 512
                mmp(ps[bank][:, 0:512], wStr[:, slot * 128:(slot + 1) * 128],
                    hbuf[:, hoff:hoff + 512], hc == 0, hc == 31, [bank])
                mark_slot(("wstr", slot))
            guard_read("V", [bank])
            V(lambda e, bank=bank: e.tensor_tensor(out=tA[:, :], in0=ps[bank][:, 0:512],
                                                   in1=absb[:, :], op=ALU.mult))
            mark_read("V", [bank])
            V(lambda e, c=c: e.tensor_scalar(out=sqb[:, :], in0=bbsb[:, :], scalar1=cs1[:, c:c + 1],
                                             scalar2=cs2[:, c:c + 1], op0=ALU.mult, op1=ALU.add))
            V(lambda e: e.tensor_tensor(out=tA[:, :], in0=tA[:, :], in1=sqb[:, :], op=ALU.add))
            W("S", "V")
            S(lambda e, c=c: e.activation(out=gsb[:, c * 512:(c + 1) * 512], in_=tA[:, :], func=AF.Silu))
        W("G", "S")
        for c in range(NCK):
            dma("G", g2_s[c * 128:(c + 1) * 128, t0:t0 + TT], gsb[:, c * 512:(c + 1) * 512])

    load_w(wA, wo1T)
    kH = dlt[0:64, 0:4096]
    rH = xmix[0:64, 0:4096]
    rwbH = xnB[0:64, 0:4096]
    frepR = gsb[0:64, 0:4096]
    for h in range(HD):
        W("S", "V"); W("S", "G")
        S(lambda e, h=h: e.activation(out=frepR[:, h * 256:(h + 1) * 256], in_=tmod[0:64, 0:256],
                                      func=AF.Exp, scale=lnwH[:, h:h + 1]))
    NHT = 2 * NT
    for ht in range(NHT):
        t0 = ht * 256
        E("D", lambda e, t0=t0: e.dma_start(
            out=kH, in_=ap3(kf_s, t0, [[TLOC, 64], [64 * TLOC, 16], [1, 256]])), 16)
        E("D", lambda e, t0=t0: e.dma_start(
            out=rH, in_=ap3(r1_s, t0, [[TLOC, 64], [64 * TLOC, 16], [1, 256]])), 16)
        W("V", "D"); W("V", "S")
        V(lambda e: e.tensor_tensor(out=rwbH, in0=rH, in1=frepR, op=ALU.mult))
        for j in range(4):
            tj = t0 + j * 64
            ld(kchk, kt_s[tj:tj + 64, :])
            ld(vchk, vt_s[tj:tj + 64, :])
            W("V", "D")
            V(lambda e: e.tensor_tensor(out=kwkH, in0=kchk, in1=wkrep[0:64, :], op=ALU.mult))
            W("T", "V")
            guard_write([2, 3])
            for h in range(HD):
                bank = 2 + h // 8
                mmp(ps[bank][0:64, (h % 8) * 64:(h % 8) * 64 + 64],
                    kH[:, h * 256 + j * 64:h * 256 + j * 64 + 64],
                    rH[:, h * 256 + j * 64:h * 256 + j * 64 + 64],
                    True, True, [bank], inc=(h % 8 == 7))
            guard_read("V", [2, 3])
            V(lambda e: e.tensor_tensor(out=attwt[0:64, 0:512], in0=ps[2][0:64, 0:512],
                                        in1=wmt[0:64, 0:512], op=ALU.mult))
            V(lambda e: e.tensor_tensor(out=attwt[0:64, 512:1024], in0=ps[3][0:64, 0:512],
                                        in1=wmt[0:64, 512:1024], op=ALU.mult))
            mark_read("V", [2, 3])
            V(lambda e: e.tensor_copy(out=stateb_H[:, :], in_=state_H[:, :]))
            W("T", "V")
            guard_write([4])
            for h in range(HD):
                hp = 64 * (h % 2)
                mmp(ps[4][hp:hp + 64, (h // 2) * 64:(h // 2) * 64 + 64],
                    vchk[:, h * 64:(h + 1) * 64],
                    attwt[0:64, h * 64:(h + 1) * 64],
                    True, False, [4], inc=False, tp=(0, hp))
                mmp(ps[4][hp:hp + 64, (h // 2) * 64:(h // 2) * 64 + 64],
                    stateb_H[:, h * 64:(h + 1) * 64],
                    rwbH[:, h * 256 + j * 64:h * 256 + j * 64 + 64],
                    False, True, [4], inc=(h == HD - 1), tp=(0, hp))
            guard_read("V", [4])
            V(lambda e, j=j: e.tensor_copy(
                out=ap3(yL1, j * 64, [[2048, 128], [256, 8], [1, 64]]),
                in_=ap3(ps[4], 0, [[512, 128], [64, 8], [1, 64]])))
            mark_read("V", [4])
            scan_chunk()
        for p in range(NCK):
            W("S", "T"); W("S", "V")
            S(lambda e, p=p: e.activation(out=sqb[:, 0:256], in_=yL1[:, p * 256:(p + 1) * 256],
                                          func=AF.Square))
            guard_write([2, 3])
            W("T", "S"); W("T", "V")
            mmp(ps[2][0:16, 0:256], e16b[:, p * 16:(p + 1) * 16], yL1[:, p * 256:(p + 1) * 256],
                p == 0, p == NCK - 1, [2], inc=False)
            mmp(ps[3][0:16, 0:256], e16[:, p * 16:(p + 1) * 16], sqb[:, 0:256], p == 0, p == NCK - 1, [3])
        guard_read("V", [2, 3])
        V(lambda e: e.tensor_scalar(out=st16[:, 0:256], in0=ps[2][0:16, 0:256], scalar1=1.0 / 64,
                                    scalar2=None, op0=ALU.mult))
        V(lambda e: e.tensor_scalar(out=st16b[:, 0:256], in0=ps[3][0:16, 0:256], scalar1=1.0 / 64,
                                    scalar2=None, op0=ALU.mult))
        mark_read("V", [2, 3])
        V(lambda e: e.tensor_tensor(out=rows16[:, 0:256], in0=st16[:, 0:256], in1=st16[:, 0:256],
                                    op=ALU.mult))
        V(lambda e: e.tensor_tensor(out=st16b[:, 0:256], in0=st16b[:, 0:256], in1=rows16[:, 0:256],
                                    op=ALU.subtract))
        V(lambda e: e.tensor_scalar(out=st16b[:, 0:256], in0=st16b[:, 0:256], scalar1=EPS_GN,
                                    scalar2=None, op0=ALU.add))
        W("S", "V")
        S(lambda e: e.activation(out=st16b[:, 0:256], in_=st16b[:, 0:256], func=AF.Sqrt))
        W("V", "S")
        V(lambda e: e.reciprocal(out=st16b[:, 0:256], in_=st16b[:, 0:256]))
        SW("V")
        V(lambda e: e.tensor_tensor(out=st16[:, 0:256], in0=st16[:, 0:256], in1=st16b[:, 0:256],
                                    op=ALU.mult))
        V(lambda e: e.tensor_scalar(out=st16[:, 0:256], in0=st16[:, 0:256], scalar1=-1.0,
                                    scalar2=None, op0=ALU.mult))
        W("T", "V")
        for p in range(NCK):
            guard_write([2, 3])
            mmp(ps[2][:, 0:256], e16t[:, p * 128:(p + 1) * 128], st16b[0:16, 0:256], True, True,
                [2], inc=False)
            mmp(ps[3][:, 0:256], e16t[:, p * 128:(p + 1) * 128], st16[0:16, 0:256], True, True, [3])
            ld(jit2[:, 0:256], g1_s[p * 128:(p + 1) * 128, t0:t0 + 256])
            guard_read("V", [2, 3])
            W("V", "D")
            V(lambda e, p=p: e.tensor_tensor(out=tA[:, 0:256], in0=yL1[:, p * 256:(p + 1) * 256],
                                             in1=ps[2][:, 0:256], op=ALU.mult))
            V(lambda e: e.tensor_tensor(out=tA[:, 0:256], in0=tA[:, 0:256], in1=ps[3][:, 0:256],
                                        op=ALU.add))
            mark_read("V", [2, 3])
            V(lambda e, p=p: e.tensor_scalar(out=tBb[:, 0:256], in0=tA[:, 0:256],
                                             scalar1=ln1w[:, p:p + 1],
                                             scalar2=ln1b[:, p:p + 1], op0=ALU.mult, op1=ALU.add))
            V(lambda e, p=p: e.tensor_tensor(out=ysb[:, p * 256:(p + 1) * 256], in0=tBb[:, 0:256],
                                             in1=jit2[:, 0:256], op=ALU.mult))
        W("T", "V")
        for c in range(NCK):
            bank = 6 + (c % 2)
            guard_write([bank])
            for k in range(NCK):
                mmp(ps[bank][:, 0:256], wA[:, k * 1024 + c * 128:k * 1024 + (c + 1) * 128],
                    ysb[:, k * 256:(k + 1) * 256], k == 0, k == NCK - 1, [bank])
            ld(jit1[:, 0:256], x1_s[c * 128:(c + 1) * 128, t0:t0 + 256])
            ld(jit2[:, 0:256], g2_s[c * 128:(c + 1) * 128, t0:t0 + 256])
            guard_read("V", [bank])
            W("V", "D")
            V(lambda e, bank=bank: e.tensor_tensor(out=tA[:, 0:256], in0=ps[bank][:, 0:256],
                                                   in1=jit2[:, 0:256], op=ALU.mult))
            mark_read("V", [bank])
            V(lambda e: e.tensor_tensor(out=sqb[:, 0:256], in0=tA[:, 0:256], in1=jit1[:, 0:256],
                                        op=ALU.add))
            W("G", "V")
            dma("G", out_d[c * 128:(c + 1) * 128, t0:t0 + 256], sqb[:, 0:256])

    steps.append(("G", lambda eng, v=cnt["G"]: eng.wait_ge(sem_handles["G"], v)))

    if nsteps is not None:
        nsteps = min(nsteps, len(cnt_hist))
        final = cnt_hist[nsteps - 1]
        steps = steps[:nsteps]
        for e in ("T", "V", "S", "D", "G"):
            for s in ("T", "V", "S", "D", "G", "C"):
                vv = final.get(s, 0)
                if vv > 0:
                    steps.append((e, lambda eng, s2=s, v=vv: eng.wait_ge(sem_handles[s2], v)))
    from contextlib import ExitStack
    with ExitStack() as _stack:
        for _n in ["T", "V", "S", "D", "G", "C"] + STREAMS:
            sem_handles[_n] = _stack.enter_context(nc.semaphore("sem" + _n))
        block = _stack.enter_context(nc.Block())
        @block.tensor
        def _(eng):
            for en, fn in steps:
                if en == "T":
                    fn(eng)
        @block.vector
        def _(eng):
            for en, fn in steps:
                if en == "V":
                    fn(eng)
        @block.scalar
        def _(eng):
            for en, fn in steps:
                if en == "S":
                    fn(eng)
        @block.sync
        def _(eng):
            for en, fn in steps:
                if en == "D":
                    fn(eng)
        @block.gpsimd
        def _(eng):
            for en, fn in steps:
                if en == "G":
                    fn(eng)
    return nc


def _consts():
    i128 = np.arange(128)
    c = {}
    c["c_ones"] = np.ones((128, 128), np.float32)
    c["c_onesb"] = np.ones((128, 128), BF)
    c["c_ident"] = np.eye(128, dtype=np.float32)
    c["c_ltri"] = (np.arange(128)[:, None] <= np.arange(128)[None, :]).astype(BF)
    e16 = np.zeros((128, 128), np.float32)
    for p in range(8):
        for r in range(128):
            e16[r, p * 16 + (2 * p + r // 64)] = 1.0
    c["c_e16"] = e16
    e16t = np.zeros((16, 1024), np.float32)
    for p in range(8):
        for r in range(128):
            e16t[2 * p + r // 64, p * 128 + r] = 1.0
    c["c_e16t"] = e16t
    s64 = np.zeros((16, 1024), np.float32)
    for k in range(16):
        s64[k, k * 64:(k + 1) * 64] = 1.0
    c["c_s64"] = s64
    psl = np.zeros((16, 8), np.float32)
    for k in range(16):
        psl[k, k // 2] = 1.0
    c["c_psel"] = psl
    hs = np.zeros((16, 128), np.float32)
    for k in range(16):
        hs[k, (k % 2) * 64:(k % 2) * 64 + 64] = 1.0
    c["c_hsel"] = hs
    q2g = i128 % 64
    q1g = np.arange(1024) % 64
    c["c_edt"] = np.maximum(q1g[None, :] - q2g[:, None] - 1, 0).astype(np.float32)
    c["c_trit"] = (q1g[None, :] > q2g[:, None]).astype(np.float32)
    c["c_eyet"] = (q1g[None, :] == q2g[:, None]).astype(np.float32)
    c["c_tmod"] = np.broadcast_to((np.arange(512) % 64).astype(BF), (128, 512)).copy()
    c["c_qrev"] = (63 - (i128 % 64)).astype(np.float32).reshape(128, 1)
    c["c_qidx"] = i128.astype(np.float32).reshape(128, 1)
    c["c_ar256"] = np.broadcast_to(np.arange(256).astype(BF), (128, 256)).copy()
    return c


def _vec_fm(v, k):
    return np.ascontiguousarray(v.reshape(k, 128).T.astype(np.float32))


def make_shared(inputs):
    f = np.float32
    T = lambda a: np.ascontiguousarray(np.asarray(a, f).T)
    C = lambda a: np.ascontiguousarray(np.asarray(a, f))
    shared = {
        "wkT": T(inputs["L0_attn_Wk"]), "wvT": T(inputs["L0_attn_Wv"]),
        "wrT": T(inputs["L0_attn_Wr"]), "woT": T(inputs["L0_attn_Wo"]),
        "w1T": T(inputs["L0_ffn_w1"]), "w1N": C(inputs["L0_ffn_w1"]),
        "w2T": T(inputs["L0_ffn_w2"]), "w2N": C(inputs["L0_ffn_w2"]),
        "wr1T": T(inputs["L1_attn_Wr"]), "wk1T": T(inputs["L1_attn_Wk"]),
        "wv1T": T(inputs["L1_attn_Wv"]), "wg1T": T(inputs["L1_attn_Wg"]),
        "wo1T": T(inputs["L1_attn_Wo"]),
        "wfcT": T(inputs["L1_ffn_W_fc"]), "wpjT": T(inputs["L1_ffn_W_proj"]),
        "p_decay": _vec_fm(np.asarray(inputs["L0_attn_decay"], f), 8),
        "p_decrow": np.ascontiguousarray(np.asarray(inputs["L0_attn_decay"], f).reshape(1, 1024)),
        "p_ln0w": _vec_fm(np.asarray(inputs["L0_attn_ln_w"], f), 8),
        "p_b1": _vec_fm(np.asarray(inputs["L0_ffn_b1"], f), 32),
        "p_b2": _vec_fm(np.asarray(inputs["L0_ffn_b2"], f), 8),
        "p_maa": np.concatenate([_vec_fm(np.asarray(inputs[n], f).reshape(-1), 8)
                                 for n in ["L1_attn_time_maa_k", "L1_attn_time_maa_v",
                                           "L1_attn_time_maa_r", "L1_attn_time_maa_g"]], axis=1),
        "p_td": np.asarray(inputs["L1_attn_time_decay"], f).reshape(16, 1),
        "p_fa": np.asarray(inputs["L1_attn_time_faaaa"], f).reshape(16, 1),
        "p_ln1w": _vec_fm(np.asarray(inputs["L1_attn_ln_w"], f), 8),
        "p_ln1b": _vec_fm(np.asarray(inputs["L1_attn_ln_b"], f), 8),
        "p_lnfw": _vec_fm(np.asarray(inputs["L1_ffn_ln_w"], f), 32),
        "p_lnfb": _vec_fm(np.asarray(inputs["L1_ffn_ln_b"], f), 32),
    }
    shared.update(_consts())
    return shared


_CACHE = {}


def kernel(**inputs):
    NT = 8
    TLOC = NT * TT
    key = ("prog", NT)
    if key not in _CACHE:
        _CACHE[key] = build_program(NT=NT)
    nc = _CACHE[key]
    x = np.asarray(inputs["L0_attn_x"], np.float32)
    shared = make_shared(inputs)
    in_maps = []
    for core in range(8):
        b, h = core // 2, core % 2
        m = np.zeros((128, 2), np.float32)
        m[:, 0] = 1.0 - h
        m[:, 1] = h
        im = dict(shared)
        im["x"] = np.ascontiguousarray(x[b, h * TLOC:(h + 1) * TLOC, :].T)
        im["m_sr"] = m
        in_maps.append(im)
    res = run_bass_kernel_spmd(nc, in_maps, core_ids=list(range(8)))
    out = np.empty((4, 2 * TLOC, D), np.float32)
    for core in range(8):
        b, h = core // 2, core % 2
        out[b, h * TLOC:(h + 1) * TLOC, :] = np.asarray(res.results[core]["out"]).T
    return out


# version 12
# speedup vs baseline: 1.7647x; 1.0588x over previous
import math
import numpy as np
import ml_dtypes
import concourse.bass as bass
import concourse.mybir as mybir
from concourse.bass_utils import run_bass_kernel_spmd

F32 = mybir.dt.float32
BF16 = mybir.dt.bfloat16
AF = mybir.ActivationFunctionType
ALU = mybir.AluOpType
AX = mybir.AxisListType
BF = ml_dtypes.bfloat16

D = 1024
NCK = 8
HD = 16
TT = 512
EPS_LN = 1e-5
EPS_RMS = 1e-6
EPS_GN = 1e-5 * 64
EPS_LNF = 1e-5


def ap3(t, off, dims):
    return bass.AP(t, off, dims)


def build_program(NT=8, rg=None, taps=(), strict=False, nsteps=None):
    if rg is None:
        rg = [[0, 1], [2, 3], [4, 5], [6, 7]]
    TLOC = NT * TT
    nc = bass.Bass("TRN2", target_bir_lowering=False,
                   detect_race_conditions=bool(strict))

    def din(name, shape, dt=F32):
        return nc.dram_tensor(name, list(shape), dt, kind="ExternalInput")

    x_d = din("x", [D, TLOC])
    wkT = din("wkT", [D, D]); wvT = din("wvT", [D, D])
    wrT = din("wrT", [D, D]); woT = din("woT", [D, D])
    w1T = din("w1T", [D, 4096]); w1N = din("w1N", [4096, D])
    w2T = din("w2T", [4096, D]); w2N = din("w2N", [D, 4096])
    wr1T = din("wr1T", [D, D]); wk1T = din("wk1T", [D, D]); wv1T = din("wv1T", [D, D])
    wg1T = din("wg1T", [D, D]); wo1T = din("wo1T", [D, D])
    wfcT = din("wfcT", [D, 4096]); wpjT = din("wpjT", [4096, D])
    p_decay = din("p_decay", [128, 8]); p_ln0w = din("p_ln0w", [128, 8])
    p_b1 = din("p_b1", [128, 32]); p_b2 = din("p_b2", [128, 8])
    p_maa = din("p_maa", [128, 32])
    p_td = din("p_td", [16, 1]); p_fa = din("p_fa", [16, 1])
    p_ln1w = din("p_ln1w", [128, 8]); p_ln1b = din("p_ln1b", [128, 8])
    p_lnfw = din("p_lnfw", [128, 32]); p_lnfb = din("p_lnfb", [128, 32])
    c_ones = din("c_ones", [128, 128])
    c_onesb = din("c_onesb", [128, 128], BF16)
    c_ident = din("c_ident", [128, 128])
    c_ltri = din("c_ltri", [128, 128], BF16)
    c_e16 = din("c_e16", [128, 128])
    c_e16t = din("c_e16t", [16, 1024])
    c_s64 = din("c_s64", [16, 1024])
    c_psel = din("c_psel", [16, 8])
    c_hsel = din("c_hsel", [16, 128])
    c_edt = din("c_edt", [128, 1024])
    c_trit = din("c_trit", [128, 1024])
    c_eyet = din("c_eyet", [128, 1024])
    c_tmod = din("c_tmod", [128, 512], BF16)
    c_qrev = din("c_qrev", [128, 1])
    p_decrow = din("p_decrow", [1, 1024])
    c_qidx = din("c_qidx", [128, 1])
    c_ar256 = din("c_ar256", [128, 256], BF16)
    m_sr = din("m_sr", [128, 2])

    def scr(name, shape, dt):
        k = "ExternalOutput" if name in taps else "Internal"
        return nc.dram_tensor(name, list(shape), dt, kind=k)

    xn_s = scr("xn_s", [D, TLOC], BF16)
    terms_s = scr("terms_s", [TLOC, D], BF16)
    g_s = scr("g_s", [D, TLOC], BF16)
    x1_s = scr("x1_s", [D, TLOC], F32)
    xn1_s = scr("xn1_s", [D, TLOC], BF16)
    r1_s = scr("r1_s", [D, TLOC], BF16)
    kf_s = scr("kf_s", [D, TLOC], BF16)
    kt_s = scr("kt_s", [TLOC, D], BF16)
    vt_s = scr("vt_s", [TLOC, D], BF16)
    g1_s = scr("g1_s", [D, TLOC], BF16)
    g2_s = scr("g2_s", [D, TLOC], BF16)
    sw1_s = scr("sw1_s", [D, 4096], BF16)
    sw2_s = scr("sw2_s", [4096, D], BF16)
    wqf_s = scr("wqf_s", [D, 4096], BF16)
    wqp_s = scr("wqp_s", [4096, D], BF16)
    cc1i = nc.dram_tensor("cc1i", [1, 1024], F32); cc1o = nc.dram_tensor("cc1o", [1, 1024], F32)
    cc2i = nc.dram_tensor("cc2i", [128, 8], F32); cc2o = nc.dram_tensor("cc2o", [128, 8], F32)
    cc3i = nc.dram_tensor("cc3i", [64, 1024], F32); cc3o = nc.dram_tensor("cc3o", [64, 1024], F32)
    out_d = nc.dram_tensor("out", [D, TLOC], F32, kind="ExternalOutput")
    h_dbg = scr("h_dbg", [4096, TLOC], BF16) if "h_dbg" in taps else None
    a_dbg = scr("a_dbg", [128, 64], F32) if "a_dbg" in taps else None
    b_dbg = scr("b_dbg", [128, 64], F32) if "b_dbg" in taps else None
    y0_dbg = nc.dram_tensor("y0_dbg", [D, 512], BF16, kind="ExternalOutput") if "b_dbg" in taps else None
    z_dbg = scr("z_dbg", [4096, TLOC], F32) if "z_dbg" in taps else None
    o_dbg = scr("o_dbg", [D, TLOC], F32) if "o_dbg" in taps else None
    o2_dbg = scr("o2_dbg", [D, TLOC], F32) if "o_dbg" in taps else None
    y_dbg = scr("y_dbg", [D, TLOC], BF16) if "y_dbg" in taps else None

    A = nc.alloc_sbuf_tensor
    ones = A("ones", [128, 128], F32); onesb = A("onesb", [128, 128], BF16)
    ident = A("ident", [128, 128], F32); ltri = A("ltri", [128, 128], BF16)
    e16 = A("e16", [128, 128], F32); e16t = A("e16t", [16, 1024], F32)
    psel = A("psel", [16, 8], F32); hsel = A("hsel", [16, 128], F32)
    tmod = A("tmod", [128, 512], BF16); qrev = A("qrev", [128, 1], F32)
    msr = A("msr", [128, 2], F32)
    decay = A("decay", [128, 8], F32); ln0w = A("ln0w", [128, 8], F32)
    b1c = A("b1c", [128, 32], F32); b2c = A("b2c", [128, 8], F32)
    maa = A("maa", [128, 32], F32)
    tdc = A("tdc", [16, 1], F32); fac = A("fac", [16, 1], F32)
    ln1w = A("ln1w", [128, 8], F32); ln1b = A("ln1b", [128, 8], F32)
    lnfw = A("lnfw", [128, 32], F32); lnfb = A("lnfb", [128, 32], F32)
    lnfwB = A("lnfwB", [128, 32], BF16); lnfbB = A("lnfbB", [128, 32], BF16)
    dec = A("dec", [128, 8], F32)
    d512 = A("d512", [128, 8 * 256], BF16)
    lfm = A("lfm", [128, 8], F32)
    lfmH = A("lfmH", [128, 8], F32)
    lrow = A("lrow", [1, 1024], F32)
    lrowH = A("lrowH", [1, 1024], F32)
    invdt = A("invdt", [128, 1024], BF16)
    invcr = A("invcr", [1, 1024], F32)
    qidx = A("qidx", [128, 1], F32)
    ar256 = A("ar256", [128, 256], BF16)
    decrow = A("decrow", [1, 1024], F32)
    ccol = A("ccol", [128, 8], F32); ccol2 = A("ccol2", [128, 8], F32)
    ptmp = A("ptmp", [128, 8], F32)
    carry0 = A("carry0", [128, 8], F32)
    crun = A("crun", [128, 8], F32)
    ucol = A("ucol", [16, 1], F32); lnwcol = A("lnwcol", [16, 1], F32)
    lnwfm = A("lnwfm", [128, 8], F32)
    wkrep = A("wkrep", [128, 1024], BF16)
    e16b = A("e16b", [128, 128], BF16)
    lnwH = A("lnwH", [64, 16], F32)
    wsH = A("wsH", [64, 16], F32)
    wsrepH = A("wsrepH", [64, 1024], F32)
    state_H = A("state_H", [64, 1024], F32)
    stateb_H = A("stateb_H", [64, 1024], BF16)
    wmt = A("wmt", [128, 1024], BF16)
    rows16 = A("rows16", [16, 512], F32)
    st16 = A("st16", [16, 512], F32)
    st16b = A("st16b", [16, 512], F32)
    wA = A("wA", [128, 8 * 1024], BF16)
    wB = A("wB", [128, 8 * 1024], BF16)
    wStr = A("wStr", [128, 2 * 1024], BF16)
    stg = A("stg", [128, 4096], F32)
    wtmp = A("wtmp", [128, 2048], F32)
    xnA = A("xnA", [128, 8 * 512], BF16)
    xnB = A("xnB", [128, 8 * 512], BF16)
    trm = A("trm", [128, 1024], BF16)
    ysb = A("ysb", [128, 8 * 512], BF16)
    rsb = A("rsb", [128, 8 * 512], BF16)
    gsb = A("gsb", [128, 8 * 512], BF16)
    dlt = A("dlt", [128, 8 * 512], BF16)
    xmix = A("xmix", [128, 8 * 512], BF16)
    ktm = A("ktm", [128, 1024], BF16)
    vtm = A("vtm", [128, 1024], BF16)
    kwkb = A("kwkb", [128, 1024], BF16)
    attwt = A("attwt", [128, 1024], BF16)
    yL1 = A("yL1", [128, 2048], BF16)
    rowA = A("rowA", [1, 512], F32); rowB = A("rowB", [1, 512], F32)
    rowC = A("rowC", [1, 512], F32)
    absb = A("absb", [128, 512], F32)
    bbsb = A("bbsb", [128, 512], F32)
    absbB = A("absbB", [128, 512], BF16)
    sqb = A("sqb", [128, 512], F32)
    jit1 = A("jit1", [128, 512], F32)
    jit2 = A("jit2", [128, 512], BF16)
    tA = A("tA", [128, 512], F32)
    tBb = A("tBb", [128, 512], BF16)
    tCb = A("tCb", [128, 512], BF16)
    shb = A("shb", [128, 512], BF16)
    pcol = A("pcol", [128, 8], BF16)
    brow = A("brow", [128, 8], F32)
    cbuf = A("cbuf", [128, 512], F32)
    alpha1 = A("alpha1", [128, 32], F32)
    alpha2 = A("alpha2", [128, 8], F32)
    gcol = A("gcol", [128, 1], F32)
    gcol2 = A("gcol2", [128, 1], F32)
    cs1 = A("cs1", [128, 8], F32); cs2 = A("cs2", [128, 8], F32)
    rowg = A("rowg", [1, 512], F32)

    s64v = stg[0:16, 0:1024]
    edtv = stg[:, 1024:2048]
    tritv = stg[:, 2048:3072]
    eyetv = stg[:, 3072:4096]

    ps = [nc.alloc_psum_tensor(f"ps{i}", [128, 512], F32) for i in range(8)]

    ENG = ["T", "V", "S", "D", "G", "C"]
    STREAMS = ["H0", "H1", "X", "T2"] + [f"S{i}" for i in range(8)]
    cnt = {e: 0 for e in ENG + STREAMS}
    steps = []
    psum_r = {}
    psum_w = {}
    sem_handles = {}
    last_wait = {}

    cnt_hist = []
    def E(engn, fn, inc=0):
        if engn != "D":
            for o in ("T", "V", "S", "D", "G"):
                W(engn, o)
        if engn in ("V", "S") and cnt[engn] > 0:
            steps.append((engn, lambda eng, s=engn, v=cnt[engn]: eng.wait_ge(sem_handles[s], v)))
            cnt_hist.append(dict(cnt))
        elif strict and cnt[engn] > 0:
            steps.append((engn, lambda eng, s=engn, v=cnt[engn]: eng.wait_ge(sem_handles[s], v)))
            cnt_hist.append(dict(cnt))
        if inc:
            cnt[engn] += inc
        def run(eng, fn=fn, inc=inc, engn=engn):
            r = fn(eng)
            if inc:
                r.then_inc(sem_handles[engn], inc)
        steps.append((engn, run))
        cnt_hist.append(dict(cnt))

    def W(engn, src, val=None):
        if engn == src:
            return
        v = cnt[src] if val is None else val
        if v <= 0 or last_wait.get((engn, src), -1) >= v:
            return
        last_wait[(engn, src)] = v
        steps.append((engn, lambda eng, s=src, v=v: eng.wait_ge(sem_handles[s], v)))
        cnt_hist.append(dict(cnt))

    def mark_read(engn, banks):
        for b in banks:
            psum_r[b] = (engn, cnt[engn])

    def mark_write(banks):
        for b in banks:
            psum_w[b] = ("T", cnt["T"])

    def guard_write(banks):
        seen = {}
        for b in banks:
            if b in psum_r:
                s, v = psum_r[b]
                seen[s] = max(seen.get(s, 0), v)
        for s, v in seen.items():
            W("T", s, v)

    def guard_read(engn, banks):
        vv = 0
        for b in banks:
            if b in psum_w:
                vv = max(vv, psum_w[b][1])
        if vv:
            W(engn, "T", vv)

    def dma(qeng, dst, src):
        E(qeng, lambda e, d=dst, s=src: e.dma_start(out=d, in_=s), 16)

    def ld(dst, src):
        W("D", "T"); W("D", "V"); W("D", "S"); W("D", "G")
        dma("D", dst, src)

    slotrd = {}
    gprod = {}

    def gsnap(name):
        gprod[name] = cnt["G"]

    def dma_s(dst, src, semname):
        cnt[semname] += 16
        def run(eng, d=dst, s=src, sn=semname):
            eng.dma_start(out=d, in_=s).then_inc(sem_handles[sn], 16)
        steps.append(("D", run))
        cnt_hist.append(dict(cnt))

    def ldw(dst, src, key, sem, prod=None):
        if prod is not None:
            W("D", "G", gprod[prod])
        for e, v in slotrd.get(key, {}).items():
            W("D", e, v)
        dma_s(dst, src, sem)

    def mark_slot(key, eng="T"):
        slotrd.setdefault(key, {})[eng] = cnt[eng]

    def mmp(bank_ap, lh, rh, start, stop, banks, inc=True, tp=None):
        def fn(e, o=bank_ap, l=lh, r=rh, st=start, sp=stop, t=tp):
            if t is None:
                return e.matmul(o, l, r, start=st, stop=sp)
            return e.matmul(o, l, r, start=st, stop=sp, tile_position=t)
        E("T", fn, 1 if (inc or strict) else 0)
        mark_write(banks)

    def V(fn):
        E("V", fn, 1)

    def SW(engn):
        steps.append((engn, lambda eng, s=engn, v=cnt[engn]: eng.wait_ge(sem_handles[s], v)))
        cnt_hist.append(dict(cnt))

    def S(fn):
        E("S", fn, 1)

    def G(fn):
        E("G", fn, 1)

    def CC(fn):
        for o in ("T", "V", "S", "D"):
            W("G", o)
        steps.append(("G", lambda eng, v=cnt["G"]: eng.wait_ge(sem_handles["G"], v)))
        cnt_hist.append(dict(cnt))
        cnt["C"] += 1
        def run(eng, fn=fn):
            fn(eng).then_inc(sem_handles["C"], 1)
        steps.append(("G", run))
        cnt_hist.append(dict(cnt))
        steps.append(("G", lambda eng, v=cnt["C"]: eng.wait_ge(sem_handles["C"], v)))
        cnt_hist.append(dict(cnt))

    for dst, src in [(ones, c_ones), (onesb, c_onesb), (ident, c_ident), (ltri, c_ltri),
                     (e16, c_e16), (e16t, c_e16t), (psel, c_psel), (hsel, c_hsel),
                     (tmod, c_tmod), (qrev, c_qrev), (msr, m_sr),
                     (decrow, p_decrow), (qidx, c_qidx), (ar256, c_ar256),
                     (decay, p_decay), (ln0w, p_ln0w), (b1c, p_b1), (b2c, p_b2), (maa, p_maa),
                     (tdc, p_td), (fac, p_fa), (ln1w, p_ln1w), (ln1b, p_ln1b),
                     (lnfw, p_lnfw), (lnfb, p_lnfb),
                     (s64v, c_s64[:, :]), (edtv, c_edt[:, :]), (tritv, c_trit[:, :]),
                     (eyetv, c_eyet[:, :])]:
        dma("D", dst if isinstance(dst, bass.AP) else dst[:, :],
            src if isinstance(src, bass.AP) else src[:, :])
    W("S", "D"); W("V", "D"); W("T", "D"); W("G", "D")

    S(lambda e: e.activation(out=dec[:, :], in_=decay[:, :], func=AF.Sigmoid))
    S(lambda e: e.activation(out=lrow[:, :], in_=decrow[:, :], func=AF.Sigmoid))
    W("V", "S")
    V(lambda e: e.tensor_scalar(out=dec[:, :], in0=dec[:, :], scalar1=1e-7, scalar2=None, op0=ALU.max))
    V(lambda e: e.tensor_scalar(out=lrow[:, :], in0=lrow[:, :], scalar1=1e-7, scalar2=None, op0=ALU.max))
    W("S", "V")
    S(lambda e: e.activation(out=lfm[:, :], in_=dec[:, :], func=AF.Ln))
    S(lambda e: e.activation(out=lrow[:, :], in_=lrow[:, :], func=AF.Ln))
    W("V", "S")
    V(lambda e: e.tensor_scalar(out=lfmH[:, :], in0=lfm[:, :], scalar1=msr[:, 1:2],
                                scalar2=None, op0=ALU.mult))
    V(lambda e: e.tensor_scalar(out=lfmH[:, :], in0=lfmH[:, :], scalar1=float(TLOC),
                                scalar2=None, op0=ALU.mult))
    V(lambda e: e.tensor_scalar(out=lrowH[:, :], in0=lrow[:, :], scalar1=msr[0:1, 1:2],
                                scalar2=None, op0=ALU.mult))
    V(lambda e: e.tensor_scalar(out=lrowH[:, :], in0=lrowH[:, :], scalar1=-float(TLOC),
                                scalar2=None, op0=ALU.mult))
    W("S", "V")
    for c in range(8):
        S(lambda e, c=c: e.activation(out=d512[:, c * 256:(c + 1) * 256], in_=ar256[:, :],
                                      func=AF.Exp, scale=lfm[:, c:c + 1]))
    W("T", "S")
    guard_write([1, 2])
    mmp(ps[1][:, 0:512], ones[0:1, :], lrow[:, 0:512], True, True, [1], inc=False)
    mmp(ps[2][:, 0:512], ones[0:1, :], lrow[:, 512:1024], True, True, [2])
    guard_read("V", [1, 2])
    V(lambda e: e.tensor_copy(out=wtmp[:, 0:512], in_=ps[1][:, 0:512]))
    V(lambda e: e.tensor_copy(out=wtmp[:, 512:1024], in_=ps[2][:, 0:512]))
    mark_read("V", [1, 2])
    W("S", "V")
    S(lambda e: e.activation(out=wtmp[:, 0:1024], in_=wtmp[:, 0:1024], func=AF.Exp, scale=qidx[:, 0:1]))
    W("V", "S")
    V(lambda e: e.tensor_scalar(out=wtmp[:, 0:1024], in0=wtmp[:, 0:1024], scalar1=1e-11,
                                scalar2=None, op0=ALU.max))
    V(lambda e: e.reciprocal(out=wtmp[:, 0:1024], in_=wtmp[:, 0:1024]))
    SW("V")
    V(lambda e: e.tensor_scalar(out=invdt[:, :], in0=wtmp[:, 0:1024], scalar1=1.001e10,
                                scalar2=None, op0=ALU.min))

    lnwbc = wtmp[:, 0:1024]
    S(lambda e: e.activation(out=ucol[:, :], in_=tdc[:, :], func=AF.Exp))
    W("V", "S")
    V(lambda e: e.tensor_scalar(out=lnwcol[:, :], in0=ucol[:, :], scalar1=-1.0, scalar2=None, op0=ALU.mult))
    V(lambda e: e.tensor_scalar(out=rows16[:, 0:8], in0=psel[:, :], scalar1=lnwcol[:, 0:1],
                                scalar2=None, op0=ALU.mult))
    W("T", "V")
    guard_write([1])
    mmp(ps[1][:, 0:8], hsel[:, :], rows16[0:16, 0:8], True, True, [1])
    guard_read("V", [1])
    V(lambda e: e.tensor_copy(out=lnwfm[:, :], in_=ps[1][:, 0:8]))
    mark_read("V", [1])
    V(lambda e: e.tensor_scalar(out=rows16[:, 0:512], in0=s64v[:, 0:512], scalar1=lnwcol[:, 0:1],
                                scalar2=None, op0=ALU.mult))
    V(lambda e: e.tensor_scalar(out=st16[:, 0:512], in0=s64v[:, 512:1024], scalar1=lnwcol[:, 0:1],
                                scalar2=None, op0=ALU.mult))
    W("T", "V")
    guard_write([1, 2])
    mmp(ps[1][:, 0:512], ones[0:16, :], rows16[0:16, 0:512], True, True, [1], inc=False)
    mmp(ps[2][:, 0:512], ones[0:16, :], st16[0:16, 0:512], True, True, [2])
    guard_read("V", [1, 2])
    V(lambda e: e.tensor_copy(out=lnwbc[:, 0:512] if False else wtmp[:, 0:512], in_=ps[1][:, 0:512]))
    V(lambda e: e.tensor_copy(out=wtmp[:, 512:1024], in_=ps[2][:, 0:512]))
    mark_read("V", [1, 2])
    W("S", "V")
    S(lambda e: e.activation(out=wkrep[:, :], in_=wtmp[:, 0:1024], func=AF.Exp, scale=qrev[:, 0:1]))
    V(lambda e: e.tensor_scalar(out=rows16[:, 0:16], in0=ident[0:16, 0:16],
                                scalar1=lnwcol[:, 0:1], scalar2=None, op0=ALU.mult))
    W("T", "V")
    guard_write([3])
    mmp(ps[3][0:64, 0:16], ones[0:16, 0:64], rows16[0:16, 0:16], True, True, [3])
    guard_read("V", [3])
    V(lambda e: e.tensor_copy(out=lnwH[:, :], in_=ps[3][0:64, 0:16]))
    mark_read("V", [3])
    W("S", "V")
    S(lambda e: e.activation(out=wsH[:, :], in_=lnwH[:, :], func=AF.Exp, scale=64.0))
    W("V", "S")
    for h in range(16):
        V(lambda e, h=h: e.tensor_scalar(out=wsrepH[:, h * 64:(h + 1) * 64],
                                         in0=ones[0:64, 0:64],
                                         scalar1=wsH[:, h:h + 1], scalar2=None, op0=ALU.mult))
    V(lambda e: e.tensor_copy(out=e16b[:, :], in_=e16[:, :]))
    V(lambda e: e.tensor_tensor(out=cbuf[:, :], in0=edtv[:, 0:512], in1=wtmp[:, 0:512], op=ALU.mult))
    V(lambda e: e.tensor_tensor(out=tA[:, :], in0=edtv[:, 512:1024], in1=wtmp[:, 512:1024], op=ALU.mult))
    W("S", "V")
    S(lambda e: e.activation(out=wmt[:, 0:512], in_=cbuf[:, :], func=AF.Exp))
    S(lambda e: e.activation(out=wmt[:, 512:1024], in_=tA[:, :], func=AF.Exp))
    W("V", "S")
    V(lambda e: e.tensor_tensor(out=wmt[:, :], in0=wmt[:, :], in1=tritv, op=ALU.mult))
    V(lambda e: e.tensor_scalar(out=rows16[:, 0:512], in0=s64v[:, 0:512], scalar1=fac[:, 0:1],
                                scalar2=None, op0=ALU.mult))
    V(lambda e: e.tensor_scalar(out=st16[:, 0:512], in0=s64v[:, 512:1024], scalar1=fac[:, 0:1],
                                scalar2=None, op0=ALU.mult))
    W("T", "V")
    guard_write([1, 2])
    mmp(ps[1][:, 0:512], ones[0:16, :], rows16[0:16, 0:512], True, True, [1], inc=False)
    mmp(ps[2][:, 0:512], ones[0:16, :], st16[0:16, 0:512], True, True, [2])
    guard_read("V", [1, 2])
    V(lambda e: e.tensor_tensor(out=wtmp[:, 0:512], in0=ps[1][:, 0:512], in1=eyetv[:, 0:512], op=ALU.mult))
    V(lambda e: e.tensor_tensor(out=wtmp[:, 512:1024], in0=ps[2][:, 0:512],
                                in1=eyetv[:, 512:1024], op=ALU.mult))
    mark_read("V", [1, 2])
    V(lambda e: e.tensor_tensor(out=wmt[:, :], in0=wmt[:, :], in1=wtmp[:, 0:1024], op=ALU.add))
    V(lambda e: e.tensor_copy(out=lnfwB[:, :], in_=lnfw[:, :]))
    V(lambda e: e.tensor_copy(out=lnfbB[:, :], in_=lnfb[:, :]))

    def load_w(dst, src):
        for c in range(NCK):
            ld(stg[:, 0:1024], src[c * 128:(c + 1) * 128, :])
            W("V", "D")
            V(lambda e, c=c, dst=dst: e.tensor_copy(out=dst[:, c * 1024:(c + 1) * 1024],
                                                    in_=stg[:, 0:1024]))
        W("T", "V")

    def ln_stream(src_d, t0, outbuf, eps):
        for c in range(NCK):
            ld(jit1[:, :], src_d[c * 128:(c + 1) * 128, t0:t0 + TT])
            W("S", "D"); W("S", "T")
            S(lambda e: e.activation(out=sqb[:, :], in_=jit1[:, :], func=AF.Square))
            guard_write([0, 1])
            W("T", "S"); W("T", "D")
            mmp(ps[0][0:1, 0:512], ones[:, 0:1], jit1[:, :], c == 0, c == NCK - 1, [0], inc=False)
            mmp(ps[1][0:1, 0:512], ones[:, 0:1], sqb[:, :], c == 0, c == NCK - 1, [1])
        guard_read("V", [0, 1])
        V(lambda e: e.tensor_scalar(out=rowA[:, :], in0=ps[0][0:1, 0:512], scalar1=1.0 / D,
                                    scalar2=None, op0=ALU.mult))
        V(lambda e: e.tensor_scalar(out=rowB[:, :], in0=ps[1][0:1, 0:512], scalar1=1.0 / D,
                                    scalar2=None, op0=ALU.mult))
        mark_read("V", [0, 1])
        V(lambda e: e.tensor_tensor(out=rowC[:, :], in0=rowA[:, :], in1=rowA[:, :], op=ALU.mult))
        V(lambda e: e.tensor_tensor(out=rowB[:, :], in0=rowB[:, :], in1=rowC[:, :], op=ALU.subtract))
        V(lambda e, eps=eps: e.tensor_scalar(out=rowB[:, :], in0=rowB[:, :], scalar1=eps,
                                             scalar2=None, op0=ALU.add))
        W("S", "V")
        S(lambda e: e.activation(out=rowB[:, :], in_=rowB[:, :], func=AF.Sqrt))
        W("V", "S")
        V(lambda e: e.reciprocal(out=rowB[:, :], in_=rowB[:, :]))
        SW("V")
        V(lambda e: e.tensor_tensor(out=rowC[:, :], in0=rowA[:, :], in1=rowB[:, :], op=ALU.mult))
        V(lambda e: e.tensor_scalar(out=rowC[:, :], in0=rowC[:, :], scalar1=-1.0, scalar2=None, op0=ALU.mult))
        W("T", "V")
        guard_write([2, 3])
        mmp(ps[2][:, 0:512], ones[0:1, :], rowB[:, :], True, True, [2], inc=False)
        mmp(ps[3][:, 0:512], ones[0:1, :], rowC[:, :], True, True, [3])
        guard_read("V", [2, 3])
        V(lambda e: e.tensor_copy(out=absb[:, :], in_=ps[2][:, 0:512]))
        V(lambda e: e.tensor_copy(out=bbsb[:, :], in_=ps[3][:, 0:512]))
        mark_read("V", [2, 3])
        for c in range(NCK):
            ld(jit1[:, :], src_d[c * 128:(c + 1) * 128, t0:t0 + TT])
            W("V", "D")
            V(lambda e: e.tensor_tensor(out=tA[:, :], in0=jit1[:, :], in1=absb[:, :], op=ALU.mult))
            V(lambda e, c=c: e.tensor_tensor(out=outbuf[:, c * 512:(c + 1) * 512],
                                             in0=tA[:, :], in1=bbsb[:, :], op=ALU.add))

    def fm_proj(wres, rhsbuf, banks, evac):
        for c in range(NCK):
            bank = banks[c % 2]
            guard_write([bank])
            for k in range(NCK):
                mmp(ps[bank][:, 0:512], wres[:, k * 1024 + c * 128:k * 1024 + (c + 1) * 128],
                    rhsbuf[:, k * 512:(k + 1) * 512], k == 0, k == NCK - 1, [bank])
            evac(c, bank)

    def mk_delta(it):
        for c in range(NCK):
            V(lambda e, c=c: e.tensor_copy(out=shb[:, 1:512], in_=xnB[:, c * 512:c * 512 + 511]))
            V(lambda e, c=c: e.tensor_copy(out=shb[:, 0:1], in_=pcol[:, c:c + 1]))
            V(lambda e, c=c: e.tensor_tensor(out=dlt[:, c * 512:(c + 1) * 512], in0=shb[:, :],
                                             in1=xnB[:, c * 512:(c + 1) * 512], op=ALU.subtract))
            V(lambda e, c=c: e.tensor_copy(out=pcol[:, c:c + 1], in_=xnB[:, c * 512 + 511:c * 512 + 512]))

    def mk_mix(mi):
        for c in range(NCK):
            V(lambda e, c=c, mi=mi: e.tensor_scalar(out=tCb[:, :], in0=dlt[:, c * 512:(c + 1) * 512],
                                                    scalar1=maa[:, mi * 8 + c:mi * 8 + c + 1],
                                                    scalar2=None, op0=ALU.mult))
            V(lambda e, c=c: e.tensor_tensor(out=xmix[:, c * 512:(c + 1) * 512], in0=tCb[:, :],
                                             in1=xnB[:, c * 512:(c + 1) * 512], op=ALU.add))

    def tm_proj(tc, wres, rhsbuf, dstb, banks=(6, 7)):
        guard_write(list(banks))
        for half in range(2):
            for c in range(NCK):
                mmp(ps[banks[half]][:, 0:512], rhsbuf[:, c * 512 + tc * 128:c * 512 + (tc + 1) * 128],
                    wres[:, c * 1024 + half * 512:c * 1024 + (half + 1) * 512], c == 0, c == NCK - 1,
                    [banks[half]], inc=(c == NCK - 1))
        guard_read("V", list(banks))
        for half in range(2):
            V(lambda e, half=half, dstb=dstb: e.tensor_copy(
                out=dstb[:, half * 512:(half + 1) * 512], in_=ps[banks[half]][:, 0:512]))
        mark_read("V", list(banks))

    kchk = ktm[0:64, 0:1024]
    vchk = vtm[0:64, 0:1024]
    kwkH = kwkb[0:64, 0:1024]

    def scan_chunk():
        W("T", "V")
        guard_write([1, 5])
        for h in range(HD):
            bank = 1 if h < 8 else 5
            mmp(ps[bank][0:64, (h % 8) * 64:(h % 8) * 64 + 64],
                kwkH[:, h * 64:(h + 1) * 64],
                vchk[:, h * 64:(h + 1) * 64],
                True, True, [bank], inc=(h % 8 == 7))
        guard_read("V", [1, 5])
        V(lambda e: e.tensor_tensor(out=state_H[:, :], in0=state_H[:, :], in1=wsrepH[:, :], op=ALU.mult))
        V(lambda e: e.tensor_tensor(out=state_H[:, 0:512], in0=state_H[:, 0:512],
                                    in1=ps[1][0:64, 0:512], op=ALU.add))
        V(lambda e: e.tensor_tensor(out=state_H[:, 512:1024], in0=state_H[:, 512:1024],
                                    in1=ps[5][0:64, 0:512], op=ALU.add))
        mark_read("V", [1, 5])

    load_w(wA, wkT)
    load_w(wB, wvT)
    TOTB = 7
    V(lambda e: e.memset(rowg[:, :], 0.0))
    V(lambda e: e.memset(cbuf[0:1, 0:512], 0.0))
    for it in range(NT):
        t0 = it * TT
        ln_stream(x_d, t0, xnA, EPS_LN)
        W("G", "V")
        for c in range(NCK):
            dma("G", xn_s[c * 128:(c + 1) * 128, t0:t0 + TT], xnA[:, c * 512:(c + 1) * 512])
        W("T", "V")
        for tc in range(4):
            kb = [4, 5]; vb = [6, 2]
            guard_write(kb + vb)
            for half in range(2):
                for c in range(NCK):
                    mmp(ps[kb[half]][:, 0:512], xnA[:, c * 512 + tc * 128:c * 512 + (tc + 1) * 128],
                        wA[:, c * 1024 + half * 512:c * 1024 + (half + 1) * 512], c == 0, c == NCK - 1,
                        [kb[half]], inc=(c == NCK - 1))
            for half in range(2):
                for c in range(NCK):
                    mmp(ps[vb[half]][:, 0:512], xnA[:, c * 512 + tc * 128:c * 512 + (tc + 1) * 128],
                        wB[:, c * 1024 + half * 512:c * 1024 + (half + 1) * 512], c == 0, c == NCK - 1,
                        [vb[half]], inc=(c == NCK - 1))
            off = float(t0 + tc * 128)
            V(lambda e, off=off: e.tensor_scalar(out=invcr[:, :], in0=lrow[:, :], scalar1=-off,
                                                 scalar2=None, op0=ALU.mult))
            V(lambda e: e.tensor_tensor(out=invcr[:, :], in0=invcr[:, :], in1=lrowH[:, :], op=ALU.add))
            V(lambda e: e.tensor_scalar(out=invcr[:, :], in0=invcr[:, :], scalar1=23.03,
                                        scalar2=None, op0=ALU.min))
            W("S", "V")
            S(lambda e: e.activation(out=invcr[:, :], in_=invcr[:, :], func=AF.Exp))
            guard_write([3, 0])
            W("T", "S")
            mmp(ps[3][:, 0:512], ones[0:1, :], invcr[:, 0:512], True, True, [3], inc=False)
            mmp(ps[0][:, 0:512], ones[0:1, :], invcr[:, 512:1024], True, True, [0])
            guard_read("V", [3, 0])
            V(lambda e: e.tensor_tensor(out=wtmp[:, 0:512], in0=invdt[:, 0:512],
                                        in1=ps[3][:, 0:512], op=ALU.mult))
            V(lambda e: e.tensor_tensor(out=wtmp[:, 512:1024], in0=invdt[:, 512:1024],
                                        in1=ps[0][:, 0:512], op=ALU.mult))
            mark_read("V", [3, 0])
            V(lambda e: e.tensor_scalar(out=wtmp[:, 0:1024], in0=wtmp[:, 0:1024], scalar1=1e10,
                                        scalar2=None, op0=ALU.min))
            guard_read("V", kb + vb)
            for half in range(2):
                V(lambda e, half=half, kb=kb: e.tensor_tensor(
                    out=wtmp[:, 1024 + half * 512:1024 + (half + 1) * 512], in0=ps[kb[half]][:, 0:512],
                    in1=wtmp[:, half * 512:(half + 1) * 512], op=ALU.mult))
                V(lambda e, half=half, vb=vb: e.tensor_tensor(
                    out=trm[:, half * 512:(half + 1) * 512],
                    in0=wtmp[:, 1024 + half * 512:1024 + (half + 1) * 512],
                    in1=ps[vb[half]][:, 0:512], op=ALU.mult))
            mark_read("V", kb + vb)
            W("G", "V")
            dma("G", terms_s[t0 + tc * 128:t0 + (tc + 1) * 128, :], trm[:, :])
            W("T", "V")
            if tc == 0:
                guard_write([TOTB, 1])
            mmp(ps[TOTB][0:1, 0:512], onesb[:, 0:1], trm[:, 0:512],
                tc == 0, tc == 3, [TOTB], inc=False)
            mmp(ps[1][0:1, 0:512], onesb[:, 0:1], trm[:, 512:1024],
                tc == 0, tc == 3, [1], inc=True)
        guard_read("V", [TOTB, 1])
        V(lambda e: e.tensor_tensor(out=rowg[:, 0:512], in0=rowg[:, 0:512],
                                    in1=ps[TOTB][0:1, 0:512], op=ALU.add))
        V(lambda e: e.tensor_tensor(out=cbuf[0:1, 0:512], in0=cbuf[0:1, 0:512],
                                    in1=ps[1][0:1, 0:512], op=ALU.add))
        mark_read("V", [TOTB, 1])

    gsnap("ph1")
    V(lambda e: e.tensor_scalar(out=rowg[:, 0:512], in0=rowg[:, 0:512], scalar1=msr[0:1, 0:1],
                                scalar2=None, op0=ALU.mult))
    V(lambda e: e.tensor_scalar(out=tA[0:1, 0:512], in0=cbuf[0:1, 0:512], scalar1=msr[0:1, 0:1],
                                scalar2=None, op0=ALU.mult))
    W("G", "V")
    dma("G", cc1i[0:1, 0:512], rowg[:, 0:512])
    dma("G", cc1i[0:1, 512:1024], tA[0:1, 0:512])
    CC(lambda e: e.collective_compute("AllReduce", ALU.add, replica_groups=rg,
                                      ins=[cc1i.ap().opt()], outs=[cc1o.ap().opt()]))
    E("G", lambda e: e.dma_start(out=ptmp[:, :], in_=ap3(cc1o, 0, [[1, 128], [128, 8]]),
                                 allow_slow_non_contiguous=True), 16)
    W("V", "G")
    V(lambda e: e.tensor_scalar(out=carry0[:, :], in0=ptmp[:, :], scalar1=msr[:, 1:2],
                                scalar2=None, op0=ALU.mult))

    for hc in range(32):
        ld(stg[:, 0:1024], w1N[hc * 128:(hc + 1) * 128, :])
        W("S", "D")
        S(lambda e: e.activation(out=stg[:, 0:1024], in_=stg[:, 0:1024], func=AF.Abs))
        W("V", "S")
        V(lambda e, hc=hc: e.reduce_sum(out=alpha1[:, hc:hc + 1], in_=stg[:, 0:1024], axis=AX.X))
    V(lambda e: e.tensor_scalar(out=alpha1[:, :], in0=alpha1[:, :], scalar1=1.0 / D,
                                scalar2=None, op0=ALU.mult))
    for c in range(NCK):
        ld(stg[:, 0:4096], w2N[c * 128:(c + 1) * 128, :])
        W("S", "D")
        S(lambda e: e.activation(out=stg[:, 0:4096], in_=stg[:, 0:4096], func=AF.Abs))
        W("V", "S")
        V(lambda e, c=c: e.reduce_sum(out=alpha2[:, c:c + 1], in_=stg[:, 0:4096], axis=AX.X))
    V(lambda e: e.tensor_scalar(out=alpha2[:, :], in0=alpha2[:, :], scalar1=1.0 / 4096,
                                scalar2=None, op0=ALU.mult))
    if a_dbg is not None:
        W("G", "V")
        dma("G", a_dbg[:, 0:32], alpha1[:, :])
        dma("G", a_dbg[:, 32:40], alpha2[:, :])
        dma("G", a_dbg[:, 40:72][:, 0:24] if False else a_dbg[:, 40:64], b1c[:, 0:24])
    for c in range(NCK):
        ld(stg[:, 0:4096], w1T[c * 128:(c + 1) * 128, :])
        W("S", "D")
        for q in range(4):
            S(lambda e, q=q: e.activation(out=ysb[:, q * 1024:(q + 1) * 1024],
                                          in_=stg[:, q * 1024:(q + 1) * 1024], func=AF.Sign))
        W("G", "S")
        dma("G", sw1_s[c * 128:(c + 1) * 128, :], ysb[:, 0:4096])
    for hc in range(32):
        ld(stg[:, 0:1024], w2T[hc * 128:(hc + 1) * 128, :])
        W("S", "D")
        S(lambda e: e.activation(out=ysb[:, 0:1024], in_=stg[:, 0:1024], func=AF.Sign))
        W("G", "S")
        dma("G", sw2_s[hc * 128:(hc + 1) * 128, :], ysb[:, 0:1024])
    gsnap("sw")
    for it in range(NT):
        t0 = it * TT
        W("D", "G", gprod["ph1"])
        for c in range(NCK):
            for e, v in slotrd.get(("xna", c), {}).items():
                W("D", e, v)
        dma_s(xnA[:, 0:4096],
              ap3(xn_s, t0, [[TLOC, 128], [128 * TLOC, 8], [1, TT]]), "X")
        W("T", "X")
        for hc in range(32):
            half = hc % 2
            W("D", "G", gprod["sw"])
            for c in range(NCK):
                for e, v in slotrd.get(("wstr", half * 8 + c), {}).items():
                    W("D", e, v)
            dma_s(wStr[:, half * 1024:half * 1024 + 1024],
                  ap3(sw1_s, hc * 128, [[4096, 128], [128 * 4096, 8], [1, 128]]),
                  "H%d" % half)
            W("T", "H%d" % half)
            bank = 4 + half
            guard_write([bank])
            for c in range(NCK):
                mmp(ps[bank][:, 0:512], wStr[:, half * 1024 + c * 128:half * 1024 + (c + 1) * 128],
                    xnA[:, c * 512:(c + 1) * 512], c == 0, c == NCK - 1, [bank])
                mark_slot(("wstr", half * 8 + c))
            if hc == 31:
                for c in range(NCK):
                    mark_slot(("xna", c))
            guard_read("S", [bank])
            hbuf = wA if hc < 16 else wB
            hoff = (hc % 16) * 512
            S(lambda e, hc=hc, bank=bank, hbuf=hbuf, hoff=hoff: e.activation(
                out=hbuf[:, hoff:hoff + 512], in_=ps[bank][:, 0:512], func=AF.Gelu,
                bias=b1c[:, hc:hc + 1], scale=alpha1[:, hc:hc + 1]))
            mark_read("S", [bank])
            if z_dbg is not None:
                guard_read("V", [bank])
                V(lambda e, bank=bank: e.tensor_copy(out=tA[:, :], in_=ps[bank][:, 0:512]))
                mark_read("V", [bank])
                W("G", "V")
                dma("G", z_dbg[hc * 128:(hc + 1) * 128, t0:t0 + TT], tA[:, :])
        if h_dbg is not None:
            W("G", "S")
            for hc in range(32):
                hbuf = wA if hc < 16 else wB
                hoff = (hc % 16) * 512
                dma("G", h_dbg[hc * 128:(hc + 1) * 128, t0:t0 + TT], hbuf[:, hoff:hoff + 512])
        W("T", "S")
        for c in range(NCK):
            bank = 6 + (c % 2)
            guard_write([bank])
            for hc in range(32):
                slot = hc % 8
                ldw(wStr[:, slot * 128:(slot + 1) * 128],
                    sw2_s[hc * 128:(hc + 1) * 128, c * 128:(c + 1) * 128],
                    ("wstr", slot), "S%d" % slot, prod="sw")
                W("T", "S%d" % slot)
                hbuf = wA if hc < 16 else wB
                hoff = (hc % 16) * 512
                mmp(ps[bank][:, 0:512], wStr[:, slot * 128:(slot + 1) * 128],
                    hbuf[:, hoff:hoff + 512], hc == 0, hc == 31, [bank])
                mark_slot(("wstr", slot))
            if o_dbg is not None:
                guard_read("V", [bank])
                V(lambda e, bank=bank: e.tensor_copy(out=tA[:, :], in_=ps[bank][:, 0:512]))
                W("G", "V")
                dma("G", o_dbg[c * 128:(c + 1) * 128, t0:t0 + TT], tA[:, :])
            guard_read("S", [bank])
            S(lambda e, c=c, bank=bank: e.activation(
                out=gsb[:, c * 512:(c + 1) * 512], in_=ps[bank][:, 0:512], func=AF.Silu,
                bias=b2c[:, c:c + 1], scale=alpha2[:, c:c + 1]))
            mark_read("S", [bank])
            if o_dbg is not None:
                W("V", "S"); W("V", "G")
                V(lambda e, bank=bank: e.tensor_copy(out=tA[:, :], in_=ps[bank][:, 0:512]))
                mark_read("V", [bank])
                W("G", "V")
                dma("G", o2_dbg[c * 128:(c + 1) * 128, t0:t0 + TT], tA[:, :])
        W("G", "S")
        for c in range(NCK):
            dma("G", g_s[c * 128:(c + 1) * 128, t0:t0 + TT], gsb[:, c * 512:(c + 1) * 512])

    load_w(wA, wrT)
    load_w(wB, woT)
    W("V", "G")
    V(lambda e: e.tensor_copy(out=crun[:, :], in_=carry0[:, :]))
    for it in range(NT):
        t0 = it * TT
        V(lambda e, t0=t0: e.tensor_scalar(out=ccol[:, :], in0=lfm[:, :], scalar1=float(t0),
                                           scalar2=None, op0=ALU.mult))
        V(lambda e: e.tensor_tensor(out=ccol[:, :], in0=ccol[:, :], in1=lfmH[:, :], op=ALU.add))
        V(lambda e, t0=t0: e.tensor_scalar(out=ccol2[:, :], in0=lfm[:, :], scalar1=float(t0 + 256),
                                           scalar2=None, op0=ALU.mult))
        V(lambda e: e.tensor_tensor(out=ccol2[:, :], in0=ccol2[:, :], in1=lfmH[:, :], op=ALU.add))
        W("S", "V")
        S(lambda e: e.activation(out=ccol[:, :], in_=ccol[:, :], func=AF.Exp))
        S(lambda e: e.activation(out=ccol2[:, :], in_=ccol2[:, :], func=AF.Exp))
        W("V", "S")
        for c in range(NCK):
            ld(xnA[:, c * 512:(c + 1) * 512], xn_s[c * 128:(c + 1) * 128, t0:t0 + TT])
        W("T", "D")
        def evac_r(c, bank):
            guard_read("S", [bank])
            S(lambda e, c=c, bank=bank: e.activation(out=rsb[:, c * 512:(c + 1) * 512],
                                                     in_=ps[bank][:, 0:512], func=AF.Sigmoid))
            mark_read("S", [bank])
        fm_proj(wA, xnA, [4, 5], evac_r)
        W("V", "S")
        for tc in range(4):
            ldw(trm[:, :], terms_s[t0 + tc * 128:t0 + (tc + 1) * 128, :], ("trm",),
                "T2", prod="ph1")
            W("T", "T2")
            guard_write([6, 7])
            for c in range(NCK):
                bank = 6 + c // 4
                mmp(ps[bank][:, (c % 4) * 128:(c % 4 + 1) * 128], trm[:, c * 128:(c + 1) * 128],
                    ltri[:, :], True, True, [bank], inc=(c == NCK - 1))
            mark_slot(("trm",))
            guard_read("V", [6, 7])
            scol = ccol if tc < 2 else ccol2
            dso = (tc % 2) * 128
            for c in range(NCK):
                bank = 6 + c // 4
                off = (c % 4) * 128
                V(lambda e, bank=bank, off=off, c=c: e.tensor_scalar(
                    out=tA[:, 0:128], in0=ps[bank][:, off:off + 128],
                    scalar1=crun[:, c:c + 1], scalar2=None, op0=ALU.add))
                V(lambda e, c=c, scol=scol: e.tensor_scalar(
                    out=tA[:, 128:256], in0=tA[:, 0:128],
                    scalar1=scol[:, c:c + 1], scalar2=None, op0=ALU.mult))
                V(lambda e, c=c, dso=dso: e.tensor_tensor(
                    out=tA[:, 256:384], in0=tA[:, 128:256],
                    in1=d512[:, c * 256 + dso:c * 256 + dso + 128], op=ALU.mult))
                V(lambda e, c=c, tc=tc: e.tensor_tensor(
                    out=ysb[:, c * 512 + tc * 128:c * 512 + (tc + 1) * 128],
                    in0=tA[:, 256:384], in1=rsb[:, c * 512 + tc * 128:c * 512 + (tc + 1) * 128],
                    op=ALU.mult))
                V(lambda e, c=c, bank=bank, off=off: e.tensor_tensor(
                    out=crun[:, c:c + 1], in0=crun[:, c:c + 1],
                    in1=ps[bank][:, off + 127:off + 128], op=ALU.add))
            mark_read("V", [6, 7])
        if b_dbg is not None and it == 0:
            V(lambda e: e.tensor_copy(out=cbuf[:, 0:8], in_=ccol[:, :]))
            V(lambda e: e.tensor_copy(out=cbuf[:, 8:16], in_=ccol2[:, :]))
            V(lambda e: e.tensor_copy(out=cbuf[:, 16:24], in_=crun[:, :]))
            V(lambda e: e.tensor_copy(out=cbuf[:, 24:32], in_=carry0[:, :]))
            V(lambda e: e.tensor_copy(out=cbuf[:, 32:40], in_=lfm[:, :]))
            V(lambda e: e.tensor_copy(out=cbuf[:, 40:48], in_=lfmH[:, :]))
            V(lambda e: e.tensor_copy(out=cbuf[:, 48:56], in_=d512[:, 0:8]))
            W("G", "V")
            dma("G", b_dbg[:, :], cbuf[:, 0:64])
            for c in range(NCK):
                dma("G", y0_dbg[c * 128:(c + 1) * 128, :], ysb[:, c * 512:(c + 1) * 512])
        for c in range(NCK):
            W("S", "V"); W("S", "T")
            S(lambda e, c=c: e.activation(out=sqb[:, :], in_=ysb[:, c * 512:(c + 1) * 512], func=AF.Square))
            guard_write([0])
            W("T", "S")
            mmp(ps[0][0:1, 0:512], ones[:, 0:1], sqb[:, :], c == 0, c == NCK - 1, [0],
                inc=(c == NCK - 1))
        guard_read("V", [0])
        V(lambda e: e.tensor_scalar(out=rowB[:, :], in0=ps[0][0:1, 0:512], scalar1=1.0 / D,
                                    scalar2=EPS_RMS, op0=ALU.mult, op1=ALU.add))
        mark_read("V", [0])
        W("S", "V")
        S(lambda e: e.activation(out=rowB[:, :], in_=rowB[:, :], func=AF.Sqrt))
        W("V", "S")
        V(lambda e: e.reciprocal(out=rowB[:, :], in_=rowB[:, :]))
        SW("V")
        W("T", "V")
        guard_write([2])
        mmp(ps[2][:, 0:512], ones[0:1, :], rowB[:, :], True, True, [2])
        guard_read("V", [2])
        V(lambda e: e.tensor_copy(out=absbB[:, :], in_=ps[2][:, 0:512]))
        mark_read("V", [2])
        for c in range(NCK):
            V(lambda e, c=c: e.tensor_scalar(out=tBb[:, :], in0=ysb[:, c * 512:(c + 1) * 512],
                                             scalar1=ln0w[:, c:c + 1], scalar2=None, op0=ALU.mult))
            V(lambda e, c=c: e.tensor_tensor(out=ysb[:, c * 512:(c + 1) * 512],
                                             in0=tBb[:, :], in1=absbB[:, :], op=ALU.mult))
        W("T", "V")
        for c in range(NCK):
            bank = 4 + (c % 2)
            guard_write([bank])
            for k in range(NCK):
                mmp(ps[bank][:, 0:512], wB[:, k * 1024 + c * 128:k * 1024 + (c + 1) * 128],
                    ysb[:, k * 512:(k + 1) * 512], k == 0, k == NCK - 1, [bank])
            ld(jit1[:, :], x_d[c * 128:(c + 1) * 128, t0:t0 + TT])
            ld(jit2[:, :], g_s[c * 128:(c + 1) * 128, t0:t0 + TT])
            guard_read("V", [bank])
            W("V", "D")
            V(lambda e, bank=bank: e.tensor_tensor(out=tA[:, :], in0=ps[bank][:, 0:512],
                                                   in1=jit2[:, :], op=ALU.mult))
            mark_read("V", [bank])
            V(lambda e: e.tensor_tensor(out=sqb[:, :], in0=tA[:, :], in1=jit1[:, :], op=ALU.add))
            W("G", "V")
            dma("G", x1_s[c * 128:(c + 1) * 128, t0:t0 + TT], sqb[:, :])
        ln_stream(x1_s, t0, xnB, EPS_LN)
        W("G", "V")
        for c in range(NCK):
            dma("G", xn1_s[c * 128:(c + 1) * 128, t0:t0 + TT], xnB[:, c * 512:(c + 1) * 512])
        if it == NT - 1:
            for c in range(NCK):
                V(lambda e, c=c: e.tensor_copy(out=brow[:, c:c + 1],
                                               in_=xnB[:, c * 512 + 511:c * 512 + 512]))

    gsnap("ph3")
    V(lambda e: e.tensor_scalar(out=cbuf[:, 0:8], in0=brow[:, :], scalar1=msr[:, 0:1],
                                scalar2=None, op0=ALU.mult))
    W("G", "V")
    dma("G", cc2i[:, :], cbuf[:, 0:8])
    CC(lambda e: e.collective_compute("AllReduce", ALU.add, replica_groups=rg,
                                      ins=[cc2i.ap().opt()], outs=[cc2o.ap().opt()]))
    dma("G", cbuf[:, 8:16], cc2o[:, :])
    W("V", "G")
    V(lambda e: e.tensor_scalar(out=cbuf[:, 16:24], in0=cbuf[:, 8:16], scalar1=msr[:, 1:2],
                                scalar2=None, op0=ALU.mult))
    V(lambda e: e.tensor_copy(out=pcol[:, :], in_=cbuf[:, 16:24]))

    load_w(wA, wr1T)
    load_w(wB, wk1T)
    for it in range(NT):
        t0 = it * TT
        for c in range(NCK):
            ld(xnB[:, c * 512:(c + 1) * 512], xn1_s[c * 128:(c + 1) * 128, t0:t0 + TT])
        W("V", "D")
        mk_delta(it)
        mk_mix(2)
        W("T", "V")
        def evac_cp(dstb):
            def f(c, bank):
                guard_read("V", [bank])
                V(lambda e, c=c, bank=bank, dstb=dstb: e.tensor_copy(
                    out=dstb[:, c * 512:(c + 1) * 512], in_=ps[bank][:, 0:512]))
                mark_read("V", [bank])
            return f
        fm_proj(wA, xmix, [4, 5], evac_cp(ysb))
        W("G", "V")
        for c in range(NCK):
            dma("G", r1_s[c * 128:(c + 1) * 128, t0:t0 + TT], ysb[:, c * 512:(c + 1) * 512])
        mk_mix(0)
        W("T", "V")
        fm_proj(wB, xmix, [4, 5], evac_cp(gsb))
        W("G", "V")
        for c in range(NCK):
            dma("G", kf_s[c * 128:(c + 1) * 128, t0:t0 + TT], gsb[:, c * 512:(c + 1) * 512])
        for tc in range(4):
            tm_proj(tc, wB, xmix, ktm)
            W("G", "V")
            dma("G", kt_s[t0 + tc * 128:t0 + (tc + 1) * 128, :], ktm[:, :])

    load_w(wA, wv1T)
    load_w(wB, wg1T)
    V(lambda e: e.memset(state_H[:, :], 0.0))
    V(lambda e: e.tensor_copy(out=pcol[:, :], in_=cbuf[:, 16:24]))
    for it in range(NT):
        t0 = it * TT
        for c in range(NCK):
            ld(xnB[:, c * 512:(c + 1) * 512], xn1_s[c * 128:(c + 1) * 128, t0:t0 + TT])
        W("V", "D")
        mk_delta(it)
        mk_mix(3)
        W("T", "V")
        def evac_gelu(c, bank):
            guard_read("S", [bank])
            S(lambda e, c=c, bank=bank: e.activation(out=gsb[:, c * 512:(c + 1) * 512],
                                                     in_=ps[bank][:, 0:512], func=AF.Gelu))
            mark_read("S", [bank])
        fm_proj(wB, xmix, [4, 5], evac_gelu)
        W("G", "S")
        for c in range(NCK):
            dma("G", g1_s[c * 128:(c + 1) * 128, t0:t0 + TT], gsb[:, c * 512:(c + 1) * 512])
        mk_mix(1)
        W("T", "V")
        for tc in range(4):
            tm_proj(tc, wA, xmix, vtm)
            W("G", "V")
            dma("G", vt_s[t0 + tc * 128:t0 + (tc + 1) * 128, :], vtm[:, :])
            for par in range(2):
                tj = t0 + tc * 128 + par * 64
                ld(kchk, kt_s[tj:tj + 64, :])
                ld(vchk, vt_s[tj:tj + 64, :])
                W("V", "D")
                V(lambda e: e.tensor_tensor(out=kwkH, in0=kchk, in1=wkrep[0:64, :], op=ALU.mult))
                scan_chunk()

    V(lambda e: e.tensor_scalar(out=stg[0:64, 0:1024], in0=state_H[:, :], scalar1=msr[0:64, 0:1],
                                scalar2=None, op0=ALU.mult))
    W("G", "V")
    dma("G", cc3i[0:64, 0:1024], stg[0:64, 0:1024])
    CC(lambda e: e.collective_compute("AllReduce", ALU.add, replica_groups=rg,
                                      ins=[cc3i.ap().opt()], outs=[cc3o.ap().opt()]))
    dma("G", stg[0:64, 1024:2048], cc3o[0:64, 0:1024])
    W("V", "G")
    V(lambda e: e.tensor_scalar(out=state_H[:, :], in0=stg[0:64, 1024:2048], scalar1=msr[0:64, 1:2],
                                scalar2=None, op0=ALU.mult))

    def gamma_pass(wsrc, nchunks, width, gdst):
        V(lambda e: e.memset(rowg[:, :], 0.0))
        for c in range(nchunks):
            ld(stg[:, 0:width], wsrc[c * 128:(c + 1) * 128, :])
            W("S", "D"); W("S", "T"); W("S", "V")
            S(lambda e, width=width: e.activation(out=stg[:, 0:width], in_=stg[:, 0:width], func=AF.Abs))
            W("T", "S")
            for eb in range(width // 512):
                bank = eb % 2
                guard_write([bank])
                mmp(ps[bank][0:1, 0:512], ones[:, 0:1], stg[:, eb * 512:(eb + 1) * 512],
                    True, True, [bank], inc=True)
                guard_read("V", [bank])
                V(lambda e, bank=bank: e.tensor_tensor(
                    out=rowg[:, 0:512], in0=rowg[:, 0:512],
                    in1=ps[bank][0:1, 0:512], op=ALU.add))
                mark_read("V", [bank])
        V(lambda e: e.reduce_sum(out=rowA[:, 0:1], in_=rowg[:, 0:512], axis=AX.X))
        V(lambda e: e.tensor_scalar(out=rowA[:, 0:1], in0=rowA[:, 0:1], scalar1=1.0 / (D * 4096),
                                    scalar2=1e-6, op0=ALU.mult, op1=ALU.add))
        V(lambda e: e.tensor_scalar(out=rowA[:, 1:2], in0=rowA[:, 0:1], scalar1=0.5,
                                    scalar2=None, op0=ALU.mult))
        W("T", "V")
        guard_write([2])
        mmp(ps[2][:, 0:1], ones[0:1, :], rowA[:, 1:2], True, True, [2])
        guard_read("V", [2])
        V(lambda e, gdst=gdst: e.tensor_copy(out=gdst[:, :], in_=ps[2][:, 0:1]))
        mark_read("V", [2])

    def quant_pass(wsrc, nchunks, width, gc, dst_s):
        for c in range(nchunks):
            ld(stg[:, 0:width], wsrc[c * 128:(c + 1) * 128, :])
            W("S", "D"); W("S", "V"); W("S", "G")
            for q in range(max(1, width // 1024)):
                S(lambda e, q=q: e.activation(out=ysb[:, q * 1024:(q + 1) * 1024],
                                              in_=stg[:, q * 1024:(q + 1) * 1024], func=AF.Sign))
            S(lambda e, width=width: e.activation(out=stg[:, 0:width], in_=stg[:, 0:width], func=AF.Abs))
            W("V", "S")
            V(lambda e, width=width, gc=gc: e.tensor_scalar(out=stg[:, 0:width], in0=stg[:, 0:width],
                                                            scalar1=gc[:, 0:1], scalar2=None,
                                                            op0=ALU.is_ge))
            V(lambda e, width=width: e.tensor_copy(out=gsb[:, 0:width], in_=stg[:, 0:width]))
            V(lambda e, width=width: e.tensor_tensor(out=ysb[:, 0:width], in0=ysb[:, 0:width],
                                                     in1=gsb[:, 0:width], op=ALU.mult))
            W("G", "V")
            dma("G", dst_s[c * 128:(c + 1) * 128, :], ysb[:, 0:width])

    gamma_pass(wfcT, NCK, 4096, gcol)
    gamma_pass(wpjT, 32, 1024, gcol2)
    SW("V")
    quant_pass(wfcT, NCK, 4096, gcol, wqf_s)
    quant_pass(wpjT, 32, 1024, gcol2, wqp_s)
    gsnap("quant")
    for (target, lncol) in [(cs1, lnfwB), (cs2, lnfbB)]:
        for c in range(NCK):
            bank = c % 2
            guard_write([bank])
            for hc in range(32):
                slot = hc % 8
                ldw(wStr[:, slot * 128:(slot + 1) * 128],
                    wqp_s[hc * 128:(hc + 1) * 128, c * 128:(c + 1) * 128],
                    ("wstr", slot), "S%d" % slot, prod="quant")
                W("T", "S%d" % slot)
                mmp(ps[bank][:, 0:1], wStr[:, slot * 128:(slot + 1) * 128],
                    lncol[:, hc:hc + 1], hc == 0, hc == 31, [bank], inc=True)
                mark_slot(("wstr", slot))
            guard_read("V", [bank])
            V(lambda e, c=c, bank=bank, target=target: e.tensor_copy(out=target[:, c:c + 1],
                                                                     in_=ps[bank][:, 0:1]))
            mark_read("V", [bank])
    for it in range(NT):
        t0 = it * TT
        W("D", "G", gprod["ph3"])
        for c in range(NCK):
            for e, v in slotrd.get(("xnb", c), {}).items():
                W("D", e, v)
        dma_s(xnB[:, 0:4096],
              ap3(xn1_s, t0, [[TLOC, 128], [128 * TLOC, 8], [1, TT]]), "X")
        W("T", "X")
        for hc in range(32):
            half = hc % 2
            W("D", "G", gprod["quant"])
            for c in range(NCK):
                for e, v in slotrd.get(("wstr", half * 8 + c), {}).items():
                    W("D", e, v)
            dma_s(wStr[:, half * 1024:half * 1024 + 1024],
                  ap3(wqf_s, hc * 128, [[4096, 128], [128 * 4096, 8], [1, 128]]),
                  "H%d" % half)
            W("T", "H%d" % half)
            bank = 4 + half
            guard_write([bank])
            for c in range(NCK):
                mmp(ps[bank][:, 0:512], wStr[:, half * 1024 + c * 128:half * 1024 + (c + 1) * 128],
                    xnB[:, c * 512:(c + 1) * 512], c == 0, c == NCK - 1, [bank])
                mark_slot(("wstr", half * 8 + c))
            if hc == 31:
                for c in range(NCK):
                    mark_slot(("xnb", c))
            guard_read("S", [bank])
            hbuf = wA if hc < 16 else wB
            hoff = (hc % 16) * 512
            S(lambda e, bank=bank, hbuf=hbuf, hoff=hoff: e.activation(
                out=hbuf[:, hoff:hoff + 512], in_=ps[bank][:, 0:512], func=AF.Gelu))
            mark_read("S", [bank])
        for hc in range(32):
            hbuf = wA if hc < 16 else wB
            hoff = (hc % 16) * 512
            W("S", "T")
            S(lambda e, hbuf=hbuf, hoff=hoff: e.activation(out=sqb[:, :], in_=hbuf[:, hoff:hoff + 512],
                                                           func=AF.Square))
            guard_write([0, 1])
            W("T", "S")
            mmp(ps[0][0:1, 0:512], onesb[:, 0:1], hbuf[:, hoff:hoff + 512], hc == 0, hc == 31,
                [0], inc=False)
            mmp(ps[1][0:1, 0:512], ones[:, 0:1], sqb[:, :], hc == 0, hc == 31, [1])
        guard_read("V", [0, 1])
        V(lambda e: e.tensor_scalar(out=rowA[:, :], in0=ps[0][0:1, 0:512], scalar1=1.0 / 4096,
                                    scalar2=None, op0=ALU.mult))
        V(lambda e: e.tensor_scalar(out=rowB[:, :], in0=ps[1][0:1, 0:512], scalar1=1.0 / 4096,
                                    scalar2=None, op0=ALU.mult))
        mark_read("V", [0, 1])
        V(lambda e: e.tensor_tensor(out=rowC[:, :], in0=rowA[:, :], in1=rowA[:, :], op=ALU.mult))
        V(lambda e: e.tensor_tensor(out=rowB[:, :], in0=rowB[:, :], in1=rowC[:, :], op=ALU.subtract))
        V(lambda e: e.tensor_scalar(out=rowB[:, :], in0=rowB[:, :], scalar1=EPS_LNF,
                                    scalar2=None, op0=ALU.add))
        W("S", "V")
        S(lambda e: e.activation(out=rowB[:, :], in_=rowB[:, :], func=AF.Sqrt))
        W("V", "S")
        V(lambda e: e.reciprocal(out=rowB[:, :], in_=rowB[:, :]))
        SW("V")
        V(lambda e: e.tensor_tensor(out=rowC[:, :], in0=rowA[:, :], in1=rowB[:, :], op=ALU.mult))
        V(lambda e: e.tensor_scalar(out=rowC[:, :], in0=rowC[:, :], scalar1=-1.0,
                                    scalar2=None, op0=ALU.mult))
        W("T", "V")
        guard_write([2, 3])
        mmp(ps[2][:, 0:512], ones[0:1, :], rowB[:, :], True, True, [2], inc=False)
        mmp(ps[3][:, 0:512], ones[0:1, :], rowC[:, :], True, True, [3])
        guard_read("V", [2, 3])
        V(lambda e: e.tensor_copy(out=absb[:, :], in_=ps[2][:, 0:512]))
        V(lambda e: e.tensor_copy(out=bbsb[:, :], in_=ps[3][:, 0:512]))
        mark_read("V", [2, 3])
        for hc in range(32):
            hbuf = wA if hc < 16 else wB
            hoff = (hc % 16) * 512
            W("V", "T")
            V(lambda e, hc=hc, hbuf=hbuf, hoff=hoff: e.tensor_scalar(
                out=hbuf[:, hoff:hoff + 512], in0=hbuf[:, hoff:hoff + 512],
                scalar1=lnfw[:, hc:hc + 1], scalar2=None, op0=ALU.mult))
        W("T", "V")
        for c in range(NCK):
            bank = 6 + (c % 2)
            guard_write([bank])
            for hc in range(32):
                slot = hc % 8
                ldw(wStr[:, slot * 128:(slot + 1) * 128],
                    wqp_s[hc * 128:(hc + 1) * 128, c * 128:(c + 1) * 128],
                    ("wstr", slot), "S%d" % slot, prod="quant")
                W("T", "S%d" % slot)
                hbuf = wA if hc < 16 else wB
                hoff = (hc % 16) * 512
                mmp(ps[bank][:, 0:512], wStr[:, slot * 128:(slot + 1) * 128],
                    hbuf[:, hoff:hoff + 512], hc == 0, hc == 31, [bank])
                mark_slot(("wstr", slot))
            guard_read("V", [bank])
            V(lambda e, bank=bank: e.tensor_tensor(out=tA[:, :], in0=ps[bank][:, 0:512],
                                                   in1=absb[:, :], op=ALU.mult))
            mark_read("V", [bank])
            V(lambda e, c=c: e.tensor_scalar(out=sqb[:, :], in0=bbsb[:, :], scalar1=cs1[:, c:c + 1],
                                             scalar2=cs2[:, c:c + 1], op0=ALU.mult, op1=ALU.add))
            V(lambda e: e.tensor_tensor(out=tA[:, :], in0=tA[:, :], in1=sqb[:, :], op=ALU.add))
            W("S", "V")
            S(lambda e, c=c: e.activation(out=gsb[:, c * 512:(c + 1) * 512], in_=tA[:, :], func=AF.Silu))
        W("G", "S")
        for c in range(NCK):
            dma("G", g2_s[c * 128:(c + 1) * 128, t0:t0 + TT], gsb[:, c * 512:(c + 1) * 512])

    load_w(wA, wo1T)
    kH = dlt[0:64, 0:4096]
    rH = xmix[0:64, 0:4096]
    rwbH = xnB[0:64, 0:4096]
    frepR = gsb[0:64, 0:4096]
    for h in range(HD):
        W("S", "V"); W("S", "G")
        S(lambda e, h=h: e.activation(out=frepR[:, h * 256:(h + 1) * 256], in_=tmod[0:64, 0:256],
                                      func=AF.Exp, scale=lnwH[:, h:h + 1]))
    NHT = 2 * NT
    for ht in range(NHT):
        t0 = ht * 256
        E("D", lambda e, t0=t0: e.dma_start(
            out=kH, in_=ap3(kf_s, t0, [[TLOC, 64], [64 * TLOC, 16], [1, 256]])), 16)
        E("D", lambda e, t0=t0: e.dma_start(
            out=rH, in_=ap3(r1_s, t0, [[TLOC, 64], [64 * TLOC, 16], [1, 256]])), 16)
        W("V", "D"); W("V", "S")
        V(lambda e: e.tensor_tensor(out=rwbH, in0=rH, in1=frepR, op=ALU.mult))
        for j in range(4):
            tj = t0 + j * 64
            ld(kchk, kt_s[tj:tj + 64, :])
            ld(vchk, vt_s[tj:tj + 64, :])
            W("V", "D")
            V(lambda e: e.tensor_tensor(out=kwkH, in0=kchk, in1=wkrep[0:64, :], op=ALU.mult))
            W("T", "V")
            guard_write([2, 3])
            for h in range(HD):
                bank = 2 + h // 8
                mmp(ps[bank][0:64, (h % 8) * 64:(h % 8) * 64 + 64],
                    kH[:, h * 256 + j * 64:h * 256 + j * 64 + 64],
                    rH[:, h * 256 + j * 64:h * 256 + j * 64 + 64],
                    True, True, [bank], inc=(h % 8 == 7))
            guard_read("V", [2, 3])
            V(lambda e: e.tensor_tensor(out=attwt[0:64, 0:512], in0=ps[2][0:64, 0:512],
                                        in1=wmt[0:64, 0:512], op=ALU.mult))
            V(lambda e: e.tensor_tensor(out=attwt[0:64, 512:1024], in0=ps[3][0:64, 0:512],
                                        in1=wmt[0:64, 512:1024], op=ALU.mult))
            mark_read("V", [2, 3])
            V(lambda e: e.tensor_copy(out=stateb_H[:, :], in_=state_H[:, :]))
            W("T", "V")
            guard_write([4])
            for h in range(HD):
                hp = 64 * (h % 2)
                mmp(ps[4][hp:hp + 64, (h // 2) * 64:(h // 2) * 64 + 64],
                    vchk[:, h * 64:(h + 1) * 64],
                    attwt[0:64, h * 64:(h + 1) * 64],
                    True, False, [4], inc=False, tp=(0, hp))
                mmp(ps[4][hp:hp + 64, (h // 2) * 64:(h // 2) * 64 + 64],
                    stateb_H[:, h * 64:(h + 1) * 64],
                    rwbH[:, h * 256 + j * 64:h * 256 + j * 64 + 64],
                    False, True, [4], inc=(h == HD - 1), tp=(0, hp))
            guard_read("V", [4])
            V(lambda e, j=j: e.tensor_copy(
                out=ap3(yL1, j * 64, [[2048, 128], [256, 8], [1, 64]]),
                in_=ap3(ps[4], 0, [[512, 128], [64, 8], [1, 64]])))
            mark_read("V", [4])
            scan_chunk()
        for p in range(NCK):
            W("S", "T"); W("S", "V")
            S(lambda e, p=p: e.activation(out=sqb[:, 0:256], in_=yL1[:, p * 256:(p + 1) * 256],
                                          func=AF.Square))
            guard_write([2, 3])
            W("T", "S"); W("T", "V")
            mmp(ps[2][0:16, 0:256], e16b[:, p * 16:(p + 1) * 16], yL1[:, p * 256:(p + 1) * 256],
                p == 0, p == NCK - 1, [2], inc=False)
            mmp(ps[3][0:16, 0:256], e16[:, p * 16:(p + 1) * 16], sqb[:, 0:256], p == 0, p == NCK - 1, [3])
        guard_read("V", [2, 3])
        V(lambda e: e.tensor_scalar(out=st16[:, 0:256], in0=ps[2][0:16, 0:256], scalar1=1.0 / 64,
                                    scalar2=None, op0=ALU.mult))
        V(lambda e: e.tensor_scalar(out=st16b[:, 0:256], in0=ps[3][0:16, 0:256], scalar1=1.0 / 64,
                                    scalar2=None, op0=ALU.mult))
        mark_read("V", [2, 3])
        V(lambda e: e.tensor_tensor(out=rows16[:, 0:256], in0=st16[:, 0:256], in1=st16[:, 0:256],
                                    op=ALU.mult))
        V(lambda e: e.tensor_tensor(out=st16b[:, 0:256], in0=st16b[:, 0:256], in1=rows16[:, 0:256],
                                    op=ALU.subtract))
        V(lambda e: e.tensor_scalar(out=st16b[:, 0:256], in0=st16b[:, 0:256], scalar1=EPS_GN,
                                    scalar2=None, op0=ALU.add))
        W("S", "V")
        S(lambda e: e.activation(out=st16b[:, 0:256], in_=st16b[:, 0:256], func=AF.Sqrt))
        W("V", "S")
        V(lambda e: e.reciprocal(out=st16b[:, 0:256], in_=st16b[:, 0:256]))
        SW("V")
        V(lambda e: e.tensor_tensor(out=st16[:, 0:256], in0=st16[:, 0:256], in1=st16b[:, 0:256],
                                    op=ALU.mult))
        V(lambda e: e.tensor_scalar(out=st16[:, 0:256], in0=st16[:, 0:256], scalar1=-1.0,
                                    scalar2=None, op0=ALU.mult))
        W("T", "V")
        for p in range(NCK):
            guard_write([2, 3])
            mmp(ps[2][:, 0:256], e16t[:, p * 128:(p + 1) * 128], st16b[0:16, 0:256], True, True,
                [2], inc=False)
            mmp(ps[3][:, 0:256], e16t[:, p * 128:(p + 1) * 128], st16[0:16, 0:256], True, True, [3])
            ld(jit2[:, 0:256], g1_s[p * 128:(p + 1) * 128, t0:t0 + 256])
            guard_read("V", [2, 3])
            W("V", "D")
            V(lambda e, p=p: e.tensor_tensor(out=tA[:, 0:256], in0=yL1[:, p * 256:(p + 1) * 256],
                                             in1=ps[2][:, 0:256], op=ALU.mult))
            V(lambda e: e.tensor_tensor(out=tA[:, 0:256], in0=tA[:, 0:256], in1=ps[3][:, 0:256],
                                        op=ALU.add))
            mark_read("V", [2, 3])
            V(lambda e, p=p: e.tensor_scalar(out=tBb[:, 0:256], in0=tA[:, 0:256],
                                             scalar1=ln1w[:, p:p + 1],
                                             scalar2=ln1b[:, p:p + 1], op0=ALU.mult, op1=ALU.add))
            V(lambda e, p=p: e.tensor_tensor(out=ysb[:, p * 256:(p + 1) * 256], in0=tBb[:, 0:256],
                                             in1=jit2[:, 0:256], op=ALU.mult))
        W("T", "V")
        for c in range(NCK):
            bank = 6 + (c % 2)
            guard_write([bank])
            for k in range(NCK):
                mmp(ps[bank][:, 0:256], wA[:, k * 1024 + c * 128:k * 1024 + (c + 1) * 128],
                    ysb[:, k * 256:(k + 1) * 256], k == 0, k == NCK - 1, [bank])
            ld(jit1[:, 0:256], x1_s[c * 128:(c + 1) * 128, t0:t0 + 256])
            ld(jit2[:, 0:256], g2_s[c * 128:(c + 1) * 128, t0:t0 + 256])
            guard_read("V", [bank])
            W("V", "D")
            V(lambda e, bank=bank: e.tensor_tensor(out=tA[:, 0:256], in0=ps[bank][:, 0:256],
                                                   in1=jit2[:, 0:256], op=ALU.mult))
            mark_read("V", [bank])
            V(lambda e: e.tensor_tensor(out=sqb[:, 0:256], in0=tA[:, 0:256], in1=jit1[:, 0:256],
                                        op=ALU.add))
            W("G", "V")
            dma("G", out_d[c * 128:(c + 1) * 128, t0:t0 + 256], sqb[:, 0:256])

    steps.append(("G", lambda eng, v=cnt["G"]: eng.wait_ge(sem_handles["G"], v)))

    if nsteps is not None:
        nsteps = min(nsteps, len(cnt_hist))
        final = cnt_hist[nsteps - 1]
        steps = steps[:nsteps]
        for e in ("T", "V", "S", "D", "G"):
            for s in ("T", "V", "S", "D", "G", "C"):
                vv = final.get(s, 0)
                if vv > 0:
                    steps.append((e, lambda eng, s2=s, v=vv: eng.wait_ge(sem_handles[s2], v)))
    from contextlib import ExitStack
    with ExitStack() as _stack:
        for _n in ["T", "V", "S", "D", "G", "C"] + STREAMS:
            sem_handles[_n] = _stack.enter_context(nc.semaphore("sem" + _n))
        block = _stack.enter_context(nc.Block())
        @block.tensor
        def _(eng):
            for en, fn in steps:
                if en == "T":
                    fn(eng)
        @block.vector
        def _(eng):
            for en, fn in steps:
                if en == "V":
                    fn(eng)
        @block.scalar
        def _(eng):
            for en, fn in steps:
                if en == "S":
                    fn(eng)
        @block.sync
        def _(eng):
            for en, fn in steps:
                if en == "D":
                    fn(eng)
        @block.gpsimd
        def _(eng):
            for en, fn in steps:
                if en == "G":
                    fn(eng)
    return nc


def _consts():
    i128 = np.arange(128)
    c = {}
    c["c_ones"] = np.ones((128, 128), np.float32)
    c["c_onesb"] = np.ones((128, 128), BF)
    c["c_ident"] = np.eye(128, dtype=np.float32)
    c["c_ltri"] = (np.arange(128)[:, None] <= np.arange(128)[None, :]).astype(BF)
    e16 = np.zeros((128, 128), np.float32)
    for p in range(8):
        for r in range(128):
            e16[r, p * 16 + (2 * p + r // 64)] = 1.0
    c["c_e16"] = e16
    e16t = np.zeros((16, 1024), np.float32)
    for p in range(8):
        for r in range(128):
            e16t[2 * p + r // 64, p * 128 + r] = 1.0
    c["c_e16t"] = e16t
    s64 = np.zeros((16, 1024), np.float32)
    for k in range(16):
        s64[k, k * 64:(k + 1) * 64] = 1.0
    c["c_s64"] = s64
    psl = np.zeros((16, 8), np.float32)
    for k in range(16):
        psl[k, k // 2] = 1.0
    c["c_psel"] = psl
    hs = np.zeros((16, 128), np.float32)
    for k in range(16):
        hs[k, (k % 2) * 64:(k % 2) * 64 + 64] = 1.0
    c["c_hsel"] = hs
    q2g = i128 % 64
    q1g = np.arange(1024) % 64
    c["c_edt"] = np.maximum(q1g[None, :] - q2g[:, None] - 1, 0).astype(np.float32)
    c["c_trit"] = (q1g[None, :] > q2g[:, None]).astype(np.float32)
    c["c_eyet"] = (q1g[None, :] == q2g[:, None]).astype(np.float32)
    c["c_tmod"] = np.broadcast_to((np.arange(512) % 64).astype(BF), (128, 512)).copy()
    c["c_qrev"] = (63 - (i128 % 64)).astype(np.float32).reshape(128, 1)
    c["c_qidx"] = i128.astype(np.float32).reshape(128, 1)
    c["c_ar256"] = np.broadcast_to(np.arange(256).astype(BF), (128, 256)).copy()
    return c


def _vec_fm(v, k):
    return np.ascontiguousarray(v.reshape(k, 128).T.astype(np.float32))


def make_shared(inputs):
    f = np.float32
    T = lambda a: np.ascontiguousarray(np.asarray(a, f).T)
    C = lambda a: np.ascontiguousarray(np.asarray(a, f))
    shared = {
        "wkT": T(inputs["L0_attn_Wk"]), "wvT": T(inputs["L0_attn_Wv"]),
        "wrT": T(inputs["L0_attn_Wr"]), "woT": T(inputs["L0_attn_Wo"]),
        "w1T": T(inputs["L0_ffn_w1"]), "w1N": C(inputs["L0_ffn_w1"]),
        "w2T": T(inputs["L0_ffn_w2"]), "w2N": C(inputs["L0_ffn_w2"]),
        "wr1T": T(inputs["L1_attn_Wr"]), "wk1T": T(inputs["L1_attn_Wk"]),
        "wv1T": T(inputs["L1_attn_Wv"]), "wg1T": T(inputs["L1_attn_Wg"]),
        "wo1T": T(inputs["L1_attn_Wo"]),
        "wfcT": T(inputs["L1_ffn_W_fc"]), "wpjT": T(inputs["L1_ffn_W_proj"]),
        "p_decay": _vec_fm(np.asarray(inputs["L0_attn_decay"], f), 8),
        "p_decrow": np.ascontiguousarray(np.asarray(inputs["L0_attn_decay"], f).reshape(1, 1024)),
        "p_ln0w": _vec_fm(np.asarray(inputs["L0_attn_ln_w"], f), 8),
        "p_b1": _vec_fm(np.asarray(inputs["L0_ffn_b1"], f), 32),
        "p_b2": _vec_fm(np.asarray(inputs["L0_ffn_b2"], f), 8),
        "p_maa": np.concatenate([_vec_fm(np.asarray(inputs[n], f).reshape(-1), 8)
                                 for n in ["L1_attn_time_maa_k", "L1_attn_time_maa_v",
                                           "L1_attn_time_maa_r", "L1_attn_time_maa_g"]], axis=1),
        "p_td": np.asarray(inputs["L1_attn_time_decay"], f).reshape(16, 1),
        "p_fa": np.asarray(inputs["L1_attn_time_faaaa"], f).reshape(16, 1),
        "p_ln1w": _vec_fm(np.asarray(inputs["L1_attn_ln_w"], f), 8),
        "p_ln1b": _vec_fm(np.asarray(inputs["L1_attn_ln_b"], f), 8),
        "p_lnfw": _vec_fm(np.asarray(inputs["L1_ffn_ln_w"], f), 32),
        "p_lnfb": _vec_fm(np.asarray(inputs["L1_ffn_ln_b"], f), 32),
    }
    shared.update(_consts())
    return shared


_CACHE = {}


def kernel(**inputs):
    NT = 8
    TLOC = NT * TT
    key = ("prog", NT)
    if key not in _CACHE:
        _CACHE[key] = build_program(NT=NT)
    nc = _CACHE[key]
    x = np.asarray(inputs["L0_attn_x"], np.float32)
    shared = make_shared(inputs)
    in_maps = []
    for core in range(8):
        b, h = core // 2, core % 2
        m = np.zeros((128, 2), np.float32)
        m[:, 0] = 1.0 - h
        m[:, 1] = h
        im = dict(shared)
        im["x"] = np.ascontiguousarray(x[b, h * TLOC:(h + 1) * TLOC, :].T)
        im["m_sr"] = m
        in_maps.append(im)
    res = run_bass_kernel_spmd(nc, in_maps, core_ids=list(range(8)))
    out = np.empty((4, 2 * TLOC, D), np.float32)
    for core in range(8):
        b, h = core // 2, core % 2
        out[b, h * TLOC:(h + 1) * TLOC, :] = np.asarray(res.results[core]["out"]).T
    return out
